# version 11
# baseline (speedup 1.0000x reference)
import numpy as np
from contextlib import ExitStack

import concourse.bass as bass
import concourse.mybir as mybir
from concourse.bass_utils import run_bass_kernel_spmd

F32 = mybir.dt.float32
BF16 = mybir.dt.bfloat16
I32 = mybir.dt.int32
AF = mybir.ActivationFunctionType
ALU = mybir.AluOpType
AX = mybir.AxisListType

NCORES = 8
TOK = 1024
D = 4096
NEG = -1.0e9


class Sem:
    def __init__(self, handle):
        self.h = handle
        self.n = 0


class Prog:
    def __init__(self, nc, ctx):
        self.nc = nc
        self.ctx = ctx
        names = ["dW", "dW2", "dX", "dXs", "dS", "dO", "dOg", "pe", "act",
                 "dve", "pool", "cc", "st0", "st1", "st2", "st3",
                 "kv0", "kv1", "vv0", "vv1", "dWp"]
        self.s = {nm: Sem(ctx.enter_context(nc.semaphore(nm))) for nm in names}

    def sb(self, name, shape, dt):
        return self.ctx.enter_context(self.nc.sbuf_tensor(name, shape, dt))

    def ps(self, name, shape, dt=F32):
        return self.ctx.enter_context(self.nc.psum_tensor(name, shape, dt))


def row_ap(handle, n, off=0):
    return bass.AP(handle, off, [[1, 1], [1, n]])


def emit_fence(p):
    nc = p.nc
    dma_sems = {"dW", "dW2", "dX", "dXs", "dS", "dO", "dOg", "st0", "st1",
                "st2", "st3", "kv0", "kv1", "vv0", "vv1"}
    targets = [(s.h, (16 if nm in dma_sems else 1) * s.n)
               for nm, s in p.s.items() if s.n > 0 and nm not in ("cc", "dWp")]
    with nc.Block() as blk:
        for eng_name in ["sync", "gpsimd", "scalar", "vector", "tensor"]:
            def body(eng, targets=targets):
                for h, v in targets:
                    eng.wait_ge(h, v)
            getattr(blk, eng_name)(body)



def emit_setup_core(p, C):
    nc = p.nc
    C["ident_bf"] = p.sb("c_ident_bf", [128, 128], BF16)
    C["ident_f"] = p.sb("c_ident_f", [128, 128], F32)
    C["mask"] = p.sb("c_mask", [128, 128], F32)
    C["ones_bf"] = p.sb("c_ones_bf", [1, 128], BF16)
    C["eps"] = p.sb("c_eps", [128, 1], F32)
    C["convb_fm"] = p.sb("c_convb_fm", [128, 24], F32)
    C["bq_fm"] = p.sb("c_bq_fm", [128, 8], F32)
    C["bk_fm"] = p.sb("c_bk_fm", [128, 8], F32)
    C["bv_row"] = p.sb("c_bv_row", [1, 1024], BF16)
    C["bo_fm"] = p.sb("c_bo_fm", [128, 8], F32)
    C["projb_row"] = p.sb("c_projb_row", [1, 4096], BF16)
    _setup_ctx = ExitStack()
    lp = _Sub(p, _setup_ctx)
    with p.nc.Block() as blk:
        @blk.gpsimd
        def _(g):
            g.memset(C["ident_bf"][:, :], 0.0)
            g.memset(C["ident_f"][:, :], 0.0)
            g.memset(C["mask"][:, :], 0.0)
            g.drain()
            g.affine_select(out=C["ident_bf"][:, :], in_=C["ident_bf"][:, :],
                            compare_op=ALU.not_equal, fill=1.0, base=0,
                            pattern=[[-1, 128]], channel_multiplier=1)
            g.affine_select(out=C["ident_f"][:, :], in_=C["ident_f"][:, :],
                            compare_op=ALU.not_equal, fill=1.0, base=0,
                            pattern=[[-1, 128]], channel_multiplier=1)
            g.affine_select(out=C["mask"][:, :], in_=C["mask"][:, :],
                            compare_op=ALU.is_ge, fill=NEG, base=0,
                            pattern=[[-1, 128]], channel_multiplier=1)
            g.memset(C["ones_bf"][:, :], 1.0)
            g.memset(C["eps"][:, :], 1e-5).then_inc(p.s["pool"].h, 1)
    p.s["pool"].n += 1
    emit_fence(p)


def emit_setup_l0a(p, t_in, C):
    nc = p.nc
    _setup_ctx = ExitStack()
    lp = _Sub(p, _setup_ctx)
    stage = lp.sb("c_stage", [48, 128], F32)
    psA = lp.ps("c_psA", [128, 128], F32)

    dS, pe, dve, act = p.s["dS"], p.s["pe"], p.s["dve"], p.s["act"]
    b_dS, b_pe, b_dve, b_act = dS.n, pe.n, dve.n, act.n

    with nc.Block() as blk:
        @blk.gpsimd
        def _(g):
            g.dma_start(stage[0:24, :],
                        t_in["L0_attn_conv_b"].ap().rearrange("(a b) -> a b", b=128)).then_inc(dS.h, 16)
            g.dma_start(stage[24:32, :],
                        t_in["L0_attn_bq"].ap().rearrange("(a c) e -> a (c e)", c=2)).then_inc(dS.h, 16)
            g.dma_start(stage[32:40, :],
                        t_in["L0_attn_bk"].ap().rearrange("(a c) e -> a (c e)", c=2)).then_inc(dS.h, 16)
            g.dma_start(stage[40:48, :],
                        t_in["L0_attn_bo"].ap().rearrange("(a b) -> a b", b=128)).then_inc(dS.h, 16)
            g.dma_start(C["bv_row"][0:1, :], row_ap(t_in["L0_attn_bv"], 1024)).then_inc(dS.h, 16)
            g.dma_start(C["projb_row"][0:1, :], row_ap(t_in["L0_attn_proj_b"], 4096)).then_inc(dS.h, 16)

        @blk.tensor
        def _(t):
            t.wait_ge(dS.h, 16 * (b_dS + 6))
            t.transpose(psA[:, 0:48], stage[0:48, :], C["ident_f"][0:48, 0:48]).then_inc(pe.h, 1)

        @blk.vector
        def _(v):
            v.wait_ge(pe.h, b_pe + 1)
            v.tensor_copy(C["convb_fm"][:, 0:24], psA[:, 0:24])
            v.tensor_copy(C["bq_fm"][:, 0:8], psA[:, 24:32])
            v.tensor_copy(C["bk_fm"][:, 0:8], psA[:, 32:40])
            v.tensor_copy(C["bo_fm"][:, 0:8], psA[:, 40:48]).then_inc(dve.h, 1)

        @blk.scalar
        def _(s):
            s.wait_ge(dve.h, b_dve + 1)
            s.mul(C["bq_fm"][:, 0:8], C["bq_fm"][:, 0:8], 0.125).then_inc(act.h, 1)

    dS.n = b_dS + 6
    pe.n = b_pe + 1
    dve.n = b_dve + 1
    act.n = b_act + 1
    emit_fence(p)
    _setup_ctx.close()



def emit_ln_to_fm(p, C, x_dram, row_starts, out_fm, col0, name):
    nc = p.nc
    NT = len(row_starts)
    NDVE, NACT, NPE = 36, 2, 32
    with ExitStack() as lctx:
        lp = _Sub(p, lctx)
        x_t = [lp.sb(f"{name}_x{i}", [128, 4096], F32) for i in range(2)]
        sums = lp.sb(f"{name}_sums", [128, 2], F32)
        ssq = lp.sb(f"{name}_ssq", [128, 2], F32)
        mean = lp.sb(f"{name}_mean", [128, 2], F32)
        m2 = lp.sb(f"{name}_m2", [128, 2], F32)
        var = lp.sb(f"{name}_var", [128, 2], F32)
        lnv = lp.sb(f"{name}_lnv", [128, 2], F32)
        rstd = lp.sb(f"{name}_rstd", [128, 2], F32)
        sqd = lp.sb(f"{name}_sqd", [128, 4096], BF16)
        ln_tm = [lp.sb(f"{name}_tm{i}", [128, 4096], BF16) for i in range(2)]
        fmst = [lp.sb(f"{name}_fm{i}", [128, 4096], BF16) for i in range(2)]
        psT = [lp.ps(f"{name}_psT{i}", [128, 128], BF16) for i in range(2)]

        dX, dO, pe, act, dve = p.s["dX"], p.s["dOg"], p.s["pe"], p.s["act"], p.s["dve"]
        W = out_fm.shape[1]
        b = dict(dX=dX.n, dO=dO.n, pe=pe.n, act=act.n, dve=dve.n)

        def d_n(it):
            return b["dve"] + it * NDVE

        def a_n(it):
            return b["act"] + it * NACT

        def p_n(it):
            return b["pe"] + it * NPE

        def tp_evac(tj):
            return d_n(tj // 32) + 4 + (tj % 32) + 1

        with nc.Block() as blk:
            @blk.sync
            def _(sy):
                for it, r0 in enumerate(row_starts):
                    if it >= 2:
                        sy.wait_ge(dve.h, d_n(it - 2) + 4)
                    sy.dma_start(x_t[it % 2][:, :],
                                 x_dram[r0:r0 + 128, :]).then_inc(dX.h, 16)

            @blk.vector
            def _(v):
                for it in range(NT):
                    j = it % 2
                    v.wait_ge(dX.h, 16 * (b["dX"] + it + 1))
                    if it >= 2:
                        v.wait_ge(act.h, a_n(it - 2) + 2)
                    v.reduce_sum(sums[:, j:j + 1], x_t[j][:, :],
                                 axis=AX.X).then_inc(dve.h, 1)
                    v.scalar_tensor_tensor(sqd[:, :], x_t[j][:, :], 1.0, x_t[j][:, :],
                                           op0=ALU.mult, op1=ALU.mult,
                                           accum_out=ssq[:, j:j + 1]).then_inc(dve.h, 1)
                    v.wait_ge(act.h, a_n(it) + 1)
                    v.drain()
                    v.scalar_tensor_tensor(var[:, j:j + 1], ssq[:, j:j + 1], 1.0 / 4096,
                                           m2[:, j:j + 1], op0=ALU.mult,
                                           op1=ALU.subtract).then_inc(dve.h, 1)
                    v.wait_ge(act.h, a_n(it) + 2)
                    if it >= 2:
                        v.wait_ge(pe.h, p_n(it - 2) + 32)
                    v.tensor_scalar(ln_tm[j][:, :], x_t[j][:, :], mean[:, j:j + 1],
                                    rstd[:, j:j + 1], op0=ALU.subtract,
                                    op1=ALU.mult).then_inc(dve.h, 1)
                    if it >= 2:
                        v.wait_ge(dO.h, 16 * (b["dO"] + it - 1))
                    for jc in range(32):
                        v.wait_ge(pe.h, p_n(it) + jc + 1)
                        v.tensor_copy(fmst[j][:, jc * 128:(jc + 1) * 128],
                                      psT[(it * 32 + jc) % 2][:, :]).then_inc(dve.h, 1)

            @blk.scalar
            def _(s):
                for it in range(NT):
                    j = it % 2
                    s.wait_ge(dve.h, d_n(it) + 1)
                    if it >= 2:
                        s.wait_ge(dve.h, d_n(it - 2) + 4)
                    s.mul(mean[:, j:j + 1], sums[:, j:j + 1], 1.0 / 4096)
                    s.drain()
                    s.activation(m2[:, j:j + 1], mean[:, j:j + 1],
                                 AF.Square).then_inc(act.h, 1)
                    s.wait_ge(dve.h, d_n(it) + 3)
                    s.activation(lnv[:, j:j + 1], var[:, j:j + 1], AF.Ln,
                                 bias=C["eps"][:, 0:1])
                    s.drain()
                    s.activation(rstd[:, j:j + 1], lnv[:, j:j + 1], AF.Exp,
                                 scale=-0.5).then_inc(act.h, 1)

            @blk.tensor
            def _(t):
                for it in range(NT):
                    j = it % 2
                    t.wait_ge(dve.h, d_n(it) + 4)
                    for jc in range(32):
                        tj = it * 32 + jc
                        if tj >= 2:
                            t.wait_ge(dve.h, tp_evac(tj - 2))
                        t.transpose(psT[tj % 2][:, :],
                                    ln_tm[j][:, jc * 128:(jc + 1) * 128],
                                    C["ident_bf"][:, :]).then_inc(pe.h, 1)

            @blk.gpsimd
            def _(g):
                for it, r0 in enumerate(row_starts):
                    g.wait_ge(dve.h, d_n(it) + 36)
                    dst = bass.AP(out_fm, col0 + r0,
                                  [[W, 128], [128 * W, 32], [1, 128]])
                    g.dma_start(dst, fmst[it % 2][:, :]).then_inc(dO.h, 16)

        dX.n = b["dX"] + NT
        dve.n = b["dve"] + NT * NDVE
        act.n = b["act"] + NT * NACT
        pe.n = b["pe"] + NT * NPE
        dO.n = b["dO"] + NT


def emit_zero_cols(p, out_fm, c0, name):
    nc = p.nc
    W = out_fm.shape[1]
    nz = W - c0
    with ExitStack() as lctx:
        lp = _Sub(p, lctx)
        z = lp.sb(f"{name}_z", [128, 32 * nz], BF16)
        dO = p.s["dOg"]
        b = dO.n
        with nc.Block() as blk:
            @blk.gpsimd
            def _(g):
                g.memset(z[:, :], 0.0)
                g.drain()
                g.dma_start(bass.AP(out_fm, c0, [[W, 128], [128 * W, 32], [1, nz]]),
                            z[:, :]).then_inc(dO.h, 16)
        dO.n = b + 1


class _Sub:

    def __init__(self, p, ctx):
        self.nc = p.nc
        self.ctx = ctx

    def sb(self, name, shape, dt):
        return self.ctx.enter_context(self.nc.sbuf_tensor(name, shape, dt))

    def ps(self, name, shape, dt=F32):
        return self.ctx.enter_context(self.nc.psum_tensor(name, shape, dt))



def emit_conv(p, C, ln0_fm, conv_w, conv_fm, name="cv"):
    nc = p.nc
    LW = ln0_fm.shape[1]
    with ExitStack() as lctx:
        lp = _Sub(p, lctx)
        ln_sb = lp.sb(f"{name}_ln", [128, 32 * LW], BF16)
        w_sb = [lp.sb(f"{name}_w{i}", [128, 1536], BF16) for i in range(2)]
        out_sb = lp.sb(f"{name}_out", [128, 8 * 512], BF16)
        pso = [lp.ps(f"{name}_ps{i}", [128, 1024], F32) for i in range(4)]

        dW, dW2, dX, dO, pe, act = (p.s["dW"], p.s["dW2"], p.s["dX"], p.s["dO"],
                                     p.s["pe"], p.s["act"])
        b = dict(dW=dW.n, dW2=dW2.n, dX=dX.n, dO=dO.n, pe=pe.n, act=act.n)
        NIT = 6 * 32
        pe_done = [(i + 1) * 24 for i in range(NIT)]

        def w_src(ocg, jc):
            off = jc * 128 * 3072 + ocg * 512
            return bass.AP(conv_w, off, [[3072, 128], [4096 * 3072, 3], [1, 512]])

        with nc.Block() as blk:
            @blk.sync
            def _(sy):
                for jc in range(32):
                    sy.dma_start(ln_sb[:, jc * LW:(jc + 1) * LW],
                                 ln0_fm[jc * 128:(jc + 1) * 128, :]).then_inc(dX.h, 16)

            @blk.gpsimd
            def _(g):
                for ocg in range(6):
                    for jc in range(32):
                        i = ocg * 32 + jc
                        if i >= 2:
                            g.wait_ge(pe.h, b["pe"] + pe_done[i - 2])
                        sl = dW if i % 2 == 0 else dW2
                        g.dma_start(w_sb[i % 2][:, :], w_src(ocg, jc)).then_inc(sl.h, 16)

            @blk.tensor
            def _(t):
                t.wait_ge(dX.h, 16 * (b["dX"] + 32))
                for ocg in range(6):
                    for jc in range(32):
                        i = ocg * 32 + jc
                        sl, bn = (dW, b["dW"]) if i % 2 == 0 else (dW2, b["dW2"])
                        t.wait_ge(sl.h, 16 * (bn + i // 2 + 1))
                        if jc == 0 and ocg > 0:
                            t.wait_ge(act.h, b["act"] + ocg * 4)
                        for k in range(3):
                            for oc in range(4):
                                for th in range(2):
                                    t.matmul(
                                        pso[oc][:, th * 512:(th + 1) * 512],
                                        w_sb[i % 2][:, k * 512 + oc * 128:k * 512 + oc * 128 + 128],
                                        ln_sb[:, jc * LW + th * 512 + k:jc * LW + th * 512 + k + 512],
                                        start=(jc == 0 and k == 0),
                                        stop=(jc == 31 and k == 2),
                                    ).then_inc(pe.h, 1)

            @blk.scalar
            def _(s):
                for ocg in range(6):
                    s.wait_ge(pe.h, b["pe"] + (ocg + 1) * 32 * 24)
                    if ocg > 0:
                        s.wait_ge(dO.h, 16 * (b["dO"] + ocg * 8))
                    for oc in range(4):
                        of = ocg * 4 + oc
                        s.activation(
                            out_sb[:, oc * 1024:(oc + 1) * 1024],
                            pso[oc][:, :], AF.Identity,
                            bias=C["convb_fm"][:, of:of + 1],
                        ).then_inc(act.h, 1)

            @blk.sync
            def _(sy):
                for ocg in range(6):
                    for oc in range(4):
                        for th in range(2):
                            of = ocg * 4 + oc
                            sy.wait_ge(act.h, b["act"] + ocg * 4 + oc + 1)
                            sy.dma_start(
                                bass.AP(conv_fm, of * 128 * 1024 + th * 512,
                                        [[1024, 128], [1, 512]]),
                                out_sb[:, (oc * 2 + th) * 512:(oc * 2 + th + 1) * 512],
                            ).then_inc(dO.h, 16)

        dX.n = b["dX"] + 32
        dW.n = b["dW"] + NIT // 2
        dW2.n = b["dW2"] + NIT // 2
        pe.n = b["pe"] + NIT * 24
        act.n = b["act"] + 24
        dO.n = b["dO"] + 48
    emit_fence(p)



def emit_l0_qkv(p, C, conv_fm, wq, wk, wv, qh_fm, kh_fm, vh_tm, name="qkv"):
    nc = p.nc
    with ExitStack() as lctx:
        lp = _Sub(p, lctx)
        conv_sb = lp.sb(f"{name}_conv", [128, 24 * 1024], BF16)
        wq_sb = lp.sb(f"{name}_wq", [128, 8 * 1024], BF16)
        wk_sb = lp.sb(f"{name}_wk", [128, 8 * 1024], BF16)
        wv_sb = lp.sb(f"{name}_wv", [128, 8 * 1024], BF16)
        ost = lp.sb(f"{name}_ost", [128, 8 * 512], BF16)
        pso = [lp.ps(f"{name}_ps{i}", [128, 512], F32) for i in range(4)]

        dW, dX, dO, pe, act, dve = (p.s["dW"], p.s["dX"], p.s["dO"], p.s["pe"],
                                    p.s["act"], p.s["dve"])
        st = [p.s[f"st{i}"] for i in range(4)]
        b = dict(dW=dW.n, dX=dX.n, dO=dO.n, pe=pe.n, act=act.n, dve=dve.n,
                 st=[x.n for x in st])

        with nc.Block() as blk:
            @blk.sync
            def _(sy):
                for c in range(24):
                    sy.dma_start(conv_sb[:, c * 1024:(c + 1) * 1024],
                                 conv_fm[c * 128:(c + 1) * 128, :]).then_inc(dX.h, 16)

            @blk.gpsimd
            def _(g):
                for wi, w in enumerate([wq, wk, wv]):
                    dst = [wq_sb, wk_sb, wv_sb][wi]
                    for jc in range(8):
                        g.dma_start(dst[:, jc * 1024:(jc + 1) * 1024],
                                    bass.AP(w, jc * 128 * 1024, [[1024, 128], [1, 1024]])
                                    ).then_inc(dW.h, 16)

        with nc.Block() as blk:
            @blk.tensor
            def _(t):
                t.wait_ge(dX.h, 16 * (b["dX"] + 24))
                t.wait_ge(dW.h, 16 * (b["dW"] + 24))
                gi = 0
                for wi, wsb in enumerate([wq_sb, wk_sb]):
                    for hc in range(8):
                        if gi >= 2:
                            t.wait_ge(act.h, b["act"] + (gi - 2 + 1) * 2)
                        for th in range(2):
                            for jc in range(8):
                                t.matmul(pso[(gi % 2) * 2 + th][:, :],
                                         wsb[:, jc * 1024 + hc * 128:jc * 1024 + hc * 128 + 128],
                                         conv_sb[:, (wi * 8 + jc) * 1024 + th * 512:(wi * 8 + jc) * 1024 + th * 512 + 512],
                                         start=(jc == 0), stop=(jc == 7)).then_inc(pe.h, 1)
                        gi += 1

            @blk.scalar
            def _(s):
                gi = 0
                for wi in range(2):
                    bias = C["bq_fm"] if wi == 0 else C["bk_fm"]
                    scale = 0.125 if wi == 0 else 1.0
                    for hc in range(8):
                        t0 = b["pe"] + (gi + 1) * 16
                        s.wait_ge(pe.h, t0)
                        if gi >= 4:
                            sl = gi % 4
                            s.wait_ge(st[sl].h, 16 * (b["st"][sl] + 2 * (gi // 4)))
                        for th in range(2):
                            s.activation(ost[:, (gi % 4) * 1024 + th * 512:(gi % 4) * 1024 + (th + 1) * 512],
                                         pso[(gi % 2) * 2 + th][:, :], AF.Identity,
                                         bias=bias[:, hc:hc + 1],
                                         scale=scale).then_inc(act.h, 1)
                        gi += 1

            @blk.sync
            def _(sy):
                gi = 0
                for wi in range(2):
                    out = qh_fm if wi == 0 else kh_fm
                    for hc in range(8):
                        for th in range(2):
                            sy.wait_ge(act.h, b["act"] + gi * 2 + th + 1)
                            sy.dma_start(
                                bass.AP(out, hc * 128 * 1024 + th * 512,
                                        [[1024, 128], [1, 512]]),
                                ost[:, (gi % 4) * 1024 + th * 512:(gi % 4) * 1024 + (th + 1) * 512],
                            ).then_inc(st[gi % 4].h, 16)
                        gi += 1

        b2 = dict(pe=b["pe"] + 256, act=b["act"] + 32, dO=b["dO"],
                  st=[x + 8 for x in b["st"]])

        with nc.Block() as blk:
            @blk.tensor
            def _(t):
                gi = 0
                for tt in range(8):
                    for hh in range(2):
                        if gi >= 2:
                            t.wait_ge(dve.h, b["dve"] + (gi - 2 + 1))
                        ps = pso[gi % 2]
                        t.matmul(ps[:, :], C["ones_bf"][0:1, :],
                                 C["bv_row"][0:1, hh * 512:(hh + 1) * 512],
                                 start=True, stop=False).then_inc(pe.h, 1)
                        for jc in range(8):
                            t.matmul(ps[:, :],
                                     conv_sb[:, (16 + jc) * 1024 + tt * 128:(16 + jc) * 1024 + tt * 128 + 128],
                                     wv_sb[:, jc * 1024 + hh * 512:jc * 1024 + hh * 512 + 512],
                                     start=False, stop=(jc == 7)).then_inc(pe.h, 1)
                        gi += 1

            @blk.vector
            def _(v):
                gi = 0
                for tt in range(8):
                    for hh in range(2):
                        v.wait_ge(pe.h, b2["pe"] + (gi + 1) * 9)
                        if gi < 4:
                            for sl2 in range(4):
                                v.wait_ge(st[sl2].h, 16 * b2["st"][sl2])
                        else:
                            sl = gi % 4
                            v.wait_ge(st[sl].h, 16 * (b2["st"][sl] + gi // 4))
                        v.tensor_copy(ost[:, (gi % 4) * 512:((gi % 4) + 1) * 512],
                                      pso[gi % 2][:, :]).then_inc(dve.h, 1)
                        gi += 1

            @blk.sync
            def _(sy):
                gi = 0
                for tt in range(8):
                    for hh in range(2):
                        sy.wait_ge(dve.h, b["dve"] + gi + 1)
                        sy.dma_start(
                            bass.AP(vh_tm, tt * 128 * 1024 + hh * 512,
                                    [[1024, 128], [1, 512]]),
                            ost[:, (gi % 4) * 512:((gi % 4) + 1) * 512],
                        ).then_inc(st[gi % 4].h, 16)
                        gi += 1

        dX.n = b["dX"] + 24
        dW.n = b["dW"] + 24
        pe.n = b2["pe"] + 16 * 9
        act.n = b2["act"]
        dve.n = b["dve"] + 16
        for i in range(4):
            st[i].n = b2["st"][i] + 4
    emit_fence(p)



def emit_l0_battn(p, C, qh_fm, kh_fm, vh_tm, o_fm, name="ba"):
    nc = p.nc
    with ExitStack() as lctx:
        lp = _Sub(p, lctx)
        qh_sb = lp.sb(f"{name}_qh", [128, 8 * 1024], BF16)
        kh_sb = lp.sb(f"{name}_kh", [128, 8 * 1024], BF16)
        vh_sb = lp.sb(f"{name}_vh", [128, 8 * 1024], BF16)
        o_sb = lp.sb(f"{name}_o", [128, 8 * 1024], BF16)
        s_sb = lp.sb(f"{name}_s", [128, 2 * 128], F32)
        p_bf = lp.sb(f"{name}_p", [128, 2 * 128], BF16)
        pT_sb = lp.sb(f"{name}_pT", [128, 2 * 128], BF16)
        o_tm = lp.sb(f"{name}_otm", [128, 2 * 64], BF16)
        mx = lp.sb(f"{name}_mx", [128, 2], F32)
        negmx = lp.sb(f"{name}_negmx", [128, 2], F32)
        lsum = lp.sb(f"{name}_l", [128, 2], F32)
        rl = lp.sb(f"{name}_rl", [128, 2], F32)
        ps_s = [lp.ps(f"{name}_ps_s{i}", [128, 128], F32) for i in range(2)]
        ps_pT = [lp.ps(f"{name}_ps_pT{i}", [128, 128], BF16) for i in range(2)]
        ps_o = [lp.ps(f"{name}_ps_o{i}", [128, 64], F32) for i in range(2)]
        ps_oT = [lp.ps(f"{name}_ps_oT{i}", [128, 128], BF16) for i in range(2)]

        dX, dO, pe, act, dve = (p.s["dX"], p.s["dO"], p.s["pe"], p.s["act"],
                                p.s["dve"])
        b = dict(dX=dX.n, dO=dO.n, pe=pe.n, act=act.n, dve=dve.n)

        with nc.Block() as blk:
            @blk.sync
            def _(sy):
                for src, dst in [(qh_fm, qh_sb), (kh_fm, kh_sb), (vh_tm, vh_sb)]:
                    for c in range(8):
                        sy.dma_start(dst[:, c * 1024:(c + 1) * 1024],
                                     src[c * 128:(c + 1) * 128, :]).then_inc(dX.h, 16)

        N = 128
        score_pe, tp_pe, pv_pe, to_pe = {}, {}, {}, {}
        cnt = b["pe"]
        for k in range(N + 4):
            if k < N:
                cnt += 1; score_pe[k] = cnt
            if 0 <= k - 2 < N:
                cnt += 1; tp_pe[k - 2] = cnt
            if 0 <= k - 3 < N:
                cnt += 1; pv_pe[k - 3] = cnt
            if 0 <= k - 4 < N:
                cnt += 1; to_pe[k - 4] = cnt
        add_d, max_d, ptc_d, rec_d, scl_d, oc_d = {}, {}, {}, {}, {}, {}
        cnt = b["dve"]
        for k in range(N + 4):
            if k < N:
                add_d[k] = cnt + 1; max_d[k] = cnt + 2; cnt += 2
            if 0 <= k - 2 < N:
                cnt += 1; ptc_d[k - 2] = cnt
            if 0 <= k - 3 < N:
                rec_d[k - 3] = cnt + 1; scl_d[k - 3] = cnt + 2; cnt += 2
            if 0 <= k - 4 < N:
                cnt += 1; oc_d[k - 4] = cnt
        ngm_a, exp_a = {}, {}
        cnt = b["act"]
        for k in range(N + 4):
            if 0 <= k - 1 < N:
                ngm_a[k - 1] = cnt + 1; exp_a[k - 1] = cnt + 2; cnt += 2

        def loc(it):
            bb, h = it // 16, it % 16
            return bb, h, h // 2, (h % 2) * 64

        with nc.Block() as blk:
            @blk.tensor
            def _(t):
                t.wait_ge(dX.h, 16 * (b["dX"] + 24))
                for k in range(N + 4):
                    if k < N:
                        it = k
                        bb, h, hc, hp = loc(it)
                        j = it % 2
                        if it >= 2:
                            t.wait_ge(dve.h, add_d[it - 2])
                        t.matmul(ps_s[j][:, :],
                                 qh_sb[hp:hp + 64, hc * 1024 + bb * 128:hc * 1024 + bb * 128 + 128],
                                 kh_sb[hp:hp + 64, hc * 1024 + bb * 128:hc * 1024 + bb * 128 + 128],
                                 start=True, stop=True).then_inc(pe.h, 1)
                    if 0 <= k - 2 < N:
                        it = k - 2
                        j = it % 2
                        t.wait_ge(act.h, exp_a[it])
                        if it >= 2:
                            t.wait_ge(dve.h, ptc_d[it - 2])
                        t.transpose(ps_pT[j][:, :], p_bf[:, j * 128:(j + 1) * 128],
                                    C["ident_bf"][:, :]).then_inc(pe.h, 1)
                    if 0 <= k - 3 < N:
                        it = k - 3
                        bb, h, hc, hp = loc(it)
                        j = it % 2
                        t.wait_ge(dve.h, ptc_d[it])
                        if it >= 2:
                            t.wait_ge(dve.h, scl_d[it - 2])
                        t.matmul(ps_o[j][:, :], pT_sb[:, j * 128:(j + 1) * 128],
                                 vh_sb[:, bb * 1024 + h * 64:bb * 1024 + h * 64 + 64],
                                 start=True, stop=True).then_inc(pe.h, 1)
                    if 0 <= k - 4 < N:
                        it = k - 4
                        j = it % 2
                        t.wait_ge(dve.h, scl_d[it])
                        if it >= 2:
                            t.wait_ge(dve.h, oc_d[it - 2])
                        t.transpose(ps_oT[j][0:64, :], o_tm[:, j * 64:j * 64 + 64],
                                    C["ident_bf"][:, :]).then_inc(pe.h, 1)

            @blk.vector
            def _(v):
                for k in range(N + 4):
                    if k < N:
                        it = k
                        j = it % 2
                        v.wait_ge(pe.h, score_pe[it])
                        if it >= 2:
                            v.wait_ge(act.h, exp_a[it - 2])
                        v.tensor_add(s_sb[:, j * 128:(j + 1) * 128], ps_s[j][:, :],
                                     C["mask"][:, :]).then_inc(dve.h, 1)
                        v.drain()
                        v.reduce_max(mx[:, j:j + 1], s_sb[:, j * 128:(j + 1) * 128],
                                     axis=AX.X).then_inc(dve.h, 1)
                    if 0 <= k - 2 < N:
                        it = k - 2
                        j = it % 2
                        v.wait_ge(pe.h, tp_pe[it])
                        if it >= 2:
                            v.wait_ge(pe.h, pv_pe[it - 2])
                        v.tensor_copy(pT_sb[:, j * 128:(j + 1) * 128],
                                      ps_pT[j][:, :]).then_inc(dve.h, 1)
                    if 0 <= k - 3 < N:
                        it = k - 3
                        j = it % 2
                        v.wait_ge(act.h, exp_a[it])
                        v.reciprocal(rl[:, j:j + 1], lsum[:, j:j + 1]).then_inc(dve.h, 1)
                        v.drain()
                        v.wait_ge(pe.h, pv_pe[it])
                        if it >= 2:
                            v.wait_ge(pe.h, to_pe[it - 2])
                        v.tensor_scalar(o_tm[:, j * 64:j * 64 + 64], ps_o[j][:, 0:64],
                                        rl[:, j:j + 1], None,
                                        op0=ALU.mult).then_inc(dve.h, 1)
                    if 0 <= k - 4 < N:
                        it = k - 4
                        bb, h, hc, hp = loc(it)
                        j = it % 2
                        v.wait_ge(pe.h, to_pe[it])
                        v.tensor_copy(o_sb[hp:hp + 64, hc * 1024 + bb * 128:hc * 1024 + bb * 128 + 128],
                                      ps_oT[j][0:64, :]).then_inc(dve.h, 1)

            @blk.scalar
            def _(s):
                for k in range(N + 4):
                    if 0 <= k - 1 < N:
                        it = k - 1
                        j = it % 2
                        s.wait_ge(dve.h, max_d[it])
                        if it >= 2:
                            s.wait_ge(dve.h, rec_d[it - 2])
                            s.wait_ge(pe.h, tp_pe[it - 2])
                        s.mul(negmx[:, j:j + 1], mx[:, j:j + 1], -1.0).then_inc(act.h, 1)
                        s.drain()
                        s.activation(p_bf[:, j * 128:(j + 1) * 128],
                                     s_sb[:, j * 128:(j + 1) * 128], AF.Exp,
                                     bias=negmx[:, j:j + 1],
                                     accum_out=lsum[:, j:j + 1]).then_inc(act.h, 1)

        b2 = dict(pe=b["pe"] + 128 * 4, act=b["act"] + 128 * 2,
                  dve=b["dve"] + 128 * 6)

        with nc.Block() as blk:
            @blk.sync
            def _(sy):
                sy.wait_ge(dve.h, b2["dve"])
                for c in range(8):
                    sy.dma_start(
                        bass.AP(o_fm, c * 128 * 1024, [[1024, 128], [1, 1024]]),
                        o_sb[:, c * 1024:(c + 1) * 1024]).then_inc(dO.h, 16)

        dX.n = b["dX"] + 24
        pe.n = b2["pe"]
        act.n = b2["act"]
        dve.n = b2["dve"]
        dO.n = b["dO"] + 8
    emit_fence(p)



def emit_l0_out(p, C, o_fm, wo, proj_w, x0s, x1, name="l0o"):
    nc = p.nc
    with ExitStack() as lctx:
        lp = _Sub(p, lctx)
        o_sb = lp.sb(f"{name}_o", [128, 8 * 1024], BF16)
        wo_sb = lp.sb(f"{name}_wo", [128, 8 * 1024], BF16)
        ap_sb = lp.sb(f"{name}_ap", [128, 8 * 1024], BF16)
        pw_sb = lp.sb(f"{name}_pw", [128, 8 * 4096], BF16)
        x0_t = lp.sb(f"{name}_x0", [128, 4096], F32)
        x1_sb = lp.sb(f"{name}_x1", [128, 4096], F32)
        pso = [lp.ps(f"{name}_ps{i}", [128, 512], F32) for i in range(4)]

        dW, dW2, dX, dO, pe, act, dve = (p.s["dW"], p.s["dW2"], p.s["dX"],
                                         p.s["dO"], p.s["pe"], p.s["act"],
                                         p.s["dve"])
        b = dict(dW=dW.n, dW2=dW2.n, dX=dX.n, dO=dO.n, pe=pe.n, act=act.n,
                 dve=dve.n)

        with nc.Block() as blk:
            @blk.sync
            def _(sy):
                for c in range(8):
                    sy.dma_start(o_sb[:, c * 1024:(c + 1) * 1024],
                                 o_fm[c * 128:(c + 1) * 128, :]).then_inc(dX.h, 16)

            @blk.gpsimd
            def _(g):
                for c in range(8):
                    g.dma_start(wo_sb[:, c * 1024:(c + 1) * 1024],
                                bass.AP(wo, c * 128 * 1024, [[1024, 128], [1, 1024]])
                                ).then_inc(dW.h, 16)
                for c in range(8):
                    g.dma_start(pw_sb[:, c * 4096:(c + 1) * 4096],
                                bass.AP(proj_w, c * 128 * 4096, [[4096, 128], [1, 4096]])
                                ).then_inc(dW2.h, 16)

        with nc.Block() as blk:
            @blk.tensor
            def _(t):
                t.wait_ge(dX.h, 16 * (b["dX"] + 8))
                t.wait_ge(dW.h, 16 * (b["dW"] + 8))
                for oc in range(8):
                    if oc >= 2:
                        t.wait_ge(act.h, b["act"] + (oc - 2 + 1) * 2)
                    for th in range(2):
                        for hc in range(8):
                            t.matmul(pso[(oc % 2) * 2 + th][:, :],
                                     wo_sb[:, hc * 1024 + oc * 128:hc * 1024 + oc * 128 + 128],
                                     o_sb[:, hc * 1024 + th * 512:hc * 1024 + th * 512 + 512],
                                     start=(hc == 0), stop=(hc == 7)).then_inc(pe.h, 1)

            @blk.scalar
            def _(s):
                for oc in range(8):
                    s.wait_ge(pe.h, b["pe"] + (oc + 1) * 16)
                    for th in range(2):
                        s.activation(ap_sb[:, oc * 1024 + th * 512:oc * 1024 + th * 512 + 512],
                                     pso[(oc % 2) * 2 + th][:, :], AF.Identity,
                                     bias=C["bo_fm"][:, oc:oc + 1]).then_inc(act.h, 1)

        b2 = dict(pe=b["pe"] + 128, act=b["act"] + 16)

        b_dW3 = dW.n + 8
        with nc.Block() as blk:
            @blk.gpsimd
            def _(g):
                for tt in range(8):
                    if tt > 0:
                        g.wait_ge(dve.h, b["dve"] + tt * 8)
                    g.dma_start(x0_t[:, :], x0s[1 + tt * 128:1 + tt * 128 + 128, :]).then_inc(dW.h, 16)

            @blk.tensor
            def _(t):
                t.wait_ge(act.h, b2["act"])
                t.wait_ge(dW2.h, 16 * (b["dW2"] + 8))
                for tt in range(8):
                    for og in range(8):
                        gi = tt * 8 + og
                        if gi >= 2:
                            t.wait_ge(dve.h, b["dve"] + gi - 1)
                        ps = pso[gi % 2]
                        t.matmul(ps[:, :], C["ones_bf"][0:1, :],
                                 C["projb_row"][0:1, og * 512:(og + 1) * 512],
                                 start=True, stop=False).then_inc(pe.h, 1)
                        for fc in range(8):
                            t.matmul(ps[:, :],
                                     ap_sb[:, fc * 1024 + tt * 128:fc * 1024 + tt * 128 + 128],
                                     pw_sb[:, fc * 4096 + og * 512:fc * 4096 + og * 512 + 512],
                                     start=False, stop=(fc == 7)).then_inc(pe.h, 1)

            @blk.vector
            def _(v):
                for tt in range(8):
                    v.wait_ge(dW.h, 16 * (b_dW3 + tt + 1))
                    if tt > 0:
                        v.wait_ge(dO.h, 16 * (b["dO"] + tt))
                    for og in range(8):
                        gi = tt * 8 + og
                        v.wait_ge(pe.h, b2["pe"] + (gi + 1) * 9)
                        v.tensor_add(x1_sb[:, og * 512:(og + 1) * 512],
                                     pso[gi % 2][:, :],
                                     x0_t[:, og * 512:(og + 1) * 512]).then_inc(dve.h, 1)

            @blk.sync
            def _(sy):
                for tt in range(8):
                    sy.wait_ge(dve.h, b["dve"] + (tt + 1) * 8)
                    sy.dma_start(x1[tt * 128:(tt + 1) * 128, :],
                                 x1_sb[:, :]).then_inc(dO.h, 16)

        dX.n = b["dX"] + 8
        dW.n = b["dW"] + 16
        dW2.n = b["dW2"] + 8
        pe.n = b2["pe"] + 8 * 8 * 9
        act.n = b2["act"]
        dve.n = b["dve"] + 64
        dO.n = b["dO"] + 8
    emit_fence(p)




def emit_l0_ffn(p, C, t_in, x1, ln1_fm, x2, cc_in, cc_out, dbg_w=None,
                rg=None, name="ff", wTb=None, wpref_sb=None):
    nc = p.nc
    ALPHA = 1.0 / 16.0
    CAP = 3072.0
    with ExitStack() as lctx:
        lp = _Sub(p, lctx)
        ln_sb = lp.sb(f"{name}_ln", [128, 32 * 1024], BF16)
        gw1n = lp.sb(f"{name}_gw1n", [32, 4096], BF16)
        gw1T = lp.sb(f"{name}_gw1T", [128, 1024], BF16)
        gw2n = lp.sb(f"{name}_gw2n", [8, 32], F32)
        gw2T = lp.sb(f"{name}_gw2T", [32, 8], F32)
        gb1_row = lp.sb(f"{name}_gb1r", [1, 32], BF16)
        gb2_row = lp.sb(f"{name}_gb2r", [1, 8], F32)
        ones_f = lp.sb(f"{name}_onesf", [1, 128], F32)
        ones_col = lp.sb(f"{name}_onescol", [128, 1], F32)
        grow = lp.sb(f"{name}_grow", [1, 64], F32)
        gam_bc = lp.sb(f"{name}_gam", [128, 32], F32)
        bet_bc = lp.sb(f"{name}_bet", [128, 32], F32)
        h_t = lp.sb(f"{name}_h", [128, 32], F32)
        sqd = lp.sb(f"{name}_sqd", [128, 32], F32)
        sums = lp.sb(f"{name}_sums", [128, 1], F32)
        ssq = lp.sb(f"{name}_ssq", [128, 1], F32)
        mean = lp.sb(f"{name}_mean", [128, 1], F32)
        msq = lp.sb(f"{name}_msq", [128, 1], F32)
        var = lp.sb(f"{name}_var", [128, 1], F32)
        lnv = lp.sb(f"{name}_lnv", [128, 1], F32)
        rstd = lp.sb(f"{name}_rstd", [128, 1], F32)
        hn = lp.sb(f"{name}_hn", [128, 32], F32)
        t1g = lp.sb(f"{name}_t1g", [128, 32], F32)
        h2 = lp.sb(f"{name}_h2", [128, 32], F32)
        hr = lp.sb(f"{name}_hr", [128, 32], F32)
        hT = lp.sb(f"{name}_hT", [32, 128], F32)
        gates = lp.sb(f"{name}_gates", [128, 8], F32)
        gm1 = lp.sb(f"{name}_gm1", [128, 1], F32)
        neggm1 = lp.sb(f"{name}_ngm1", [128, 1], F32)
        gmasked = lp.sb(f"{name}_gmsk", [128, 8], F32)
        gm2 = lp.sb(f"{name}_gm2", [128, 1], F32)
        gdexp = lp.sb(f"{name}_gdx", [128, 1], F32)
        gden = lp.sb(f"{name}_gden", [128, 1], F32)
        grden = lp.sb(f"{name}_grd", [128, 1], F32)
        eqb_all = lp.sb(f"{name}_eqb", [128, 8 * 16], F32)
        g1_all = lp.sb(f"{name}_g1", [128, 8], F32)
        g2_all = lp.sb(f"{name}_g2", [128, 8], F32)
        cnt_sb = lp.sb(f"{name}_cnt", [16, 1], F32)
        cnt_k = lp.sb(f"{name}_cntk", [1, 8], F32)
        cnt_k1 = lp.sb(f"{name}_cntk1", [1, 8], F32)
        alw = lp.sb(f"{name}_alw", [1, 16], F32)
        a1_bc = lp.sb(f"{name}_a1bc", [128, 8], F32)
        a2_bc = lp.sb(f"{name}_a2bc", [128, 8], F32)
        u1 = lp.sb(f"{name}_u1", [128, 8], F32)
        u2 = lp.sb(f"{name}_u2", [128, 8], F32)
        u1b = lp.sb(f"{name}_u1b", [128, 8], F32)
        u2b = lp.sb(f"{name}_u2b", [128, 8], F32)
        w_all = lp.sb(f"{name}_w", [128, 64], F32)
        ps_ctxF = ExitStack()
        lpf = _Sub(p, ps_ctxF)
        ps_h = lpf.ps(f"{name}_psh", [128, 512], F32)
        ps_hT = lpf.ps(f"{name}_psht", [128, 128], F32)
        ps_g = lpf.ps(f"{name}_psg", [128, 8], F32)
        ps_c = lpf.ps(f"{name}_psc", [16, 1], F32)
        ps_gw = lpf.ps(f"{name}_psgw", [128, 128], BF16)

        dW, dW2, dX, dXs, dO, pe, act, dve, pool, cc = (
            p.s["dW"], p.s["dW2"], p.s["dX"], p.s["dXs"], p.s["dO"], p.s["pe"],
            p.s["act"], p.s["dve"], p.s["pool"], p.s["cc"])
        st = [p.s[f"st{i}"] for i in range(4)]
        b = dict(dW=dW.n, dX=dX.n, pe=pe.n, dve=dve.n)

        with nc.Block() as blk:
            @blk.sync
            def _(sy):
                for c in range(32):
                    sy.dma_start(ln_sb[:, c * 1024:(c + 1) * 1024],
                                 ln1_fm[c * 128:(c + 1) * 128, :]).then_inc(dX.h, 16)

            @blk.gpsimd
            def _(g):
                g.memset(ones_f[:, :], 1.0)
                g.memset(ones_col[:, :], 1.0)
                g.dma_start(gw1n[:, :], t_in["L0_ffn_gw1"].ap()).then_inc(dW.h, 16)
                g.dma_start(gw2n[:, :], t_in["L0_ffn_gw2"].ap()).then_inc(dW.h, 16)
                g.dma_start(gb1_row[0:1, :], row_ap(t_in["L0_ffn_gb1"], 32)).then_inc(dW.h, 16)
                g.dma_start(gb2_row[0:1, :], row_ap(t_in["L0_ffn_gb2"], 8)).then_inc(dW.h, 16)
                g.dma_start(grow[0:1, 0:32], row_ap(t_in["L0_ffn_ln_gamma"], 32)).then_inc(dW.h, 16)
                g.dma_start(grow[0:1, 32:64], row_ap(t_in["L0_ffn_ln_beta"], 32)).then_inc(dW.h, 16)

            @blk.tensor
            def _(t):
                t.wait_ge(dW.h, 16 * (b["dW"] + 6))
                for jc in range(32):
                    if jc >= 4:
                        t.wait_ge(dve.h, b["dve"] + (jc - 4) + 1)
                    t.transpose(ps_gw[:, (jc % 4) * 32:(jc % 4) * 32 + 32],
                                gw1n[:, jc * 128:(jc + 1) * 128],
                                C["ident_bf"][0:32, 0:32]).then_inc(pe.h, 1)
                t.transpose(ps_g[0:32, 0:8], gw2n[:, :],
                            C["ident_f"][0:8, 0:8]).then_inc(pe.h, 1)
                t.matmul(ps_h[:, 0:64], ones_f[0:1, :], grow[0:1, :],
                         start=True, stop=True).then_inc(pe.h, 1)

            @blk.vector
            def _(v):
                for jc in range(32):
                    v.wait_ge(pe.h, b["pe"] + jc + 1)
                    v.tensor_copy(gw1T[:, jc * 32:(jc + 1) * 32],
                                  ps_gw[:, (jc % 4) * 32:(jc % 4) * 32 + 32]).then_inc(dve.h, 1)
                v.wait_ge(pe.h, b["pe"] + 33)
                v.tensor_copy(gw2T[0:32, :], ps_g[0:32, 0:8]).then_inc(dve.h, 1)
                v.wait_ge(pe.h, b["pe"] + 34)
                v.tensor_copy(gam_bc[:, :], ps_h[:, 0:32])
                v.tensor_copy(bet_bc[:, :], ps_h[:, 32:64]).then_inc(dve.h, 1)

        dX.n = b["dX"] + 32
        dW.n = b["dW"] + 6
        pe.n = b["pe"] + 34
        dve.n = b["dve"] + 34
        emit_fence(p)

        NPE, NDVE, NACT = 37, 19, 4
        b2 = dict(pe=pe.n, act=act.n, dve=dve.n)

        def d_n(it):
            return b2["dve"] + it * NDVE

        def a_n(it):
            return b2["act"] + it * NACT

        def p_n(it):
            return b2["pe"] + it * NPE

        with nc.Block() as blk:
            @blk.tensor
            def _(t):
                for it in range(8):
                    if it > 0:
                        t.wait_ge(dve.h, d_n(it))
                    t.matmul(ps_h[:, 0:32], C["ones_bf"][0:1, :],
                             gb1_row[0:1, :], start=True, stop=False).then_inc(pe.h, 1)
                    for jc in range(32):
                        t.matmul(ps_h[:, 0:32],
                                 ln_sb[:, jc * 1024 + it * 128:jc * 1024 + it * 128 + 128],
                                 gw1T[:, jc * 32:(jc + 1) * 32],
                                 start=False, stop=(jc == 31)).then_inc(pe.h, 1)
                    t.wait_ge(dve.h, d_n(it) + 8)
                    t.transpose(ps_hT[0:32, 0:128], hr[:, 0:32],
                                C["ident_f"][:, :]).then_inc(pe.h, 1)
                    t.wait_ge(dve.h, d_n(it) + 9)
                    t.matmul(ps_g[:, 0:8], ones_f[0:1, :], gb2_row[0:1, :],
                             start=True, stop=False).then_inc(pe.h, 1)
                    t.matmul(ps_g[:, 0:8], hT[0:32, :], gw2T[0:32, :],
                             start=False, stop=True).then_inc(pe.h, 1)
                    t.wait_ge(dve.h, d_n(it) + 15)
                    t.matmul(ps_c[0:16, 0:1], eqb_all[:, it * 16:(it + 1) * 16],
                             ones_col[:, 0:1], start=(it == 0), stop=(it == 7)).then_inc(pe.h, 1)

            @blk.vector
            def _(v):
                for it in range(8):
                    v.wait_ge(pe.h, p_n(it) + 33)
                    v.tensor_copy(h_t[:, :], ps_h[:, 0:32]).then_inc(dve.h, 1)
                    v.drain()
                    v.reduce_sum(sums[:, :], h_t[:, :], axis=AX.X).then_inc(dve.h, 1)
                    v.scalar_tensor_tensor(sqd[:, :], h_t[:, :], 1.0, h_t[:, :],
                                           op0=ALU.mult, op1=ALU.mult,
                                           accum_out=ssq[:, 0:1]).then_inc(dve.h, 1)
                    v.wait_ge(act.h, a_n(it) + 1)
                    v.drain()
                    v.scalar_tensor_tensor(var[:, :], ssq[:, :], 1.0 / 32, msq[:, :],
                                           op0=ALU.mult, op1=ALU.subtract).then_inc(dve.h, 1)
                    v.wait_ge(act.h, a_n(it) + 2)
                    v.tensor_scalar(hn[:, :], h_t[:, :], mean[:, 0:1], rstd[:, 0:1],
                                    op0=ALU.subtract, op1=ALU.mult).then_inc(dve.h, 1)
                    v.drain()
                    v.tensor_mul(t1g[:, :], hn[:, :], gam_bc[:, :]).then_inc(dve.h, 1)
                    v.drain()
                    v.tensor_add(h2[:, :], t1g[:, :], bet_bc[:, :]).then_inc(dve.h, 1)
                    v.drain()
                    v.tensor_relu(hr[:, :], h2[:, :]).then_inc(dve.h, 1)
                    v.wait_ge(pe.h, p_n(it) + 34)
                    v.tensor_copy(hT[0:32, :], ps_hT[0:32, 0:128]).then_inc(dve.h, 1)
                    v.wait_ge(pe.h, p_n(it) + 36)
                    v.tensor_copy(gates[:, :], ps_g[:, 0:8]).then_inc(dve.h, 1)
                    v.drain()
                    v.reduce_max(gm1[:, :], gates[:, :], axis=AX.X).then_inc(dve.h, 1)
                    v.drain()
                    v.tensor_scalar(eqb_all[:, it * 16:it * 16 + 8], gates[:, :],
                                    gm1[:, 0:1], None, op0=ALU.is_ge).then_inc(dve.h, 1)
                    v.drain()
                    v.scalar_tensor_tensor(gmasked[:, :], eqb_all[:, it * 16:it * 16 + 8],
                                           -1.0e30, gates[:, :], op0=ALU.mult,
                                           op1=ALU.add).then_inc(dve.h, 1)
                    v.drain()
                    v.reduce_max(gm2[:, :], gmasked[:, :], axis=AX.X).then_inc(dve.h, 1)
                    v.drain()
                    v.tensor_scalar(eqb_all[:, it * 16 + 8:it * 16 + 16], gmasked[:, :],
                                    gm2[:, 0:1], None, op0=ALU.is_ge).then_inc(dve.h, 1)
                    v.wait_ge(act.h, a_n(it) + 4)
                    v.tensor_scalar(gden[:, :], gdexp[:, :], 1.0, None,
                                    op0=ALU.add).then_inc(dve.h, 1)
                    v.drain()
                    v.reciprocal(grden[:, :], gden[:, :]).then_inc(dve.h, 1)
                    v.drain()
                    v.tensor_scalar(g1_all[:, it:it + 1], grden[:, :], ALPHA, None,
                                    op0=ALU.mult).then_inc(dve.h, 1)
                    v.scalar_tensor_tensor(g2_all[:, it:it + 1], gdexp[:, :], ALPHA,
                                           grden[:, :], op0=ALU.mult,
                                           op1=ALU.mult).then_inc(dve.h, 1)

            @blk.scalar
            def _(s):
                for it in range(8):
                    s.wait_ge(dve.h, d_n(it) + 2)
                    s.mul(mean[:, :], sums[:, :], 1.0 / 32)
                    s.drain()
                    s.activation(msq[:, :], mean[:, :], AF.Square).then_inc(act.h, 1)
                    s.wait_ge(dve.h, d_n(it) + 4)
                    s.activation(lnv[:, :], var[:, :], AF.Ln, bias=C["eps"][:, 0:1])
                    s.drain()
                    s.activation(rstd[:, :], lnv[:, :], AF.Exp, scale=-0.5).then_inc(act.h, 1)
                    s.wait_ge(dve.h, d_n(it) + 11)
                    s.mul(neggm1[:, :], gm1[:, :], -1.0).then_inc(act.h, 1)
                    s.drain()
                    s.wait_ge(dve.h, d_n(it) + 14)
                    s.activation(gdexp[:, :], gm2[:, :], AF.Exp,
                                 bias=neggm1[:, 0:1]).then_inc(act.h, 1)

        pe.n = b2["pe"] + 8 * NPE
        dve.n = b2["dve"] + 8 * NDVE
        act.n = b2["act"] + 8 * NACT
        emit_fence(p)

        b3 = dict(pe=pe.n, dve=dve.n, cc=cc.n, dW=dW.n)
        with nc.Block() as blk:
            @blk.vector
            def _(v):
                v.tensor_copy(cnt_sb[:, :], ps_c[0:16, 0:1]).then_inc(dve.h, 1)

            @blk.gpsimd
            def _(g):
                g.wait_ge(dve.h, b3["dve"] + 1)
                g.dma_start(cc_in[:, :], cnt_sb[:, :]).then_inc(dW.h, 16)
                g.wait_ge(dW.h, 16 * (b3["dW"] + 1))
                g.collective_compute(
                    "AllReduce", ALU.add,
                    replica_groups=rg if rg is not None else [list(range(NCORES))],
                    ins=[cc_in.ap()], outs=[cc_out.ap()]).then_inc(cc.h, 1)
                g.wait_ge(cc.h, b3["cc"] + 1)
                g.dma_start(cnt_k[0:1, 0:8],
                            bass.AP(cc_out, 0, [[1, 1], [1, 8]])).then_inc(dW.h, 16)
                g.dma_start(cnt_k1[0:1, 0:8],
                            bass.AP(cc_out, 8, [[1, 1], [1, 8]])).then_inc(dW.h, 16)

            @blk.vector
            def _(v):
                v.wait_ge(dW.h, 16 * (b3["dW"] + 3))
                v.tensor_scalar(alw[0:1, 0:8], cnt_k[0:1, 0:8], CAP, None,
                                op0=ALU.is_le).then_inc(dve.h, 1)
                v.tensor_scalar(alw[0:1, 8:16], cnt_k1[0:1, 0:8], CAP, None,
                                op0=ALU.is_le).then_inc(dve.h, 1)

            @blk.tensor
            def _(t):
                t.wait_ge(dve.h, b3["dve"] + 3)
                t.matmul(ps_hT[:, 0:8], ones_f[0:1, :], alw[0:1, 0:8],
                         start=True, stop=True).then_inc(pe.h, 1)
                t.matmul(ps_g[:, 0:8], ones_f[0:1, :], alw[0:1, 8:16],
                         start=True, stop=True).then_inc(pe.h, 1)

            @blk.vector
            def _(v):
                v.wait_ge(pe.h, b3["pe"] + 2)
                v.tensor_copy(a1_bc[:, :], ps_hT[:, 0:8])
                v.tensor_copy(a2_bc[:, :], ps_g[:, 0:8]).then_inc(dve.h, 1)
                v.drain()
                for it in range(8):
                    v.tensor_scalar(u1[:, :], eqb_all[:, it * 16:it * 16 + 8],
                                    g1_all[:, it:it + 1], None, op0=ALU.mult)
                    v.tensor_scalar(u2[:, :], eqb_all[:, it * 16 + 8:it * 16 + 16],
                                    g2_all[:, it:it + 1], None, op0=ALU.mult)
                    v.drain()
                    v.tensor_mul(u1b[:, :], u1[:, :], a1_bc[:, :])
                    v.tensor_mul(u2b[:, :], u2[:, :], a2_bc[:, :])
                    v.drain()
                    v.tensor_add(w_all[:, it * 8:(it + 1) * 8], u1b[:, :],
                                 u2b[:, :]).then_inc(dve.h, 1)
                    v.drain()

        pe.n = b3["pe"] + 2
        dve.n = b3["dve"] + 3 + 1 + 8
        dW.n = b3["dW"] + 3
        cc.n = b3["cc"] + 1
        if dbg_w is not None:
            bw = p.s["dO"].n
            with nc.Block() as blk:
                @blk.sync
                def _(sy):
                    sy.wait_ge(dve.h, dve.n)
                    sy.dma_start(dbg_w[:, :], w_all[:, :]).then_inc(p.s["dO"].h, 16)
            p.s["dO"].n = bw + 1
        emit_fence(p)

        ps_ctxF.close()

        A_all = lp.sb(f"{name}_Aall", [128, 32 * 128], BF16)
        B_all = lp.sb(f"{name}_Ball", [128, 4096], BF16)
        tmpw = lp.sb(f"{name}_tmpw", [128, 128], BF16)
        loraT = lp.sb(f"{name}_loraT", [128, 8 * 128], BF16)
        ps_ctxH = ExitStack()
        lph = _Sub(p, ps_ctxH)
        ps_l = lph.ps(f"{name}_psl", [128, 128], F32)
        ps_lT = lph.ps(f"{name}_pslT", [128, 128], BF16)

        b4 = dict(pe=pe.n, dve=dve.n, dW=dW.n)
        with nc.Block() as blk:
            @blk.gpsimd
            def _(g):
                for e in range(8):
                    src = bass.AP(t_in["L0_ffn_lora_A"], e * 65536,
                                  [[16, 128], [2048, 32], [1, 16]])
                    dst = bass.AP(A_all, e * 16, [[4096, 128], [128, 32], [1, 16]])
                    g.dma_start(dst, src).then_inc(dW.h, 16)
                g.dma_start(B_all[:, :],
                            bass.AP(t_in["L0_ffn_lora_B"], 0, [[4096, 128], [1, 4096]])
                            ).then_inc(dW.h, 16)

            @blk.tensor
            def _(t):
                t.wait_ge(dW.h, 16 * (b4["dW"] + 9))
                for it in range(8):
                    if it > 0:
                        t.wait_ge(dve.h, b4["dve"] + (it - 1) * 9 + 8)
                    for jc in range(32):
                        t.matmul(ps_l[:, :],
                                 ln_sb[:, jc * 1024 + it * 128:jc * 1024 + it * 128 + 128],
                                 A_all[:, jc * 128:(jc + 1) * 128],
                                 start=(jc == 0), stop=(jc == 31)).then_inc(pe.h, 1)
                    t.wait_ge(dve.h, b4["dve"] + it * 9 + 8)
                    t.transpose(ps_lT[:, :], tmpw[:, :],
                                C["ident_bf"][:, :]).then_inc(pe.h, 1)

            @blk.vector
            def _(v):
                for it in range(8):
                    v.wait_ge(pe.h, b4["pe"] + it * 33 + 32)
                    for e in range(8):
                        v.tensor_scalar(tmpw[:, e * 16:(e + 1) * 16],
                                        ps_l[:, e * 16:(e + 1) * 16],
                                        w_all[:, it * 8 + e:it * 8 + e + 1], None,
                                        op0=ALU.mult).then_inc(dve.h, 1)
                    v.wait_ge(pe.h, b4["pe"] + it * 33 + 33)
                    v.tensor_copy(loraT[:, it * 128:(it + 1) * 128],
                                  ps_lT[:, :]).then_inc(dve.h, 1)

        pe.n = b4["pe"] + 8 * 33
        dve.n = b4["dve"] + 8 * 9
        dW.n = b4["dW"] + 9
        emit_fence(p)

        ps_ctxH.close()

        wT = [wpref_sb, lp.sb(f"{name}_wT1", [128, 32 * 512], BF16)]
        x1_t = lp.sb(f"{name}_x1t", [128, 512], F32)
        x2_sb = lp.sb(f"{name}_x2sb", [128, 2 * 512], F32)
        ps_m = [lp.ps(f"{name}_psm{i}", [128, 512], F32) for i in range(2)]

        b5 = dict(pe=pe.n, dve=dve.n, dW=dW.n, dXs=dXs.n,
                  st=[x.n for x in st])
        NPE_OG = 8 * 33

        def x2_evac(gi):
            return b5["dve"] + gi + 1

        with nc.Block() as blk:
            @blk.gpsimd
            def _(g):
                for og in range(8):
                    if og >= 2:
                        g.wait_ge(pe.h, b5["pe"] + (og - 1) * NPE_OG)
                    g.dma_start(wT[og % 2][:, :],
                                bass.AP(wTb, og * 128 * 16384,
                                        [[16384, 128], [1, 16384]])).then_inc(dW.h, 16)

            @blk.scalar
            def _(s):
                for og in range(8):
                    for tt in range(8):
                        gi = og * 8 + tt
                        if gi > 0:
                            s.wait_ge(dve.h, x2_evac(gi - 1))
                        s.dma_start(x1_t[:, :],
                                    x1[tt * 128:(tt + 1) * 128, og * 512:(og + 1) * 512]
                                    ).then_inc(dXs.h, 16)

            @blk.tensor
            def _(t):
                for og in range(8):
                    t.wait_ge(dW.h, 16 * (b5["dW"] + og + 1))
                    for tt in range(8):
                        gi = og * 8 + tt
                        if gi >= 2:
                            t.wait_ge(dve.h, x2_evac(gi - 2))
                        ps = ps_m[gi % 2]
                        t.matmul(ps[:, :], loraT[:, tt * 128:(tt + 1) * 128],
                                 B_all[:, og * 512:(og + 1) * 512],
                                 start=True, stop=False).then_inc(pe.h, 1)
                        for dc in range(32):
                            t.matmul(ps[:, :],
                                     ln_sb[:, dc * 1024 + tt * 128:dc * 1024 + tt * 128 + 128],
                                     wT[og % 2][:, dc * 512:(dc + 1) * 512],
                                     start=False, stop=(dc == 31)).then_inc(pe.h, 1)

            @blk.vector
            def _(v):
                for og in range(8):
                    for tt in range(8):
                        gi = og * 8 + tt
                        v.wait_ge(pe.h, b5["pe"] + (gi + 1) * 33)
                        v.wait_ge(dXs.h, 16 * (b5["dXs"] + gi + 1))
                        sl = gi % 2
                        v.wait_ge(st[sl].h, 16 * (b5["st"][sl] + gi // 2))
                        v.tensor_add(x2_sb[:, sl * 512:(sl + 1) * 512],
                                     ps_m[gi % 2][:, :], x1_t[:, :]).then_inc(dve.h, 1)

            @blk.sync
            def _(sy):
                for og in range(8):
                    for tt in range(8):
                        gi = og * 8 + tt
                        sy.wait_ge(dve.h, x2_evac(gi))
                        sl = gi % 2
                        sy.dma_start(x2[tt * 128:(tt + 1) * 128, og * 512:(og + 1) * 512],
                                     x2_sb[:, sl * 512:(sl + 1) * 512]).then_inc(st[sl].h, 16)

        pe.n = b5["pe"] + 8 * NPE_OG
        dve.n = b5["dve"] + 64
        dW.n = b5["dW"] + 8
        dXs.n = b5["dXs"] + 64
        for i in range(2):
            st[i].n = b5["st"][i] + 32
    emit_fence(p)
    return b5["pe"] + 7 * NPE_OG




def emit_l1_qkv(p, C, t_in, ln2_fm, q_dram, kv_c, kn_c, qnm_row, name="l1q", sk=()):
    nc = p.nc
    with ExitStack() as lctx:
        lp = _Sub(p, lctx)
        ln_sb = lp.sb(f"{name}_ln", [128, 32 * 1024], BF16)
        wnat = lp.sb(f"{name}_wnat", [128, 4 * 4096], BF16)
        wT = lp.sb(f"{name}_wT", [128, 32 * 512], BF16)
        ost = lp.sb(f"{name}_ost", [128, 4 * 512], BF16)
        k2_sb = lp.sb(f"{name}_k2", [128, 512], F32)
        kn_tm = lp.sb(f"{name}_kntm", [128, 8 * 32], F32)
        kn_T = lp.sb(f"{name}_knT", [32, 8 * 128], F32)
        q2_sb = lp.sb(f"{name}_q2", [128, 512], F32)
        ones_colf = lp.sb(f"{name}_onescf", [128, 1], F32)
        qtmp = lp.sb(f"{name}_qtmp", [1, 1], F32)
        ps_m = [lp.ps(f"{name}_psm{i}", [128, 512], F32) for i in range(2)]
        ps_w = [lp.ps(f"{name}_psw{i}", [128, 128], BF16) for i in range(2)]
        ps_qn = lp.ps(f"{name}_psqn", [1, 512], F32)
        ps_kn = lp.ps(f"{name}_pskn", [32, 128], F32)

        dW, dX, dO, pe, act, dve, pool = (p.s["dW"], p.s["dX"], p.s["dO"],
                                          p.s["pe"], p.s["act"], p.s["dve"],
                                          p.s["pool"])
        st = [p.s[f"st{i}"] for i in range(4)]
        b = dict(dW=dW.n, dX=dX.n, dO=dO.n, pe=pe.n, act=act.n, dve=dve.n,
                 pool=pool.n, st=[x.n for x in st])

        NPE_SEG = [392, 384, 384]
        NDVE_SEG = [148, 160, 136]
        NACT_SEG = [8, 16, 0]
        PE_OFF, DVE_OFF, ACT_OFF = [0], [0], [0]
        for s_ in range(24):
            pr = s_ // 8
            PE_OFF.append(PE_OFF[-1] + NPE_SEG[pr])
            DVE_OFF.append(DVE_OFF[-1] + NDVE_SEG[pr])
            ACT_OFF.append(ACT_OFF[-1] + NACT_SEG[pr])

        def tp_evac(kg):
            s_, k_ = kg // 128, kg % 128
            return b["dve"] + DVE_OFF[s_] + k_ + 1

        def grp_consumed(s_, gr):
            pr = s_ // 8
            if pr == 0:
                dv = b["dve"] + DVE_OFF[s_] + 128 + _q_dve_off(gr) + 1
                ac = b["act"] + ACT_OFF[s_] + gr + 1
                return dv, ac
            if pr == 1:
                dv = b["dve"] + DVE_OFF[s_] + 128 + gr * 4 + 4
                ac = b["act"] + ACT_OFF[s_] + gr * 2 + 2
                return dv, ac
            return b["dve"] + DVE_OFF[s_] + 128 + gr + 1, None

        def _q_dve_off(gr):
            return sum(3 if g % 2 == 1 else 2 for g in range(gr))

        wsrc = [t_in["L1_attn_w_q"], t_in["L1_attn_w_k"], t_in["L1_attn_w_v"]]

        with nc.Block() as blk:
            @blk.sync
            def _(sy):
                for c in range(32):
                    sy.dma_start(ln_sb[:, c * 1024:(c + 1) * 1024],
                                 ln2_fm[c * 128:(c + 1) * 128, :]).then_inc(dX.h, 16)

            @blk.gpsimd
            def _(g):
                g.memset(ones_colf[:, :], 1.0).then_inc(pool.h, 1)
                for s_ in range(24):
                    pr, eg = s_ // 8, s_ % 8
                    if s_ > 0:
                        g.wait_ge(pe.h, b["pe"] + PE_OFF[s_ - 1] + 128)
                    for sub in range(4):
                        g.dma_start(wnat[:, sub * 4096:(sub + 1) * 4096],
                                    bass.AP(wsrc[pr], (eg * 512 + sub * 128) * 4096,
                                            [[4096, 128], [1, 4096]])).then_inc(dW.h, 16)

            @blk.tensor
            def _(t):
                t.wait_ge(dX.h, 16 * (b["dX"] + 32))
                for s_ in range(24):
                    pr, eg = s_ // 8, s_ % 8
                    t.wait_ge(dW.h, 16 * (b["dW"] + (s_ + 1) * 4))
                    for j in range(128):
                        kg = s_ * 128 + j
                        if kg >= 2:
                            t.wait_ge(dve.h, tp_evac(kg - 2))
                        dc, sub = j // 4, j % 4
                        t.transpose(ps_w[kg % 2][:, :],
                                    wnat[:, sub * 4096 + dc * 128:sub * 4096 + dc * 128 + 128],
                                    C["ident_bf"][:, :]).then_inc(pe.h, 1)
                    t.wait_ge(dve.h, b["dve"] + DVE_OFF[s_] + 128)
                    for gr in range(8):
                        gi = s_ * 8 + gr
                        if gi >= 2:
                            s2, g2 = (gi - 2) // 8, (gi - 2) % 8
                            dv, ac = grp_consumed(s2, g2)
                            t.wait_ge(dve.h, dv)
                            if ac is not None:
                                t.wait_ge(act.h, ac)
                        ps = ps_m[gi % 2]
                        if pr == 0:
                            ec, th = gr // 2, gr % 2
                            for dc in range(32):
                                t.matmul(ps[:, :],
                                         wT[:, dc * 512 + ec * 128:dc * 512 + ec * 128 + 128],
                                         ln_sb[:, dc * 1024 + th * 512:dc * 1024 + th * 512 + 512],
                                         start=(dc == 0), stop=(dc == 31)).then_inc(pe.h, 1)
                            t.wait_ge(act.h, b["act"] + ACT_OFF[s_] + gr + 1)
                            if gr > 0:
                                t.wait_ge(dve.h, b["dve"] + DVE_OFF[s_] + 128 + _q_dve_off(gr - 1) + 2)
                            if "qn_mm" in sk:
                                t.matmul(ps_w[0][0:1, 0:128], ones_colf[:, 0:1],
                                         q2_sb[:, 0:128].bitcast(BF16)[:, 0:128],
                                         start=True, stop=True).then_inc(pe.h, 1) if False else t.nop().then_inc(pe.h, 1)
                            else:
                                t.matmul(ps_qn[0:1, :], ones_colf[:, 0:1], q2_sb[:, :],
                                         start=True, stop=True).then_inc(pe.h, 1)
                        else:
                            tt = gr
                            for dc in range(32):
                                t.matmul(ps[:, :],
                                         ln_sb[:, dc * 1024 + tt * 128:dc * 1024 + tt * 128 + 128],
                                         wT[:, dc * 512:(dc + 1) * 512],
                                         start=(dc == 0), stop=(dc == 31)).then_inc(pe.h, 1)
                t.wait_ge(dve.h, b["dve"] + DVE_OFF[16])
                for tt in range(8):
                    if tt > 0:
                        t.wait_ge(dve.h, b["dve"] + DVE_OFF[24] + tt)
                    t.transpose(ps_kn[0:32, :], kn_tm[:, tt * 32:(tt + 1) * 32],
                                C["ident_f"][:, :]).then_inc(pe.h, 1)

            @blk.scalar
            def _(s):
                for s_ in range(24):
                    pr, eg = s_ // 8, s_ % 8
                    if pr == 0:
                        for gr in range(8):
                            gi = s_ * 8 + gr
                            s.wait_ge(pe.h, b["pe"] + PE_OFF[s_] + 128 + gr * 33 + 32)
                            if gi > 0:
                                s2, g2 = (gi - 1) // 8, (gi - 1) % 8
                                s.wait_ge(pe.h, b["pe"] + PE_OFF[s2] + 128 + g2 * 33 + 33)
                            if "qn_sq" in sk:
                                s.nop().then_inc(act.h, 1)
                            else:
                                s.wait_ge(dve.h, b["dve"] + DVE_OFF[s_] + 128 + _q_dve_off(gr) + 1)
                                s.activation(q2_sb[:, :], ps_m[gi % 2][:, :],
                                             AF.Square).then_inc(act.h, 1)
                    elif pr == 1:
                        for gr in range(8):
                            gi = s_ * 8 + gr
                            tt = gr
                            s.wait_ge(pe.h, b["pe"] + PE_OFF[s_] + 128 + gr * 32 + 32)
                            sl = gi % 4
                            s.wait_ge(st[sl].h, 16 * (b["st"][sl] + max(0, (gi - 4) // 4 + 1)))
                            s.activation(ost[:, sl * 512:(sl + 1) * 512],
                                         ps_m[gi % 2][:, :], AF.Identity).then_inc(act.h, 1)
                            if gi > 0:
                                s.wait_ge(dve.h, b["dve"] + DVE_OFF[(gi - 1) // 8] + 128 + ((gi - 1) % 8) * 4 + 4)
                            if "kn_sq" in sk:
                                s.nop().then_inc(act.h, 1)
                            else:
                                s.activation(k2_sb[:, :], ps_m[gi % 2][:, :],
                                             AF.Square).then_inc(act.h, 1)

            @blk.vector
            def _(v):
                for s_ in range(24):
                    pr, eg = s_ // 8, s_ % 8
                    for j in range(128):
                        kg = s_ * 128 + j
                        v.wait_ge(pe.h, b["pe"] + PE_OFF[s_] + j + 1)
                        dc, sub = j // 4, j % 4
                        v.tensor_copy(wT[:, dc * 512 + sub * 128:dc * 512 + sub * 128 + 128],
                                      ps_w[kg % 2][:, :]).then_inc(dve.h, 1)
                    for gr in range(8):
                        gi = s_ * 8 + gr
                        if pr == 0:
                            ec, th = gr // 2, gr % 2
                            h = eg * 4 + ec
                            v.wait_ge(pe.h, b["pe"] + PE_OFF[s_] + 128 + gr * 33 + 32)
                            sl = gi % 4
                            v.wait_ge(st[sl].h, 16 * (b["st"][sl] + max(0, (gi - 4) // 4 + 1)))
                            v.tensor_scalar(ost[:, sl * 512:(sl + 1) * 512],
                                            ps_m[gi % 2][:, :], 0.08838834764831845,
                                            None, op0=ALU.mult).then_inc(dve.h, 1)
                            v.wait_ge(pe.h, b["pe"] + PE_OFF[s_] + 128 + gr * 33 + 33)
                            if "qn_red" in sk:
                                v.engine_nop().then_inc(dve.h, 1)
                                if th == 1:
                                    v.engine_nop().then_inc(dve.h, 1)
                            elif th == 0:
                                v.reduce_max(qnm_row[0:1, h:h + 1], ps_qn[0:1, :],
                                             axis=AX.X).then_inc(dve.h, 1)
                            else:
                                v.reduce_max(qtmp[0:1, 0:1], ps_qn[0:1, :],
                                             axis=AX.X).then_inc(dve.h, 1)
                                v.drain()
                                v.tensor_max(qnm_row[0:1, h:h + 1],
                                             qnm_row[0:1, h:h + 1],
                                             qtmp[0:1, 0:1]).then_inc(dve.h, 1)
                        elif pr == 1:
                            tt = gr
                            v.wait_ge(act.h, b["act"] + ACT_OFF[s_] + gr * 2 + 2)
                            for jj in range(4):
                                h = eg * 4 + jj
                                if "kn_red" in sk:
                                    v.engine_nop().then_inc(dve.h, 1)
                                else:
                                    v.tensor_reduce(kn_tm[:, tt * 32 + h:tt * 32 + h + 1],
                                                    k2_sb[:, jj * 128:(jj + 1) * 128],
                                                    axis=AX.X, op=ALU.add).then_inc(dve.h, 1)
                        else:
                            tt = gr
                            v.wait_ge(pe.h, b["pe"] + PE_OFF[s_] + 128 + gr * 32 + 32)
                            sl = gi % 4
                            v.wait_ge(st[sl].h, 16 * (b["st"][sl] + max(0, (gi - 4) // 4 + 1)))
                            v.tensor_copy(ost[:, sl * 512:(sl + 1) * 512],
                                          ps_m[gi % 2][:, :]).then_inc(dve.h, 1)
                for tt in range(8):
                    v.wait_ge(pe.h, b["pe"] + PE_OFF[24] + tt + 1)
                    v.tensor_copy(kn_T[0:32, tt * 128:(tt + 1) * 128],
                                  ps_kn[0:32, :]).then_inc(dve.h, 1)

            @blk.sync
            def _(sy):
                for s_ in range(24):
                    pr, eg = s_ // 8, s_ % 8
                    for gr in range(8):
                        gi = s_ * 8 + gr
                        sl = gi % 4
                        if pr == 0:
                            ec, th = gr // 2, gr % 2
                            h = eg * 4 + ec
                            sy.wait_ge(dve.h, b["dve"] + DVE_OFF[s_] + 128 + _q_dve_off(gr) + 1)
                            dst = bass.AP(q_dram, h * 128 * 1024 + th * 512,
                                          [[1024, 128], [1, 512]])
                        elif pr == 1:
                            tt = gr
                            sy.wait_ge(act.h, b["act"] + ACT_OFF[s_] + gr * 2 + 1)
                            dst = bass.AP(kv_c, tt * 128 * 4096 + eg * 512,
                                          [[4096, 128], [1, 512]])
                        else:
                            tt = gr
                            sy.wait_ge(dve.h, b["dve"] + DVE_OFF[s_] + 128 + gr + 1)
                            dst = bass.AP(kv_c, (1024 + tt * 128) * 4096 + eg * 512,
                                          [[4096, 128], [1, 512]])
                        sy.dma_start(dst, ost[:, sl * 512:(sl + 1) * 512]).then_inc(st[sl].h, 16)
                sy.wait_ge(dve.h, b["dve"] + DVE_OFF[24] + 8)
                sy.dma_start(kn_c[0:32, :], kn_T[0:32, :]).then_inc(dO.h, 16)

        dX.n = b["dX"] + 32
        dW.n = b["dW"] + 96
        pe.n = b["pe"] + PE_OFF[24] + 8
        act.n = b["act"] + ACT_OFF[24]
        dve.n = b["dve"] + DVE_OFF[24] + 8
        dO.n = b["dO"] + 1
        for i in range(4):
            st[i].n = b["st"][i] + 48
        pool.n = b["pool"] + 1
    emit_fence(p)




import math as _math
_KBIAS = float(_math.log(1.02) - 0.5 * _math.log(128.0))


def emit_l1_gather(p, C, kv_c, kvg, kn_c, kng, qnm_row, negc_bc, rg=None,
                   name="l1g"):
    nc = p.nc
    rg = rg if rg is not None else [list(range(NCORES))]
    NR = len(rg[0])
    with ExitStack() as lctx:
        lp = _Sub(p, lctx)
        knga = lp.sb(f"{name}_knga", [32, NR * 1024], F32)
        knm = lp.sb(f"{name}_knm", [32, 1], F32)
        qnm_col = lp.sb(f"{name}_qnc", [32, 1], F32)
        prod = lp.sb(f"{name}_prod", [32, 1], F32)
        lnp = lp.sb(f"{name}_lnp", [32, 1], F32)
        c_col = lp.sb(f"{name}_c", [32, 1], F32)
        negc = lp.sb(f"{name}_negc", [32, 1], F32)
        negc_row = lp.sb(f"{name}_negcr", [1, 32], F32)
        kb = lp.sb(f"{name}_kb", [32, 1], F32)
        ones_f = lp.sb(f"{name}_onesf", [1, 128], F32)
        ps_a = lp.ps(f"{name}_psa", [128, 128], F32)

        dW, pe, act, dve, pool, cc = (p.s["dW"], p.s["pe"], p.s["act"],
                                      p.s["dve"], p.s["pool"], p.s["cc"])
        b = dict(dW=dW.n, pe=pe.n, act=act.n, dve=dve.n, pool=pool.n, cc=cc.n)

        with nc.Block() as blk:
            @blk.gpsimd
            def _(g):
                g.memset(kb[:, :], _KBIAS)
                g.memset(ones_f[:, :], 1.0).then_inc(pool.h, 1)
                g.collective_compute(
                    "AllGather", ALU.bypass, replica_groups=rg,
                    ins=[kv_c.ap()], outs=[kvg.ap()]).then_inc(cc.h, 1)
                g.collective_compute(
                    "AllGather", ALU.bypass, replica_groups=rg,
                    ins=[kn_c.ap()], outs=[kng.ap()]).then_inc(cc.h, 1)
                g.wait_ge(cc.h, b["cc"] + 2)
                g.dma_start(knga[0:32, :],
                            bass.AP(kng, 0, [[1024, 32], [32 * 1024, NR], [1, 1024]])
                            ).then_inc(dW.h, 16)

            @blk.vector
            def _(v):
                v.wait_ge(dW.h, 16 * (b["dW"] + 1))
                v.reduce_max(knm[0:32, :], knga[0:32, :], axis=AX.X).then_inc(dve.h, 1)

            @blk.tensor
            def _(t):
                t.wait_ge(p.s["pool"].h, b["pool"] + 1)
                t.transpose(ps_a[0:32, 0:1], qnm_row[0:1, 0:32],
                            C["ident_f"][0:1, 0:1]).then_inc(pe.h, 1)

            @blk.vector
            def _(v):
                v.wait_ge(pe.h, b["pe"] + 1)
                v.tensor_copy(qnm_col[0:32, :], ps_a[0:32, 0:1]).then_inc(dve.h, 1)
                v.drain()
                v.tensor_mul(prod[0:32, :], qnm_col[0:32, :],
                             knm[0:32, :]).then_inc(dve.h, 1)

            @blk.scalar
            def _(s):
                s.wait_ge(dve.h, b["dve"] + 3)
                s.activation(lnp[0:32, :], prod[0:32, :], AF.Ln)
                s.drain()
                s.activation(c_col[0:32, :], lnp[0:32, :], AF.Exp, scale=0.5,
                             bias=kb[0:32, 0:1])
                s.drain()
                s.mul(negc[0:32, :], c_col[0:32, :], -1.0).then_inc(act.h, 1)

            @blk.tensor
            def _(t):
                t.wait_ge(act.h, b["act"] + 1)
                t.transpose(ps_a[0:1, 0:32], negc[0:32, 0:1],
                            C["ident_f"][0:32, 0:32]).then_inc(pe.h, 1)

            @blk.vector
            def _(v):
                v.wait_ge(pe.h, b["pe"] + 2)
                v.tensor_copy(negc_row[0:1, :], ps_a[0:1, 0:32]).then_inc(dve.h, 1)

            @blk.tensor
            def _(t):
                t.wait_ge(dve.h, b["dve"] + 4)
                t.matmul(ps_a[:, 0:32], ones_f[0:1, :], negc_row[0:1, :],
                         start=True, stop=True).then_inc(pe.h, 1)

            @blk.vector
            def _(v):
                v.wait_ge(pe.h, b["pe"] + 3)
                v.tensor_copy(negc_bc[:, :], ps_a[:, 0:32]).then_inc(dve.h, 1)

        dW.n = b["dW"] + 1
        pe.n = b["pe"] + 3
        act.n = b["act"] + 1
        dve.n = b["dve"] + 5
        cc.n = b["cc"] + 2
        pool.n = b["pool"] + 1
    emit_fence(p)



def emit_l1_qkv_t2(p, C, wTb_k, wTb_v, wTb_q, ln2_fm, k_c, v_c, q_dram,
                   kn_c, qnm_row, kg, vg, kng, wpref=None, pref_cnt=None,
                   name="l1q"):
    nc = p.nc
    rg = [list(range(NCORES))]
    with ExitStack() as lctx:
        lp = _Sub(p, lctx)
        ln_sb = lp.sb(f"{name}_ln", [128, 32 * 1024], BF16)
        wT = [lp.sb(f"{name}_wT{i}", [128, 32 * 512], BF16) for i in range(2)]
        ost = lp.sb(f"{name}_ost", [128, 4 * 512], BF16)
        k2_sb = lp.sb(f"{name}_k2", [128, 512], F32)
        kn_row = lp.sb(f"{name}_knr", [1, 2 * 512], F32)
        q2_sb = lp.sb(f"{name}_q2", [128, 512], F32)
        ones_colf = lp.sb(f"{name}_onescf", [128, 1], F32)
        qtmp = lp.sb(f"{name}_qtmp", [1, 1], F32)
        ps_m = [lp.ps(f"{name}_psm{i}", [128, 512], F32) for i in range(2)]
        ps_qn = lp.ps(f"{name}_psqn", [1, 512], F32)
        ps_kn = lp.ps(f"{name}_pskn", [1, 512], F32)

        dW, dX, dO, pe, act, dve, pool, cc = (
            p.s["dW"], p.s["dX"], p.s["dO"], p.s["pe"], p.s["act"],
            p.s["dve"], p.s["pool"], p.s["cc"])
        st = [p.s[f"st{i}"] for i in range(4)]
        b = dict(dW=dW.n, dX=dX.n, dO=dO.n, pe=pe.n, act=act.n, dve=dve.n,
                 pool=pool.n, cc=cc.n, st=[x.n for x in st])

        NPE_SEG = [264, 256, 264]
        NDVE_SEG = [16, 8, 20]
        NACT_SEG = [8, 0, 8]
        PE_OFF, DVE_OFF, ACT_OFF = [0], [0], [0]
        for s_ in range(24):
            pr = s_ // 8
            PE_OFF.append(PE_OFF[-1] + NPE_SEG[pr])
            DVE_OFF.append(DVE_OFF[-1] + NDVE_SEG[pr])
            ACT_OFF.append(ACT_OFF[-1] + NACT_SEG[pr])

        def _q_dve_off(gr):
            return sum(3 if g % 2 == 1 else 2 for g in range(gr))

        def grp_consumed(s_, gr):
            pr = s_ // 8
            if pr == 0:
                return (b["dve"] + DVE_OFF[s_] + gr * 2 + 1,
                        b["act"] + ACT_OFF[s_] + gr + 1)
            if pr == 1:
                return b["dve"] + DVE_OFF[s_] + gr + 1, None
            return (b["dve"] + DVE_OFF[s_] + _q_dve_off(gr) + 1,
                    b["act"] + ACT_OFF[s_] + gr + 1)

        wsrc = [wTb_k, wTb_v, wTb_q]

        def wbuf(s_):
            return wpref if (s_ == 0 and wpref is not None) else wT[s_ % 2]

        with nc.Block() as blk:
            @blk.sync
            def _(sy):
                for c in range(32):
                    sy.dma_start(ln_sb[:, c * 1024:(c + 1) * 1024],
                                 ln2_fm[c * 128:(c + 1) * 128, :]).then_inc(dX.h, 16)

            @blk.gpsimd
            def _(g):
                g.memset(ones_colf[:, :], 1.0).then_inc(pool.h, 1)
                for s_ in range(24):
                    pr, eg = s_ // 8, s_ % 8
                    if not (s_ == 0 and wpref is not None):
                        if s_ >= 2:
                            g.wait_ge(pe.h, b["pe"] + PE_OFF[s_ - 1])
                        g.dma_start(wT[s_ % 2][:, :],
                                    bass.AP(wsrc[pr], eg * 128 * 16384,
                                            [[16384, 128], [1, 16384]])).then_inc(dW.h, 16)
                    if s_ == 9:
                        for i in range(4):
                            g.wait_ge(st[i].h, 16 * (b["st"][i] + 16))
                        g.collective_compute(
                            "AllGather", ALU.bypass, replica_groups=rg,
                            ins=[k_c.ap()], outs=[kg.ap()]).then_inc(cc.h, 1)
                        g.wait_ge(dO.h, 16 * (b["dO"] + 64))
                        g.collective_compute(
                            "AllGather", ALU.bypass, replica_groups=rg,
                            ins=[kn_c.ap()], outs=[kng.ap()]).then_inc(cc.h, 1)
                    if s_ == 17:
                        for i in range(4):
                            g.wait_ge(st[i].h, 16 * (b["st"][i] + 32))
                        g.collective_compute(
                            "AllGather", ALU.bypass, replica_groups=rg,
                            ins=[v_c.ap()], outs=[vg.ap()]).then_inc(cc.h, 1)

            @blk.tensor
            def _(t):
                t.wait_ge(dX.h, 16 * (b["dX"] + 32))
                t.wait_ge(pool.h, b["pool"] + 1)
                skip0 = 1 if wpref is not None else 0
                for s_ in range(24):
                    pr, eg = s_ // 8, s_ % 8
                    if s_ == 0 and wpref is not None:
                        t.wait_ge(p.s["dWp"].h, 16 * pref_cnt)
                    else:
                        t.wait_ge(dW.h, 16 * (b["dW"] + s_ + 1 - skip0))
                    for gr in range(8):
                        gi = s_ * 8 + gr
                        if gi >= 2:
                            s2, g2 = (gi - 2) // 8, (gi - 2) % 8
                            dv, ac = grp_consumed(s2, g2)
                            t.wait_ge(dve.h, dv)
                            if ac is not None:
                                t.wait_ge(act.h, ac)
                        ps = ps_m[gi % 2]
                        if pr == 0:
                            ec, th = gr // 2, gr % 2
                            for dc in range(32):
                                t.matmul(ps[:, :],
                                         wbuf(s_)[:, dc * 512 + ec * 128:dc * 512 + ec * 128 + 128],
                                         ln_sb[:, dc * 1024 + th * 512:dc * 1024 + th * 512 + 512],
                                         start=(dc == 0), stop=(dc == 31)).then_inc(pe.h, 1)
                            t.wait_ge(act.h, b["act"] + ACT_OFF[s_] + gr + 1)
                            if gi > 0:
                                t.wait_ge(dve.h, b["dve"] + DVE_OFF[(gi - 1) // 8] + ((gi - 1) % 8) * 2 + 2)
                            t.matmul(ps_kn[0:1, :], ones_colf[:, 0:1], k2_sb[:, :],
                                     start=True, stop=True).then_inc(pe.h, 1)
                        elif pr == 1:
                            tt = gr
                            for dc in range(32):
                                t.matmul(ps[:, :],
                                         ln_sb[:, dc * 1024 + tt * 128:dc * 1024 + tt * 128 + 128],
                                         wbuf(s_)[:, dc * 512:(dc + 1) * 512],
                                         start=(dc == 0), stop=(dc == 31)).then_inc(pe.h, 1)
                        else:
                            ec, th = gr // 2, gr % 2
                            for dc in range(32):
                                t.matmul(ps[:, :],
                                         wbuf(s_)[:, dc * 512 + ec * 128:dc * 512 + ec * 128 + 128],
                                         ln_sb[:, dc * 1024 + th * 512:dc * 1024 + th * 512 + 512],
                                         start=(dc == 0), stop=(dc == 31)).then_inc(pe.h, 1)
                            t.wait_ge(act.h, b["act"] + ACT_OFF[s_] + gr + 1)
                            qgi = (s_ - 16) * 8 + gr
                            if qgi > 0:
                                t.wait_ge(dve.h, b["dve"] + DVE_OFF[16 + (qgi - 1) // 8] + _q_dve_off((qgi - 1) % 8) + 2)
                            t.matmul(ps_qn[0:1, :], ones_colf[:, 0:1], q2_sb[:, :],
                                     start=True, stop=True).then_inc(pe.h, 1)

            @blk.scalar
            def _(s):
                for s_ in range(24):
                    pr, eg = s_ // 8, s_ % 8
                    if pr == 1:
                        continue
                    for gr in range(8):
                        gi = s_ * 8 + gr
                        s.wait_ge(pe.h, b["pe"] + PE_OFF[s_] + gr * 33 + 32)
                        if pr == 0:
                            if gi > 0:
                                s2, g2 = (gi - 1) // 8, (gi - 1) % 8
                                s.wait_ge(pe.h, b["pe"] + PE_OFF[s2] + g2 * 33 + 33)
                            s.wait_ge(dve.h, b["dve"] + DVE_OFF[s_] + gr * 2 + 1)
                            s.activation(k2_sb[:, :], ps_m[gi % 2][:, :],
                                         AF.Square).then_inc(act.h, 1)
                        else:
                            qgi = (s_ - 16) * 8 + gr
                            if qgi > 0:
                                s2, g2 = 16 + (qgi - 1) // 8, (qgi - 1) % 8
                                s.wait_ge(pe.h, b["pe"] + PE_OFF[s2] + g2 * 33 + 33)
                            s.wait_ge(dve.h, b["dve"] + DVE_OFF[s_] + _q_dve_off(gr) + 1)
                            s.activation(q2_sb[:, :], ps_m[gi % 2][:, :],
                                         AF.Square).then_inc(act.h, 1)

            @blk.vector
            def _(v):
                for s_ in range(24):
                    pr, eg = s_ // 8, s_ % 8
                    for gr in range(8):
                        gi = s_ * 8 + gr
                        if pr == 0:
                            ec, th = gr // 2, gr % 2
                            kgi = gi
                            v.wait_ge(pe.h, b["pe"] + PE_OFF[s_] + gr * 33 + 32)
                            sl = gi % 4
                            v.wait_ge(st[sl].h, 16 * (b["st"][sl] + max(0, (gi - 4) // 4 + 1)))
                            v.tensor_copy(ost[:, sl * 512:(sl + 1) * 512],
                                          ps_m[gi % 2][:, :]).then_inc(dve.h, 1)
                            v.wait_ge(pe.h, b["pe"] + PE_OFF[s_] + gr * 33 + 33)
                            if kgi >= 2:
                                v.wait_ge(dO.h, 16 * (b["dO"] + kgi - 1))
                            v.tensor_copy(kn_row[0:1, (kgi % 2) * 512:(kgi % 2 + 1) * 512],
                                          ps_kn[0:1, :]).then_inc(dve.h, 1)
                        elif pr == 1:
                            tt = gr
                            v.wait_ge(pe.h, b["pe"] + PE_OFF[s_] + gr * 32 + 32)
                            sl = gi % 4
                            v.wait_ge(st[sl].h, 16 * (b["st"][sl] + max(0, (gi - 4) // 4 + 1)))
                            v.tensor_copy(ost[:, sl * 512:(sl + 1) * 512],
                                          ps_m[gi % 2][:, :]).then_inc(dve.h, 1)
                        else:
                            ec, th = gr // 2, gr % 2
                            h = eg * 4 + ec
                            v.wait_ge(pe.h, b["pe"] + PE_OFF[s_] + gr * 33 + 32)
                            sl = gi % 4
                            v.wait_ge(st[sl].h, 16 * (b["st"][sl] + max(0, (gi - 4) // 4 + 1)))
                            v.tensor_scalar(ost[:, sl * 512:(sl + 1) * 512],
                                            ps_m[gi % 2][:, :], 0.08838834764831845,
                                            None, op0=ALU.mult).then_inc(dve.h, 1)
                            v.wait_ge(pe.h, b["pe"] + PE_OFF[s_] + gr * 33 + 33)
                            if th == 0:
                                v.reduce_max(qnm_row[0:1, h:h + 1], ps_qn[0:1, :],
                                             axis=AX.X).then_inc(dve.h, 1)
                            else:
                                v.reduce_max(qtmp[0:1, 0:1], ps_qn[0:1, :],
                                             axis=AX.X).then_inc(dve.h, 1)
                                v.drain()
                                v.tensor_max(qnm_row[0:1, h:h + 1],
                                             qnm_row[0:1, h:h + 1],
                                             qtmp[0:1, 0:1]).then_inc(dve.h, 1)

            @blk.sync
            def _(sy):
                for s_ in range(24):
                    pr, eg = s_ // 8, s_ % 8
                    for gr in range(8):
                        gi = s_ * 8 + gr
                        sl = gi % 4
                        if pr == 0:
                            ec, th = gr // 2, gr % 2
                            h = eg * 4 + ec
                            sy.wait_ge(dve.h, b["dve"] + DVE_OFF[s_] + gr * 2 + 1)
                            dst = bass.AP(k_c, h * 128 * 1024 + th * 512,
                                          [[1024, 128], [1, 512]])
                            sy.dma_start(dst, ost[:, sl * 512:(sl + 1) * 512]).then_inc(st[sl].h, 16)
                            sy.wait_ge(dve.h, b["dve"] + DVE_OFF[s_] + gr * 2 + 2)
                            sy.dma_start(
                                bass.AP(kn_c, h * 1024 + th * 512, [[1, 1], [1, 512]]),
                                kn_row[0:1, (gi % 2) * 512:(gi % 2 + 1) * 512]).then_inc(dO.h, 16)
                        elif pr == 1:
                            tt = gr
                            sy.wait_ge(dve.h, b["dve"] + DVE_OFF[s_] + gr + 1)
                            dst = bass.AP(v_c, tt * 128 * 4096 + eg * 512,
                                          [[4096, 128], [1, 512]])
                            sy.dma_start(dst, ost[:, sl * 512:(sl + 1) * 512]).then_inc(st[sl].h, 16)
                        else:
                            ec, th = gr // 2, gr % 2
                            h = eg * 4 + ec
                            sy.wait_ge(dve.h, b["dve"] + DVE_OFF[s_] + _q_dve_off(gr) + 1)
                            dst = bass.AP(q_dram, h * 128 * 1024 + th * 512,
                                          [[1024, 128], [1, 512]])
                            sy.dma_start(dst, ost[:, sl * 512:(sl + 1) * 512]).then_inc(st[sl].h, 16)

        dX.n = b["dX"] + 32
        dW.n = b["dW"] + 24 - (1 if wpref is not None else 0)
        pe.n = b["pe"] + PE_OFF[24]
        act.n = b["act"] + ACT_OFF[24]
        dve.n = b["dve"] + DVE_OFF[24]
        dO.n = b["dO"] + 64
        cc.n = b["cc"] + 3
        for i in range(4):
            st[i].n = b["st"][i] + 48
        pool.n = b["pool"] + 1
    emit_fence(p)
    return b["pe"] + PE_OFF[1]


def emit_l1_negc(p, C, kng, qnm_row, negc_bc, cc_base, name="l1n"):
    nc = p.nc
    NR = NCORES
    with ExitStack() as lctx:
        lp = _Sub(p, lctx)
        knga = lp.sb(f"{name}_knga", [32, NR * 1024], F32)
        knm = lp.sb(f"{name}_knm", [32, 1], F32)
        qnm_col = lp.sb(f"{name}_qnc", [32, 1], F32)
        prod = lp.sb(f"{name}_prod", [32, 1], F32)
        lnp = lp.sb(f"{name}_lnp", [32, 1], F32)
        c_col = lp.sb(f"{name}_c", [32, 1], F32)
        negc = lp.sb(f"{name}_negc", [32, 1], F32)
        negc_row = lp.sb(f"{name}_negcr", [1, 32], F32)
        kb = lp.sb(f"{name}_kb", [32, 1], F32)
        ones_f = lp.sb(f"{name}_onesf", [1, 128], F32)
        ps_a = lp.ps(f"{name}_psa", [128, 128], F32)

        dW, pe, act, dve, pool, cc = (p.s["dW"], p.s["pe"], p.s["act"],
                                      p.s["dve"], p.s["pool"], p.s["cc"])
        b = dict(dW=dW.n, pe=pe.n, act=act.n, dve=dve.n, pool=pool.n)

        with nc.Block() as blk:
            @blk.gpsimd
            def _(g):
                g.memset(kb[:, :], _KBIAS)
                g.memset(ones_f[:, :], 1.0).then_inc(pool.h, 1)
                g.wait_ge(cc.h, cc_base + 2)
                g.dma_start(knga[0:32, :],
                            bass.AP(kng, 0, [[1024, 32], [32 * 1024, NR], [1, 1024]])
                            ).then_inc(dW.h, 16)

            @blk.vector
            def _(v):
                v.wait_ge(dW.h, 16 * (b["dW"] + 1))
                v.reduce_max(knm[0:32, :], knga[0:32, :], axis=AX.X).then_inc(dve.h, 1)

            @blk.tensor
            def _(t):
                t.wait_ge(p.s["pool"].h, b["pool"] + 1)
                t.transpose(ps_a[0:32, 0:1], qnm_row[0:1, 0:32],
                            C["ident_f"][0:1, 0:1]).then_inc(pe.h, 1)

            @blk.vector
            def _(v):
                v.wait_ge(pe.h, b["pe"] + 1)
                v.tensor_copy(qnm_col[0:32, :], ps_a[0:32, 0:1]).then_inc(dve.h, 1)
                v.drain()
                v.tensor_mul(prod[0:32, :], qnm_col[0:32, :],
                             knm[0:32, :]).then_inc(dve.h, 1)

            @blk.scalar
            def _(s):
                s.wait_ge(dve.h, b["dve"] + 3)
                s.activation(lnp[0:32, :], prod[0:32, :], AF.Ln)
                s.drain()
                s.activation(c_col[0:32, :], lnp[0:32, :], AF.Exp, scale=0.5,
                             bias=kb[0:32, 0:1])
                s.drain()
                s.mul(negc[0:32, :], c_col[0:32, :], -1.0).then_inc(act.h, 1)

            @blk.tensor
            def _(t):
                t.wait_ge(act.h, b["act"] + 1)
                t.transpose(ps_a[0:1, 0:32], negc[0:32, 0:1],
                            C["ident_f"][0:32, 0:32]).then_inc(pe.h, 1)

            @blk.vector
            def _(v):
                v.wait_ge(pe.h, b["pe"] + 2)
                v.tensor_copy(negc_row[0:1, :], ps_a[0:1, 0:32]).then_inc(dve.h, 1)

            @blk.tensor
            def _(t):
                t.wait_ge(dve.h, b["dve"] + 4)
                t.matmul(ps_a[:, 0:32], ones_f[0:1, :], negc_row[0:1, :],
                         start=True, stop=True).then_inc(pe.h, 1)

            @blk.vector
            def _(v):
                v.wait_ge(pe.h, b["pe"] + 3)
                v.tensor_copy(negc_bc[:, :], ps_a[:, 0:32]).then_inc(dve.h, 1)

        dW.n = b["dW"] + 1
        pe.n = b["pe"] + 3
        act.n = b["act"] + 1
        dve.n = b["dve"] + 5
        pool.n = b["pool"] + 1
    return dve.n



def emit_l1_flash2(p, C, q_dram, kg, vg, negc_bc, o2_tm, NR=8,
                   cc_base=None, negc_dve=None, name="l1f"):
    nc = p.nc
    cch = p.s["cc"].h
    NKC = NR * 8
    NKP = NKC // 2
    with ExitStack() as lctx:
        lp = _Sub(p, lctx)
        kh_sb = lp.sb(f"{name}_kh", [128, 2 * NR * 1024], BF16)
        vh_sb = lp.sb(f"{name}_vh", [128, 2 * NKC * 130], BF16)
        q_sb = lp.sb(f"{name}_q", [128, 2 * 1024], BF16)
        pT_sb = lp.sb(f"{name}_pT", [128, 2 * 1024], BF16)
        rl = lp.sb(f"{name}_rl", [128, 8], F32)
        o_st = lp.sb(f"{name}_ost", [128, 4 * 128], BF16)
        ps_s2 = [lp.ps(f"{name}_pss{i}", [128, 1024], F32) for i in range(2)]
        ps_o = [lp.ps(f"{name}_pso{i}", [128, 129], F32) for i in range(4)]

        pe, act, dve, pool = (p.s["pe"], p.s["act"], p.s["dve"], p.s["pool"])
        st = [p.s[f"st{i}"] for i in range(4)]
        kv = [p.s["kv0"], p.s["kv1"]]
        vv = [p.s["vv0"], p.s["vv1"]]
        b = dict(pe=pe.n, act=act.n, dve=dve.n, pool=pool.n,
                 st=[x.n for x in st], kv=[x.n for x in kv],
                 vv=[x.n for x in vv])

        PPE = 5 * NKC
        NPASS = 64

        def base(kp):
            return 2 * kp + 8 * max(0, kp - 1)

        def s_end(pp, kc):
            return b["pe"] + pp * PPE + base(kc // 2) + (kc % 2) + 1

        def pv_end_g(kpg):
            pp, j = kpg // NKP, kpg % NKP
            return b["pe"] + pp * PPE + (base(j + 1) + 10 if j < NKP - 1 else PPE)

        def exp_n(kpg):
            return b["act"] + kpg + 1

        with nc.Block() as blk:
            @blk.gpsimd
            def _(g):
                if negc_dve is not None:
                    g.wait_ge(dve.h, negc_dve)
                g.memset(vh_sb[:, :], 1.0).then_inc(pool.h, 1)
                g.wait_ge(pool.h, b["pool"] + 1)
                if cc_base is not None:
                    g.wait_ge(cch, cc_base + 3)
                for h in range(32):
                    sl = h % 2
                    if h >= 2:
                        g.wait_ge(pe.h, b["pe"] + (h - 1) * 2 * PPE)
                    for r in range(NR):
                        src = bass.AP(vg, r * 1024 * 4096 + h * 128,
                                      [[4096, 128], [128 * 4096, 8], [1, 128]])
                        dst = bass.AP(vh_sb, sl * NKC * 130 + r * 8 * 130,
                                      [[2 * NKC * 130, 128], [130, 8], [1, 128]])
                        g.dma_start(dst, src).then_inc(vv[h % 2].h, 16)

            @blk.scalar
            def _(s):
                if negc_dve is not None:
                    s.wait_ge(dve.h, negc_dve)
                for pp in range(NPASS):
                    h = pp // 2
                    for kp in range(NKP):
                        kpg = pp * NKP + kp
                        s.wait_ge(pe.h, s_end(pp, 2 * kp + 1))
                        if kpg >= 2:
                            s.wait_ge(pe.h, pv_end_g(kpg - 2))
                        s.activation(pT_sb[:, (kpg % 2) * 1024:(kpg % 2 + 1) * 1024],
                                     ps_s2[kpg % 2][:, :], AF.Exp,
                                     bias=negc_bc[:, h:h + 1]).then_inc(act.h, 1)

            @blk.tensor
            def _(t):
                t.wait_ge(pool.h, b["pool"] + 1)
                for pp in range(NPASS):
                    h, qH = pp // 2, pp % 2
                    sl = h % 2
                    if qH == 0:
                        t.wait_ge(kv[sl].h, 16 * (b["kv"][sl] + (h // 2 + 1) * (NR + 1)))
                        t.wait_ge(vv[sl].h, 16 * (b["vv"][sl] + (h // 2 + 1) * NR))
                    for kp in range(NKP):
                        kpg = pp * NKP + kp
                        for half in range(2):
                            kc = 2 * kp + half
                            if kpg >= 2 and half == 0:
                                t.wait_ge(act.h, exp_n(kpg - 2))
                            t.matmul(ps_s2[kpg % 2][:, half * 512:(half + 1) * 512],
                                     kh_sb[:, sl * NR * 1024 + kc * 128:sl * NR * 1024 + kc * 128 + 128],
                                     q_sb[:, sl * 1024 + qH * 512:sl * 1024 + qH * 512 + 512],
                                     start=True, stop=True).then_inc(pe.h, 1)
                        if kp >= 1:
                            t.wait_ge(act.h, exp_n(kpg - 1))
                            if kp == 1:
                                t.wait_ge(dve.h, b["dve"] + pp * 8)
                            for kcq in (2 * kp - 2, 2 * kp - 1):
                                pslot = ((kpg - 1) % 2) * 1024 + (kcq % 2) * 512
                                for qq in range(4):
                                    t.matmul(ps_o[qq][:, 0:129],
                                             pT_sb[:, pslot + qq * 128:pslot + qq * 128 + 128],
                                             vh_sb[:, sl * NKC * 130 + kcq * 130:sl * NKC * 130 + kcq * 130 + 129],
                                             start=(kcq == 0), stop=False).then_inc(pe.h, 1)
                    t.wait_ge(act.h, exp_n(pp * NKP + NKP - 1))
                    for kcq in (NKC - 2, NKC - 1):
                        pslot = ((NKP - 1) % 2) * 1024 + (kcq % 2) * 512
                        for qq in range(4):
                            t.matmul(ps_o[qq][:, 0:129],
                                     pT_sb[:, pslot + qq * 128:pslot + qq * 128 + 128],
                                     vh_sb[:, sl * NKC * 130 + kcq * 130:sl * NKC * 130 + kcq * 130 + 129],
                                     start=False, stop=(kcq == NKC - 1)).then_inc(pe.h, 1)

            @blk.vector
            def _(v):
                for pp in range(NPASS):
                    h, qH = pp // 2, pp % 2
                    v.wait_ge(pe.h, b["pe"] + (pp + 1) * PPE)
                    for qq in range(4):
                        qs = qH * 4 + qq
                        rc = (pp % 2) * 4 + qq
                        v.reciprocal(rl[:, rc:rc + 1], ps_o[qq][:, 128:129]).then_inc(dve.h, 1)
                        v.drain()
                        sl2 = qs % 4
                        v.wait_ge(st[sl2].h, 16 * (b["st"][sl2] + pp))
                        v.tensor_scalar(o_st[:, sl2 * 128:(sl2 + 1) * 128],
                                        ps_o[qq][:, 0:128],
                                        rl[:, rc:rc + 1], None, op0=ALU.mult).then_inc(dve.h, 1)

            @blk.sync
            def _(sy):
                if negc_dve is not None:
                    sy.wait_ge(dve.h, negc_dve)
                if cc_base is not None:
                    sy.wait_ge(cch, cc_base + 1)
                for pp in range(NPASS + 1):
                    if pp < NPASS and pp % 2 == 0:
                        h = pp // 2
                        sl = h % 2
                        if h >= 2:
                            sy.wait_ge(pe.h, b["pe"] + (h - 1) * 2 * PPE)
                        for r in range(NR):
                            sy.dma_start(
                                kh_sb[:, sl * NR * 1024 + r * 1024:sl * NR * 1024 + (r + 1) * 1024],
                                bass.AP(kg, (r * 4096 + h * 128) * 1024,
                                        [[1024, 128], [1, 1024]])
                                ).then_inc(kv[sl].h, 16)
                        sy.dma_start(q_sb[:, sl * 1024:(sl + 1) * 1024],
                                     bass.AP(q_dram, h * 128 * 1024,
                                             [[1024, 128], [1, 1024]])
                                     ).then_inc(kv[sl].h, 16)
                    if pp >= 1:
                        ppp = pp - 1
                        h, qH = ppp // 2, ppp % 2
                        for qq in range(4):
                            qs = qH * 4 + qq
                            sy.wait_ge(dve.h, b["dve"] + ppp * 8 + qq * 2 + 2)
                            sl2 = qs % 4
                            sy.dma_start(
                                bass.AP(o2_tm, qs * 128 * 4096 + h * 128,
                                        [[4096, 128], [1, 128]]),
                                o_st[:, sl2 * 128:(sl2 + 1) * 128]).then_inc(st[sl2].h, 16)

        pe.n = b["pe"] + NPASS * PPE
        act.n = b["act"] + NPASS * NKP
        dve.n = b["dve"] + NPASS * 8
        pool.n = b["pool"] + 1
        for i2 in range(2):
            kv[i2].n = b["kv"][i2] + 16 * (NR + 1)
            vv[i2].n = b["vv"][i2] + 16 * NR
        for i2 in range(4):
            st[i2].n = b["st"][i2] + 64
    emit_fence(p)



def emit_big_linear(p, C, name, act_mode, act_dram, w_dram, xin, xout,
                    n_extra=0, emit_extra=None):
    nc = p.nc
    with ExitStack() as lctx:
        lp = _Sub(p, lctx)
        a_sb = lp.sb(f"{name}_a", [128, 32 * 1024], BF16)
        wnat = lp.sb(f"{name}_wnat", [128, 4 * 4096], BF16)
        wT = lp.sb(f"{name}_wT", [128, 32 * 512], BF16)
        xi_t = lp.sb(f"{name}_xit", [128, 512], F32)
        xo_sb = lp.sb(f"{name}_xosb", [128, 2 * 512], F32)
        ps_m = [lp.ps(f"{name}_psm{i}", [128, 512], F32) for i in range(2)]
        ps_w = [lp.ps(f"{name}_psw{i}", [128, 128], BF16) for i in range(2)]

        dW, dX, dXs, pe, act, dve = (p.s["dW"], p.s["dX"], p.s["dXs"],
                                     p.s["pe"], p.s["act"], p.s["dve"])
        st = [p.s[f"st{i}"] for i in range(4)]
        b = dict(dW=dW.n, dX=dX.n, dXs=dXs.n, pe=pe.n, dve=dve.n,
                 st=[x.n for x in st])
        G = 32 + n_extra
        NPE_OG = 128 + 8 * G
        NDVE_OG = 128 + 8

        def wt_evac(k):
            return b["dve"] + (k // 128) * NDVE_OG + (k % 128) + 1

        def xo_evac(gi):
            return b["dve"] + (gi // 8) * NDVE_OG + 128 + (gi % 8) + 1

        with nc.Block() as blk:
            @blk.sync
            def _(sy):
                if act_mode == "fm":
                    for c in range(32):
                        sy.dma_start(a_sb[:, c * 1024:(c + 1) * 1024],
                                     act_dram[c * 128:(c + 1) * 128, :]).then_inc(dX.h, 16)
                else:
                    for c in range(32):
                        sy.dma_start(a_sb[:, c * 1024:(c + 1) * 1024],
                                     bass.AP(act_dram, c * 128, [[4096, 1024], [1, 128]]),
                                     transpose=True).then_inc(dX.h, 16)

            @blk.gpsimd
            def _(g):
                for og in range(8):
                    if og > 0:
                        g.wait_ge(pe.h, b["pe"] + (og - 1) * NPE_OG + 128)
                    for sub in range(4):
                        g.dma_start(wnat[:, sub * 4096:(sub + 1) * 4096],
                                    bass.AP(w_dram, (og * 512 + sub * 128) * 4096,
                                            [[4096, 128], [1, 4096]])).then_inc(dW.h, 16)

            @blk.scalar
            def _(s):
                for og in range(8):
                    for tt in range(8):
                        gi = og * 8 + tt
                        if gi > 0:
                            s.wait_ge(dve.h, xo_evac(gi - 1))
                        s.dma_start(xi_t[:, :],
                                    xin[tt * 128:(tt + 1) * 128, og * 512:(og + 1) * 512]
                                    ).then_inc(dXs.h, 16)

            @blk.tensor
            def _(t):
                t.wait_ge(dX.h, 16 * (b["dX"] + 32))
                for og in range(8):
                    t.wait_ge(dW.h, 16 * (b["dW"] + (og + 1) * 4))
                    for j in range(128):
                        k = og * 128 + j
                        if k >= 2:
                            t.wait_ge(dve.h, wt_evac(k - 2))
                        dc, sub = j // 4, j % 4
                        t.transpose(ps_w[k % 2][:, :],
                                    wnat[:, sub * 4096 + dc * 128:sub * 4096 + dc * 128 + 128],
                                    C["ident_bf"][:, :]).then_inc(pe.h, 1)
                    t.wait_ge(dve.h, b["dve"] + og * NDVE_OG + 128)
                    for tt in range(8):
                        gi = og * 8 + tt
                        if gi >= 2:
                            t.wait_ge(dve.h, xo_evac(gi - 2))
                        ps = ps_m[gi % 2]
                        if emit_extra is not None:
                            emit_extra(t, p, ps, og, tt)
                        for dc in range(32):
                            t.matmul(ps[:, :],
                                     a_sb[:, dc * 1024 + tt * 128:dc * 1024 + tt * 128 + 128],
                                     wT[:, dc * 512:(dc + 1) * 512],
                                     start=(dc == 0 and n_extra == 0),
                                     stop=(dc == 31)).then_inc(pe.h, 1)

            @blk.vector
            def _(v):
                for og in range(8):
                    for j in range(128):
                        v.wait_ge(pe.h, b["pe"] + og * NPE_OG + j + 1)
                        dc, sub = j // 4, j % 4
                        v.tensor_copy(wT[:, dc * 512 + sub * 128:dc * 512 + sub * 128 + 128],
                                      ps_w[(og * 128 + j) % 2][:, :]).then_inc(dve.h, 1)
                    for tt in range(8):
                        gi = og * 8 + tt
                        v.wait_ge(pe.h, b["pe"] + og * NPE_OG + 128 + (tt + 1) * G)
                        v.wait_ge(dXs.h, 16 * (b["dXs"] + gi + 1))
                        sl = gi % 2
                        v.wait_ge(st[sl].h, 16 * (b["st"][sl] + gi // 2))
                        v.tensor_add(xo_sb[:, sl * 512:(sl + 1) * 512],
                                     ps_m[gi % 2][:, :], xi_t[:, :]).then_inc(dve.h, 1)

            @blk.sync
            def _(sy):
                for og in range(8):
                    for tt in range(8):
                        gi = og * 8 + tt
                        sy.wait_ge(dve.h, xo_evac(gi))
                        sl = gi % 2
                        sy.dma_start(xout[tt * 128:(tt + 1) * 128, og * 512:(og + 1) * 512],
                                     xo_sb[:, sl * 512:(sl + 1) * 512]).then_inc(st[sl].h, 16)

        pe.n = b["pe"] + 8 * NPE_OG
        dve.n = b["dve"] + 8 * NDVE_OG
        dW.n = b["dW"] + 32
        dX.n = b["dX"] + 32
        dXs.n = b["dXs"] + 64
        for i in range(2):
            st[i].n = b["st"][i] + 32
    emit_fence(p)




def emit_l1_ffn_pre_w(p, C, t_in, bias_row, A_effT, B_effT, name="l1p"):
    nc = p.nc
    with ExitStack() as lctx:
        lp = _Sub(p, lctx)
        lsi_i = lp.sb(f"{name}_lsii", [1, 4], I32)
        lsi_f = lp.sb(f"{name}_lsif", [1, 4], F32)
        le0 = lp.sb(f"{name}_le0", [1, 4], F32)
        cnt = lp.sb(f"{name}_cnt", [1, 1], F32)
        use = lp.sb(f"{name}_use", [1, 1], F32)
        ai = lp.sb(f"{name}_ai", [1, 1], F32)
        ai2 = lp.sb(f"{name}_ai2", [1, 1], F32)
        ea = lp.sb(f"{name}_ea", [1, 1], F32)
        m_row = lp.sb(f"{name}_mrow", [1, 3], F32)
        m_bc = lp.sb(f"{name}_mbc", [128, 3], F32)
        ones_f = lp.sb(f"{name}_onesf", [1, 128], F32)
        An = lp.sb(f"{name}_An", [16, 3 * 4096], BF16)
        Aeffn = lp.sb(f"{name}_Aeffn", [16, 4096], BF16)
        Bn = lp.sb(f"{name}_Bn", [128, 3 * 512], BF16)
        Beffn = lp.sb(f"{name}_Beffn", [128, 32 * 16], BF16)
        ps_a = lp.ps(f"{name}_psa", [128, 128], F32)
        ps_b = [lp.ps(f"{name}_psb{i}", [128, 128], BF16) for i in range(2)]

        dW, dX, pe, act, dve, pool = (p.s["dW"], p.s["dX"], p.s["pe"],
                                      p.s["act"], p.s["dve"], p.s["pool"])
        b = dict(dW=dW.n, dX=dX.n, pe=pe.n, act=act.n, dve=dve.n, pool=pool.n)

        with nc.Block() as blk:
            @blk.gpsimd
            def _(g):
                g.memset(ones_f[:, :], 1.0).then_inc(pool.h, 1)
                g.dma_start(lsi_i[0:1, :],
                            bass.AP(t_in["L1_ffn_lora_start_indices"], 0,
                                    [[1, 1], [1, 4]])).then_inc(dW.h, 16)
                g.dma_start(bias_row[0:1, :],
                            row_ap(t_in["L1_ffn_bias"], 4096)).then_inc(dW.h, 16)
                for a in range(3):
                    g.dma_start(An[0:16, a * 4096:(a + 1) * 4096],
                                bass.AP(t_in["L1_ffn_lora_A"], a * 16 * 4096,
                                        [[4096, 16], [1, 4096]])).then_inc(dW.h, 16)
                    g.dma_start(Bn[0:128, a * 512:(a + 1) * 512],
                                bass.AP(t_in["L1_ffn_lora_B"], a * 4096 * 16,
                                        [[16, 128], [2048, 32], [1, 16]])).then_inc(dW.h, 16)

            @blk.vector
            def _(v):
                v.wait_ge(dW.h, 16 * (b["dW"] + 8))
                v.tensor_copy(lsi_f[0:1, :], lsi_i[0:1, :]).then_inc(dve.h, 1)
                v.drain()
                v.tensor_scalar(le0[0:1, :], lsi_f[0:1, :], 0.0, None,
                                op0=ALU.is_le).then_inc(dve.h, 1)
                v.drain()
                v.reduce_sum(cnt[0:1, :], le0[0:1, :], axis=AX.X).then_inc(dve.h, 1)
                v.drain()
                v.tensor_scalar(use[0:1, :], cnt[0:1, :], 2.0, None,
                                op0=ALU.is_ge).then_inc(dve.h, 1)
                v.tensor_scalar(ai[0:1, :], cnt[0:1, :], 2.0, 0.0,
                                op0=ALU.subtract, op1=ALU.max).then_inc(dve.h, 1)
                v.drain()
                v.tensor_scalar(ai2[0:1, :], ai[0:1, :], 2.0, None,
                                op0=ALU.min).then_inc(dve.h, 1)
                v.drain()
                for a in range(3):
                    v.tensor_scalar(ea[0:1, :], ai2[0:1, :], float(a), None,
                                    op0=ALU.is_equal).then_inc(dve.h, 1)
                    v.drain()
                    v.tensor_mul(m_row[0:1, a:a + 1], ea[0:1, :],
                                 use[0:1, :]).then_inc(dve.h, 1)
                    v.drain()

            @blk.tensor
            def _(t):
                t.wait_ge(dve.h, b["dve"] + 12)
                t.wait_ge(pool.h, b["pool"] + 1)
                t.matmul(ps_a[:, 0:3], ones_f[0:1, :], m_row[0:1, :],
                         start=True, stop=True).then_inc(pe.h, 1)

            @blk.vector
            def _(v):
                v.wait_ge(pe.h, b["pe"] + 1)
                v.tensor_copy(m_bc[:, :], ps_a[:, 0:3]).then_inc(dve.h, 1)
                v.drain()
                v.tensor_scalar(Aeffn[0:16, :], An[0:16, 0:4096],
                                m_bc[0:16, 0:1], None, op0=ALU.mult).then_inc(dve.h, 1)
                v.tensor_scalar(Beffn[:, :], Bn[:, 0:512],
                                m_bc[:, 0:1], None, op0=ALU.mult).then_inc(dve.h, 1)
                v.drain()
                for a in range(1, 3):
                    v.scalar_tensor_tensor(Aeffn[0:16, :], An[0:16, a * 4096:(a + 1) * 4096],
                                           m_bc[0:16, a:a + 1], Aeffn[0:16, :],
                                           op0=ALU.mult, op1=ALU.add).then_inc(dve.h, 1)
                    v.scalar_tensor_tensor(Beffn[:, :], Bn[:, a * 512:(a + 1) * 512],
                                           m_bc[:, a:a + 1], Beffn[:, :],
                                           op0=ALU.mult, op1=ALU.add).then_inc(dve.h, 1)
                    v.drain()

            @blk.tensor
            def _(t):
                t.wait_ge(dve.h, b["dve"] + 19)
                for jc in range(32):
                    if jc >= 2:
                        t.wait_ge(dve.h, b["dve"] + 19 + (jc - 2) + 1)
                    t.transpose(ps_b[jc % 2][:, 0:16], Aeffn[0:16, jc * 128:(jc + 1) * 128],
                                C["ident_bf"][0:16, 0:16]).then_inc(pe.h, 1)
                for jc in range(32):
                    t.wait_ge(dve.h, b["dve"] + 19 + (32 + jc - 2) + 1)
                    t.transpose(ps_b[jc % 2][0:16, 0:128], Beffn[:, jc * 16:(jc + 1) * 16],
                                C["ident_bf"][:, :]).then_inc(pe.h, 1)

            @blk.vector
            def _(v):
                for jc in range(32):
                    v.wait_ge(pe.h, b["pe"] + 1 + jc + 1)
                    v.tensor_copy(A_effT[:, jc * 16:(jc + 1) * 16],
                                  ps_b[jc % 2][:, 0:16]).then_inc(dve.h, 1)
                for jc in range(32):
                    v.wait_ge(pe.h, b["pe"] + 33 + jc + 1)
                    v.tensor_copy(B_effT[0:16, jc * 128:(jc + 1) * 128],
                                  ps_b[jc % 2][0:16, 0:128]).then_inc(dve.h, 1)

        pe.n = b["pe"] + 65
        dve.n = b["dve"] + 19 + 64
        dW.n = b["dW"] + 8
        pool.n = b["pool"] + 1
    emit_fence(p)


def emit_l1_ffn_tmp(p, C, ln3_fm, A_effT, tmpT, name="l1t"):
    nc = p.nc
    with ExitStack() as lctx:
        lp = _Sub(p, lctx)
        ln_sb = lp.sb(f"{name}_ln", [128, 32 * 1024], BF16)
        tmp_bf = lp.sb(f"{name}_tmpbf", [128, 16], BF16)
        ps_t = lp.ps(f"{name}_pst", [128, 16], F32)
        ps_b = [lp.ps(f"{name}_psb{i}", [128, 128], BF16) for i in range(2)]
        dX, pe, dve = p.s["dX"], p.s["pe"], p.s["dve"]
        b0 = dict(dX=dX.n)
        with nc.Block() as blk:
            @blk.sync
            def _(sy):
                for c in range(32):
                    sy.dma_start(ln_sb[:, c * 1024:(c + 1) * 1024],
                                 ln3_fm[c * 128:(c + 1) * 128, :]).then_inc(dX.h, 16)
        dX.n = b0["dX"] + 32

        b2 = dict(pe=pe.n, dve=dve.n)
        with nc.Block() as blk:
            @blk.tensor
            def _(t):
                t.wait_ge(dX.h, 16 * (b0["dX"] + 32))
                for it in range(8):
                    if it > 0:
                        t.wait_ge(dve.h, b2["dve"] + (it - 1) * 2 + 1)
                    for dc in range(32):
                        t.matmul(ps_t[:, 0:16],
                                 ln_sb[:, dc * 1024 + it * 128:dc * 1024 + it * 128 + 128],
                                 A_effT[:, dc * 16:(dc + 1) * 16],
                                 start=(dc == 0), stop=(dc == 31)).then_inc(pe.h, 1)
                    t.wait_ge(dve.h, b2["dve"] + it * 2 + 1)
                    if it >= 2:
                        t.wait_ge(dve.h, b2["dve"] + (it - 2) * 2 + 2)
                    t.transpose(ps_b[it % 2][0:16, 0:128], tmp_bf[:, 0:16],
                                C["ident_bf"][:, :]).then_inc(pe.h, 1)

            @blk.vector
            def _(v):
                for it in range(8):
                    v.wait_ge(pe.h, b2["pe"] + it * 33 + 32)
                    v.tensor_copy(tmp_bf[:, 0:16], ps_t[:, 0:16]).then_inc(dve.h, 1)
                    v.wait_ge(pe.h, b2["pe"] + it * 33 + 33)
                    v.tensor_copy(tmpT[0:16, it * 128:(it + 1) * 128],
                                  ps_b[it % 2][0:16, 0:128]).then_inc(dve.h, 1)

        pe.n = b2["pe"] + 8 * 33
        dve.n = b2["dve"] + 16
    emit_fence(p)



def emit_prefetch(p, wpref, wTb, free_pe=None):
    nc = p.nc
    dWp, pe = p.s["dWp"], p.s["pe"]
    with nc.Block() as blk:
        @blk.gpsimd
        def _(g):
            if free_pe is not None:
                g.wait_ge(pe.h, free_pe)
            g.dma_start(wpref[:, :],
                        bass.AP(wTb, 0, [[16384, 128], [1, 16384]])).then_inc(dWp.h, 16)
    dWp.n += 1
    return dWp.n


def emit_big_linear_t(p, C, name, act_mode, act_dram, wTb, xin, xout,
                      n_extra=0, emit_extra=None, wpref=None, pref_cnt=None):
    nc = p.nc
    with ExitStack() as lctx:
        lp = _Sub(p, lctx)
        a_sb = lp.sb(f"{name}_a", [128, 32 * 1024], BF16)
        wT = [lp.sb(f"{name}_wT{i}", [128, 32 * 512], BF16) for i in range(2)]
        xi_t = lp.sb(f"{name}_xit", [128, 512], F32)
        xo_sb = lp.sb(f"{name}_xosb", [128, 2 * 512], F32)
        ps_m = [lp.ps(f"{name}_psm{i}", [128, 512], F32) for i in range(2)]

        dW, dX, dXs, pe, act, dve = (p.s["dW"], p.s["dX"], p.s["dXs"],
                                     p.s["pe"], p.s["act"], p.s["dve"])
        dWp = p.s["dWp"]
        st = [p.s[f"st{i}"] for i in range(4)]
        b = dict(dW=dW.n, dX=dX.n, dXs=dXs.n, pe=pe.n, dve=dve.n,
                 st=[x.n for x in st])
        G = 32 + n_extra
        NPE_OG = 8 * G
        skip0 = 1 if wpref is not None else 0

        def wbuf(og):
            return wpref if (og == 0 and wpref is not None) else wT[og % 2]

        def xo_evac(gi):
            return b["dve"] + gi + 1

        with nc.Block() as blk:
            @blk.sync
            def _(sy):
                if act_mode == "fm":
                    for c in range(32):
                        sy.dma_start(a_sb[:, c * 1024:(c + 1) * 1024],
                                     act_dram[c * 128:(c + 1) * 128, :]).then_inc(dX.h, 16)
                else:
                    for c in range(32):
                        sy.dma_start(a_sb[:, c * 1024:(c + 1) * 1024],
                                     bass.AP(act_dram, c * 128, [[4096, 1024], [1, 128]]),
                                     transpose=True).then_inc(dX.h, 16)

            @blk.gpsimd
            def _(g):
                for og in range(8):
                    if og == 0 and wpref is not None:
                        continue
                    if og >= 2:
                        g.wait_ge(pe.h, b["pe"] + (og - 1) * NPE_OG)
                    g.dma_start(wT[og % 2][:, :],
                                bass.AP(wTb, og * 128 * 16384,
                                        [[16384, 128], [1, 16384]])).then_inc(dW.h, 16)

            @blk.scalar
            def _(s):
                for og in range(8):
                    for tt in range(8):
                        gi = og * 8 + tt
                        if gi > 0:
                            s.wait_ge(dve.h, xo_evac(gi - 1))
                        s.dma_start(xi_t[:, :],
                                    xin[tt * 128:(tt + 1) * 128, og * 512:(og + 1) * 512]
                                    ).then_inc(dXs.h, 16)

            @blk.tensor
            def _(t):
                t.wait_ge(dX.h, 16 * (b["dX"] + 32))
                for og in range(8):
                    if og == 0 and wpref is not None:
                        t.wait_ge(dWp.h, 16 * pref_cnt)
                    else:
                        t.wait_ge(dW.h, 16 * (b["dW"] + og + 1 - skip0))
                    for tt in range(8):
                        gi = og * 8 + tt
                        if gi >= 2:
                            t.wait_ge(dve.h, xo_evac(gi - 2))
                        ps = ps_m[gi % 2]
                        if emit_extra is not None:
                            emit_extra(t, p, ps, og, tt)
                        for dc in range(32):
                            t.matmul(ps[:, :],
                                     a_sb[:, dc * 1024 + tt * 128:dc * 1024 + tt * 128 + 128],
                                     wbuf(og)[:, dc * 512:(dc + 1) * 512],
                                     start=(dc == 0 and n_extra == 0),
                                     stop=(dc == 31)).then_inc(pe.h, 1)

            @blk.vector
            def _(v):
                for og in range(8):
                    for tt in range(8):
                        gi = og * 8 + tt
                        v.wait_ge(pe.h, b["pe"] + (gi + 1) * G)
                        v.wait_ge(dXs.h, 16 * (b["dXs"] + gi + 1))
                        sl = gi % 2
                        v.wait_ge(st[sl].h, 16 * (b["st"][sl] + gi // 2))
                        v.tensor_add(xo_sb[:, sl * 512:(sl + 1) * 512],
                                     ps_m[gi % 2][:, :], xi_t[:, :]).then_inc(dve.h, 1)

            @blk.sync
            def _(sy):
                for og in range(8):
                    for tt in range(8):
                        gi = og * 8 + tt
                        sy.wait_ge(dve.h, xo_evac(gi))
                        sl = gi % 2
                        sy.dma_start(xout[tt * 128:(tt + 1) * 128, og * 512:(og + 1) * 512],
                                     xo_sb[:, sl * 512:(sl + 1) * 512]).then_inc(st[sl].h, 16)

        pe.n = b["pe"] + 8 * NPE_OG
        dve.n = b["dve"] + 64
        dW.n = b["dW"] + 8 - skip0
        dX.n = b["dX"] + 32
        dXs.n = b["dXs"] + 64
        for i in range(2):
            st[i].n = b["st"][i] + 32
    emit_fence(p)
    return b["pe"] + NPE_OG


def emit_l1_qkv_t(p, C, wTb_q, wTb_k, wTb_v, ln2_fm, q_dram, kv_c, kn_c,
                  qnm_row, name="l1q"):
    nc = p.nc
    with ExitStack() as lctx:
        lp = _Sub(p, lctx)
        ln_sb = lp.sb(f"{name}_ln", [128, 32 * 1024], BF16)
        wT = [lp.sb(f"{name}_wT{i}", [128, 32 * 512], BF16) for i in range(2)]
        ost = lp.sb(f"{name}_ost", [128, 4 * 512], BF16)
        k2_sb = lp.sb(f"{name}_k2", [128, 512], F32)
        kn_tm = lp.sb(f"{name}_kntm", [128, 8 * 32], F32)
        kn_T = lp.sb(f"{name}_knT", [32, 8 * 128], F32)
        q2_sb = lp.sb(f"{name}_q2", [128, 512], F32)
        ones_colf = lp.sb(f"{name}_onescf", [128, 1], F32)
        qtmp = lp.sb(f"{name}_qtmp", [1, 1], F32)
        ps_m = [lp.ps(f"{name}_psm{i}", [128, 512], F32) for i in range(2)]
        ps_qn = lp.ps(f"{name}_psqn", [1, 512], F32)
        ps_kn = lp.ps(f"{name}_pskn", [32, 128], F32)

        dW, dX, dO, pe, act, dve, pool = (p.s["dW"], p.s["dX"], p.s["dO"],
                                          p.s["pe"], p.s["act"], p.s["dve"],
                                          p.s["pool"])
        st = [p.s[f"st{i}"] for i in range(4)]
        b = dict(dW=dW.n, dX=dX.n, dO=dO.n, pe=pe.n, act=act.n, dve=dve.n,
                 pool=pool.n, st=[x.n for x in st])

        NPE_SEG = [264, 256, 256]
        NDVE_SEG = [20, 32, 8]
        NACT_SEG = [8, 16, 0]
        PE_OFF, DVE_OFF, ACT_OFF = [0], [0], [0]
        for s_ in range(24):
            pr = s_ // 8
            PE_OFF.append(PE_OFF[-1] + NPE_SEG[pr])
            DVE_OFF.append(DVE_OFF[-1] + NDVE_SEG[pr])
            ACT_OFF.append(ACT_OFF[-1] + NACT_SEG[pr])

        def grp_consumed(s_, gr):
            pr = s_ // 8
            if pr == 0:
                dv = b["dve"] + DVE_OFF[s_] + _q_dve_off(gr) + 1
                ac = b["act"] + ACT_OFF[s_] + gr + 1
                return dv, ac
            if pr == 1:
                dv = b["dve"] + DVE_OFF[s_] + gr * 4 + 4
                ac = b["act"] + ACT_OFF[s_] + gr * 2 + 2
                return dv, ac
            return b["dve"] + DVE_OFF[s_] + gr + 1, None

        def _q_dve_off(gr):
            return sum(3 if g % 2 == 1 else 2 for g in range(gr))

        wsrc = [wTb_q, wTb_k, wTb_v]

        with nc.Block() as blk:
            @blk.sync
            def _(sy):
                for c in range(32):
                    sy.dma_start(ln_sb[:, c * 1024:(c + 1) * 1024],
                                 ln2_fm[c * 128:(c + 1) * 128, :]).then_inc(dX.h, 16)

            @blk.gpsimd
            def _(g):
                g.memset(ones_colf[:, :], 1.0).then_inc(pool.h, 1)
                for s_ in range(24):
                    pr, eg = s_ // 8, s_ % 8
                    if s_ >= 2:
                        g.wait_ge(pe.h, b["pe"] + PE_OFF[s_ - 1])
                    g.dma_start(wT[s_ % 2][:, :],
                                bass.AP(wsrc[pr], eg * 128 * 16384,
                                        [[16384, 128], [1, 16384]])).then_inc(dW.h, 16)

            @blk.tensor
            def _(t):
                t.wait_ge(dX.h, 16 * (b["dX"] + 32))
                for s_ in range(24):
                    pr, eg = s_ // 8, s_ % 8
                    t.wait_ge(dW.h, 16 * (b["dW"] + s_ + 1))
                    for gr in range(8):
                        gi = s_ * 8 + gr
                        if gi >= 2:
                            s2, g2 = (gi - 2) // 8, (gi - 2) % 8
                            dv, ac = grp_consumed(s2, g2)
                            t.wait_ge(dve.h, dv)
                            if ac is not None:
                                t.wait_ge(act.h, ac)
                        ps = ps_m[gi % 2]
                        if pr == 0:
                            ec, th = gr // 2, gr % 2
                            for dc in range(32):
                                t.matmul(ps[:, :],
                                         wT[s_ % 2][:, dc * 512 + ec * 128:dc * 512 + ec * 128 + 128],
                                         ln_sb[:, dc * 1024 + th * 512:dc * 1024 + th * 512 + 512],
                                         start=(dc == 0), stop=(dc == 31)).then_inc(pe.h, 1)
                            t.wait_ge(act.h, b["act"] + ACT_OFF[s_] + gr + 1)
                            if gr > 0:
                                t.wait_ge(dve.h, b["dve"] + DVE_OFF[s_] + _q_dve_off(gr - 1) + 2)
                            t.matmul(ps_qn[0:1, :], ones_colf[:, 0:1], q2_sb[:, :],
                                     start=True, stop=True).then_inc(pe.h, 1)
                        else:
                            tt = gr
                            for dc in range(32):
                                t.matmul(ps[:, :],
                                         ln_sb[:, dc * 1024 + tt * 128:dc * 1024 + tt * 128 + 128],
                                         wT[s_ % 2][:, dc * 512:(dc + 1) * 512],
                                         start=(dc == 0), stop=(dc == 31)).then_inc(pe.h, 1)
                t.wait_ge(dve.h, b["dve"] + DVE_OFF[16])
                for tt in range(8):
                    if tt > 0:
                        t.wait_ge(dve.h, b["dve"] + DVE_OFF[24] + tt)
                    t.transpose(ps_kn[0:32, :], kn_tm[:, tt * 32:(tt + 1) * 32],
                                C["ident_f"][:, :]).then_inc(pe.h, 1)

            @blk.scalar
            def _(s):
                for s_ in range(24):
                    pr, eg = s_ // 8, s_ % 8
                    if pr == 0:
                        for gr in range(8):
                            gi = s_ * 8 + gr
                            s.wait_ge(pe.h, b["pe"] + PE_OFF[s_] + gr * 33 + 32)
                            if gi > 0:
                                s2, g2 = (gi - 1) // 8, (gi - 1) % 8
                                s.wait_ge(pe.h, b["pe"] + PE_OFF[s2] + g2 * 33 + 33)
                            s.wait_ge(dve.h, b["dve"] + DVE_OFF[s_] + _q_dve_off(gr) + 1)
                            s.activation(q2_sb[:, :], ps_m[gi % 2][:, :],
                                         AF.Square).then_inc(act.h, 1)
                    elif pr == 1:
                        for gr in range(8):
                            gi = s_ * 8 + gr
                            s.wait_ge(pe.h, b["pe"] + PE_OFF[s_] + gr * 32 + 32)
                            sl = gi % 4
                            s.wait_ge(st[sl].h, 16 * (b["st"][sl] + max(0, (gi - 4) // 4 + 1)))
                            s.activation(ost[:, sl * 512:(sl + 1) * 512],
                                         ps_m[gi % 2][:, :], AF.Identity).then_inc(act.h, 1)
                            if gi > 64:
                                s.wait_ge(dve.h, b["dve"] + DVE_OFF[(gi - 1) // 8] + ((gi - 1) % 8) * 4 + 4)
                            s.activation(k2_sb[:, :], ps_m[gi % 2][:, :],
                                         AF.Square).then_inc(act.h, 1)

            @blk.vector
            def _(v):
                for s_ in range(24):
                    pr, eg = s_ // 8, s_ % 8
                    for gr in range(8):
                        gi = s_ * 8 + gr
                        if pr == 0:
                            ec, th = gr // 2, gr % 2
                            h = eg * 4 + ec
                            v.wait_ge(pe.h, b["pe"] + PE_OFF[s_] + gr * 33 + 32)
                            sl = gi % 4
                            v.wait_ge(st[sl].h, 16 * (b["st"][sl] + max(0, (gi - 4) // 4 + 1)))
                            v.tensor_scalar(ost[:, sl * 512:(sl + 1) * 512],
                                            ps_m[gi % 2][:, :], 0.08838834764831845,
                                            None, op0=ALU.mult).then_inc(dve.h, 1)
                            v.wait_ge(pe.h, b["pe"] + PE_OFF[s_] + gr * 33 + 33)
                            if th == 0:
                                v.reduce_max(qnm_row[0:1, h:h + 1], ps_qn[0:1, :],
                                             axis=AX.X).then_inc(dve.h, 1)
                            else:
                                v.reduce_max(qtmp[0:1, 0:1], ps_qn[0:1, :],
                                             axis=AX.X).then_inc(dve.h, 1)
                                v.drain()
                                v.tensor_max(qnm_row[0:1, h:h + 1],
                                             qnm_row[0:1, h:h + 1],
                                             qtmp[0:1, 0:1]).then_inc(dve.h, 1)
                        elif pr == 1:
                            tt = gr
                            v.wait_ge(act.h, b["act"] + ACT_OFF[s_] + gr * 2 + 2)
                            for jj in range(4):
                                h = eg * 4 + jj
                                v.tensor_reduce(kn_tm[:, tt * 32 + h:tt * 32 + h + 1],
                                                k2_sb[:, jj * 128:(jj + 1) * 128],
                                                axis=AX.X, op=ALU.add).then_inc(dve.h, 1)
                        else:
                            tt = gr
                            v.wait_ge(pe.h, b["pe"] + PE_OFF[s_] + gr * 32 + 32)
                            sl = gi % 4
                            v.wait_ge(st[sl].h, 16 * (b["st"][sl] + max(0, (gi - 4) // 4 + 1)))
                            v.tensor_copy(ost[:, sl * 512:(sl + 1) * 512],
                                          ps_m[gi % 2][:, :]).then_inc(dve.h, 1)
                for tt in range(8):
                    v.wait_ge(pe.h, b["pe"] + PE_OFF[24] + tt + 1)
                    v.tensor_copy(kn_T[0:32, tt * 128:(tt + 1) * 128],
                                  ps_kn[0:32, :]).then_inc(dve.h, 1)

            @blk.sync
            def _(sy):
                for s_ in range(24):
                    pr, eg = s_ // 8, s_ % 8
                    for gr in range(8):
                        gi = s_ * 8 + gr
                        sl = gi % 4
                        if pr == 0:
                            ec, th = gr // 2, gr % 2
                            h = eg * 4 + ec
                            sy.wait_ge(dve.h, b["dve"] + DVE_OFF[s_] + _q_dve_off(gr) + 1)
                            dst = bass.AP(q_dram, h * 128 * 1024 + th * 512,
                                          [[1024, 128], [1, 512]])
                        elif pr == 1:
                            tt = gr
                            sy.wait_ge(act.h, b["act"] + ACT_OFF[s_] + gr * 2 + 1)
                            dst = bass.AP(kv_c, tt * 128 * 4096 + eg * 512,
                                          [[4096, 128], [1, 512]])
                        else:
                            tt = gr
                            sy.wait_ge(dve.h, b["dve"] + DVE_OFF[s_] + gr + 1)
                            dst = bass.AP(kv_c, (1024 + tt * 128) * 4096 + eg * 512,
                                          [[4096, 128], [1, 512]])
                        sy.dma_start(dst, ost[:, sl * 512:(sl + 1) * 512]).then_inc(st[sl].h, 16)
                sy.wait_ge(dve.h, b["dve"] + DVE_OFF[24] + 8)
                sy.dma_start(kn_c[0:32, :], kn_T[0:32, :]).then_inc(dO.h, 16)

        dX.n = b["dX"] + 32
        dW.n = b["dW"] + 24
        pe.n = b["pe"] + PE_OFF[24] + 8
        act.n = b["act"] + ACT_OFF[24]
        dve.n = b["dve"] + DVE_OFF[24] + 8
        dO.n = b["dO"] + 1
        for i in range(4):
            st[i].n = b["st"][i] + 48
        pool.n = b["pool"] + 1
    emit_fence(p)



LN0_STARTS = [0, 128, 256, 384, 512, 640, 768, 896, 898]
LN_STARTS = [0, 128, 256, 384, 512, 640, 768, 896]

IN_SPECS = {
    "L0_attn_conv_w": [3, 4096, 3072], "L0_attn_conv_b": [3072],
    "L0_attn_wq": [1024, 16, 64], "L0_attn_bq": [16, 64],
    "L0_attn_wk": [1024, 16, 64], "L0_attn_bk": [16, 64],
    "L0_attn_wv": [1024, 16, 64], "L0_attn_bv": [16, 64],
    "L0_attn_wo": [16, 64, 1024], "L0_attn_bo": [1024],
    "L0_attn_proj_w": [1024, 4096], "L0_attn_proj_b": [4096],
    "L0_ffn_weight": [4096, 4096],
    "L0_ffn_lora_A": [8, 4096, 16], "L0_ffn_lora_B": [8, 16, 4096],
    "L0_ffn_gw1": [32, 4096], "L0_ffn_gb1": [32],
    "L0_ffn_ln_gamma": [32], "L0_ffn_ln_beta": [32],
    "L0_ffn_gw2": [8, 32], "L0_ffn_gb2": [8],
    "L1_attn_w_q": [4096, 4096], "L1_attn_w_k": [4096, 4096],
    "L1_attn_w_v": [4096, 4096], "L1_attn_w_o": [4096, 4096],
    "L1_ffn_weight": [4096, 4096], "L1_ffn_bias": [4096],
    "L1_ffn_lora_A": [3, 16, 4096], "L1_ffn_lora_B": [3, 4096, 16],
    "L1_ffn_lora_start_indices": [4],
    "L0_ffn_weightTb": [1024, 16384],
    "L1_attn_w_qTb": [1024, 16384], "L1_attn_w_kTb": [1024, 16384],
    "L1_attn_w_vTb": [1024, 16384], "L1_attn_w_oTb": [1024, 16384],
    "L1_ffn_weightTb": [1024, 16384],
}

FUSED_IN = [nm for nm in IN_SPECS
            if nm not in ("L0_ffn_weight", "L1_attn_w_q", "L1_attn_w_k",
                          "L1_attn_w_v", "L1_attn_w_o", "L1_ffn_weight")]


def declare_inputs(nc, names):
    t = {}
    for nm in names:
        dt = I32 if nm == "L1_ffn_lora_start_indices" else F32
        t[nm] = nc.dram_tensor(nm, IN_SPECS[nm], dt, kind="ExternalInput")
    return t


def build_l0a(upto="conv", dbg=()):
    nc = bass.Bass("TRN2", target_bir_lowering=False)
    ctx = ExitStack()
    t_in = declare_inputs(nc, [
        "L0_attn_conv_w", "L0_attn_conv_b", "L0_attn_wq", "L0_attn_bq",
        "L0_attn_wk", "L0_attn_bk", "L0_attn_wv", "L0_attn_bv",
        "L0_attn_wo", "L0_attn_bo", "L0_attn_proj_w", "L0_attn_proj_b",
    ])
    x0s = nc.dram_tensor("x0s", [1026, 4096], F32, kind="ExternalInput")

    def dram(nm, shape, dt):
        kind = "ExternalOutput" if nm in dbg else "Internal"
        return nc.dram_tensor(nm, shape, dt, kind=kind)

    ln0_fm = dram("ln0_fm", [4096, 1040], BF16)
    conv_fm = dram("conv_fm", [3072, 1024], BF16)
    qh_fm = dram("qh_fm", [1024, 1024], BF16)
    kh_fm = dram("kh_fm", [1024, 1024], BF16)
    vh_tm = dram("vh_tm", [1024, 1024], BF16)
    o_fm = dram("o_fm", [1024, 1024], BF16)
    x1 = dram("x1", [1024, 4096], F32)

    p = Prog(nc, ctx)
    C = {}
    emit_setup_core(p, C)
    emit_setup_l0a(p, t_in, C)
    emit_ln_to_fm(p, C, x0s, LN0_STARTS, ln0_fm, 0, "ln0")
    emit_zero_cols(p, ln0_fm, 1026, "ln0pad")
    emit_fence(p)
    if upto == "ln0":
        ctx.close()
        return nc
    emit_conv(p, C, ln0_fm, t_in["L0_attn_conv_w"], conv_fm)
    if upto == "conv":
        ctx.close()
        return nc
    emit_l0_qkv(p, C, conv_fm, t_in["L0_attn_wq"], t_in["L0_attn_wk"],
                t_in["L0_attn_wv"], qh_fm, kh_fm, vh_tm)
    if upto == "qkv":
        ctx.close()
        return nc
    emit_l0_battn(p, C, qh_fm, kh_fm, vh_tm, o_fm)
    if upto == "battn":
        ctx.close()
        return nc
    emit_l0_out(p, C, o_fm, t_in["L0_attn_wo"], t_in["L0_attn_proj_w"], x0s, x1)
    ctx.close()
    return nc


def build_l0f(upto="x2", dbg=(), rg=None):
    nc = bass.Bass("TRN2", target_bir_lowering=False)
    ctx = ExitStack()
    t_in = declare_inputs(nc, [
        "L0_ffn_weight", "L0_ffn_lora_A", "L0_ffn_lora_B", "L0_ffn_gw1",
        "L0_ffn_gb1", "L0_ffn_ln_gamma", "L0_ffn_ln_beta", "L0_ffn_gw2",
        "L0_ffn_gb2",
    ])
    x1 = nc.dram_tensor("x1", [1024, 4096], F32, kind="ExternalInput")

    def dram(nm, shape, dt):
        kind = "ExternalOutput" if nm in dbg else "Internal"
        return nc.dram_tensor(nm, shape, dt, kind=kind)

    ln1_fm = dram("ln1_fm", [4096, 1024], BF16)
    x2 = dram("x2", [1024, 4096], F32)
    dbg_w = dram("w_dbg", [128, 64], F32) if "w_dbg" in dbg else None
    cc_in = nc.dram_tensor("cc_in", [16, 1], F32)
    cc_out = nc.dram_tensor("cc_out", [16, 1], F32)

    p = Prog(nc, ctx)
    C = {}
    emit_setup_core(p, C)
    emit_ln_to_fm(p, C, x1, LN_STARTS, ln1_fm, 0, "ln1")
    emit_fence(p)
    emit_l0_ffn(p, C, t_in, x1, ln1_fm, x2, cc_in, cc_out, dbg_w=dbg_w, rg=rg)
    ctx.close()
    return nc


def build_l1a(dbg=(), rg=None, upto="x3", skq=()):
    nc = bass.Bass("TRN2", target_bir_lowering=False)
    ctx = ExitStack()
    t_in = declare_inputs(nc, [
        "L1_attn_w_q", "L1_attn_w_k", "L1_attn_w_v", "L1_attn_w_o",
    ])
    x2 = nc.dram_tensor("x2", [1024, 4096], F32, kind="ExternalInput")
    NR = len(rg[0]) if rg is not None else NCORES

    def dram(nm, shape, dt):
        kind = "ExternalOutput" if nm in dbg else "Internal"
        return nc.dram_tensor(nm, shape, dt, kind=kind)

    ln2_fm = dram("ln2_fm", [4096, 1024], BF16)
    q_dram = dram("q_dram", [4096, 1024], BF16)
    kv_c = dram("kv_c", [2048, 4096], BF16)
    kvg = dram("kvg", [NR * 2048, 4096], BF16)
    kn_c = dram("kn_c", [32, 1024], F32)
    kng = dram("kng", [NR * 32, 1024], F32)
    o2_tm = dram("o2_tm", [1024, 4096], BF16)
    x3 = dram("x3", [1024, 4096], F32)

    p = Prog(nc, ctx)
    C = {}
    emit_setup_core(p, C)
    qnm_row = p.sb("l1_qnm_row", [1, 32], F32)
    negc_bc = p.sb("l1_negc_bc", [128, 32], F32)
    emit_ln_to_fm(p, C, x2, LN_STARTS, ln2_fm, 0, "ln2")
    emit_fence(p)
    emit_l1_qkv(p, C, t_in, ln2_fm, q_dram, kv_c, kn_c, qnm_row, sk=skq)
    if upto == "qkv":
        ctx.close()
        return nc
    emit_l1_gather(p, C, kv_c, kvg, kn_c, kng, qnm_row, negc_bc, rg=rg)
    if upto == "gather":
        ctx.close()
        return nc
    emit_l1_flash(p, C, q_dram, kvg, negc_bc, o2_tm, NR=NR)
    if upto == "flash":
        ctx.close()
        return nc
    emit_big_linear(p, C, "oproj", "tm_transpose", o2_tm,
                    t_in["L1_attn_w_o"], x2, x3)
    ctx.close()
    return nc


def build_l1ffn(dbg=()):
    nc = bass.Bass("TRN2", target_bir_lowering=False)
    ctx = ExitStack()
    t_in = declare_inputs(nc, [
        "L1_ffn_weight", "L1_ffn_bias", "L1_ffn_lora_A", "L1_ffn_lora_B",
        "L1_ffn_lora_start_indices",
    ])
    x3 = nc.dram_tensor("x3", [1024, 4096], F32, kind="ExternalInput")
    out = nc.dram_tensor("out", [1024, 4096], F32, kind="ExternalOutput")

    def dram(nm, shape, dt):
        kind = "ExternalOutput" if nm in dbg else "Internal"
        return nc.dram_tensor(nm, shape, dt, kind=kind)

    ln3_fm = dram("ln3_fm", [4096, 1024], BF16)

    p = Prog(nc, ctx)
    C = {}
    emit_setup_core(p, C)
    bias_row = p.sb("l1f_bias_row", [1, 4096], BF16)
    tmpT = p.sb("l1f_tmpT", [16, 8 * 128], BF16)
    B_effT = p.sb("l1f_BeffT", [16, 4096], BF16)
    emit_ln_to_fm(p, C, x3, LN_STARTS, ln3_fm, 0, "ln3")
    emit_fence(p)
    emit_l1_ffn_pre(p, C, t_in, ln3_fm, bias_row, tmpT, B_effT)

    def extra(t, pp, ps, og, tt):
        t.matmul(ps[:, :], C["ones_bf"][0:1, :],
                 bias_row[0:1, og * 512:(og + 1) * 512],
                 start=True, stop=False).then_inc(pp.s["pe"].h, 1)
        t.matmul(ps[:, :], tmpT[0:16, tt * 128:(tt + 1) * 128],
                 B_effT[0:16, og * 512:(og + 1) * 512],
                 start=False, stop=False).then_inc(pp.s["pe"].h, 1)

    emit_big_linear(p, C, "l1ffn", "fm", ln3_fm, t_in["L1_ffn_weight"],
                    x3, out, n_extra=2, emit_extra=extra)
    ctx.close()
    return nc


def build_fused():
    nc = bass.Bass("TRN2", target_bir_lowering=False)
    ctx = ExitStack()
    t_in = declare_inputs(nc, FUSED_IN)
    x0s = nc.dram_tensor("x0s", [1026, 4096], F32, kind="ExternalInput")
    out = nc.dram_tensor("out", [1024, 4096], F32, kind="ExternalOutput")

    def dram(nm, shape, dt):
        return nc.dram_tensor(nm, shape, dt, kind="Internal")

    ln0_fm = dram("ln0_fm", [4096, 1040], BF16)
    conv_fm = dram("conv_fm", [3072, 1024], BF16)
    qh_fm = dram("qh_fm", [1024, 1024], BF16)
    kh_fm = dram("kh_fm", [1024, 1024], BF16)
    vh_tm = dram("vh_tm", [1024, 1024], BF16)
    o_fm = dram("o_fm", [1024, 1024], BF16)
    x1 = dram("x1", [1024, 4096], F32)
    ln1_fm = dram("ln1_fm", [4096, 1024], BF16)
    x2 = dram("x2", [1024, 4096], F32)
    cc_in = dram("cc_in", [16, 1], F32)
    cc_out = dram("cc_out", [16, 1], F32)
    ln2_fm = dram("ln2_fm", [4096, 1024], BF16)
    q_dram = dram("q_dram", [4096, 1024], BF16)
    k_c = dram("k_c", [4096, 1024], BF16)
    v_c = dram("v_c", [1024, 4096], BF16)
    kg = dram("kg", [NCORES * 4096, 1024], BF16)
    vg = dram("vg", [NCORES * 1024, 4096], BF16)
    kn_c = dram("kn_c", [32, 1024], F32)
    kng = dram("kng", [NCORES * 32, 1024], F32)
    o2_tm = dram("o2_tm", [1024, 4096], BF16)
    x3 = dram("x3", [1024, 4096], F32)
    ln3_fm = dram("ln3_fm", [4096, 1024], BF16)

    p = Prog(nc, ctx)
    C = {}
    emit_setup_core(p, C)
    emit_setup_l0a(p, t_in, C)
    qnm_row = p.sb("l1_qnm_row", [1, 32], F32)
    negc_bc = p.sb("l1_negc_bc", [128, 32], F32)
    bias_row = p.sb("l1f_bias_row", [1, 4096], BF16)
    tmpT = p.sb("l1f_tmpT", [16, 8 * 128], BF16)
    B_effT = p.sb("l1f_BeffT", [16, 4096], BF16)
    A_effT = p.sb("l1f_AeffT", [128, 32 * 16], BF16)
    wpref = p.sb("wpref", [128, 16384], BF16)

    emit_ln_to_fm(p, C, x0s, LN0_STARTS, ln0_fm, 0, "ln0")
    emit_zero_cols(p, ln0_fm, 1026, "ln0pad")
    emit_fence(p)
    emit_conv(p, C, ln0_fm, t_in["L0_attn_conv_w"], conv_fm)
    emit_l0_qkv(p, C, conv_fm, t_in["L0_attn_wq"], t_in["L0_attn_wk"],
                t_in["L0_attn_wv"], qh_fm, kh_fm, vh_tm)
    emit_l0_battn(p, C, qh_fm, kh_fm, vh_tm, o_fm)
    emit_l0_out(p, C, o_fm, t_in["L0_attn_wo"], t_in["L0_attn_proj_w"], x0s, x1)

    emit_ln_to_fm(p, C, x1, LN_STARTS, ln1_fm, 0, "ln1")
    emit_fence(p)
    l0ffn_free = emit_l0_ffn(p, C, t_in, x1, ln1_fm, x2, cc_in, cc_out,
                             wTb=t_in["L0_ffn_weightTb"], wpref_sb=wpref)

    pf_qkv = emit_prefetch(p, wpref, t_in["L1_attn_w_kTb"], free_pe=l0ffn_free)
    emit_l1_ffn_pre_w(p, C, t_in, bias_row, A_effT, B_effT)
    emit_ln_to_fm(p, C, x2, LN_STARTS, ln2_fm, 0, "ln2")
    emit_fence(p)
    cc_base = p.s["cc"].n
    qkv_free = emit_l1_qkv_t2(p, C, t_in["L1_attn_w_kTb"], t_in["L1_attn_w_vTb"],
                              t_in["L1_attn_w_qTb"], ln2_fm, k_c, v_c, q_dram,
                              kn_c, qnm_row, kg, vg, kng,
                              wpref=wpref, pref_cnt=pf_qkv)
    pf_oproj = emit_prefetch(p, wpref, t_in["L1_attn_w_oTb"], free_pe=qkv_free)
    negc_dve = emit_l1_negc(p, C, kng, qnm_row, negc_bc, cc_base)
    emit_l1_flash2(p, C, q_dram, kg, vg, negc_bc, o2_tm, NR=NCORES,
                   cc_base=cc_base, negc_dve=negc_dve)
    oproj_free = emit_big_linear_t(p, C, "oproj", "tm_transpose", o2_tm,
                                   t_in["L1_attn_w_oTb"], x2, x3,
                                   wpref=wpref, pref_cnt=pf_oproj)
    pf_l1f = emit_prefetch(p, wpref, t_in["L1_ffn_weightTb"], free_pe=oproj_free)

    emit_ln_to_fm(p, C, x3, LN_STARTS, ln3_fm, 0, "ln3")
    emit_fence(p)
    emit_l1_ffn_tmp(p, C, ln3_fm, A_effT, tmpT)

    def extra(t, pp, ps, og, tt):
        t.matmul(ps[:, :], C["ones_bf"][0:1, :],
                 bias_row[0:1, og * 512:(og + 1) * 512],
                 start=True, stop=False).then_inc(pp.s["pe"].h, 1)
        t.matmul(ps[:, :], tmpT[0:16, tt * 128:(tt + 1) * 128],
                 B_effT[0:16, og * 512:(og + 1) * 512],
                 start=False, stop=False).then_inc(pp.s["pe"].h, 1)

    emit_big_linear_t(p, C, "l1ffn", "fm", ln3_fm, t_in["L1_ffn_weightTb"],
                      x3, out, n_extra=2, emit_extra=extra,
                      wpref=wpref, pref_cnt=pf_l1f)
    ctx.close()
    return nc


def _blocked_T(w):
    return np.ascontiguousarray(
        w.T.reshape(32, 128, 8, 512).transpose(2, 1, 0, 3).reshape(1024, 16384))


def _prep_inputs(inputs):
    ins = {k: np.ascontiguousarray(v) for k, v in inputs.items()}
    lsi = ins["L1_ffn_lora_start_indices"]
    if lsi.dtype == np.int64:
        ins["L1_ffn_lora_start_indices"] = np.ascontiguousarray(
            lsi.view(np.int32)[0::2])
    return ins


G1_W = ["L0_attn_conv_w", "L0_attn_conv_b", "L0_attn_wq", "L0_attn_bq",
        "L0_attn_wk", "L0_attn_bk", "L0_attn_wv", "L0_attn_bv",
        "L0_attn_wo", "L0_attn_bo", "L0_attn_proj_w", "L0_attn_proj_b"]
G2_W = ["L0_ffn_weight", "L0_ffn_lora_A", "L0_ffn_lora_B", "L0_ffn_gw1",
        "L0_ffn_gb1", "L0_ffn_ln_gamma", "L0_ffn_ln_beta", "L0_ffn_gw2",
        "L0_ffn_gb2"]
G3_W = ["L1_attn_w_q", "L1_attn_w_k", "L1_attn_w_v", "L1_attn_w_o"]
G4_W = ["L1_ffn_weight", "L1_ffn_bias", "L1_ffn_lora_A", "L1_ffn_lora_B",
        "L1_ffn_lora_start_indices"]


TB_SRC = {
    "L0_ffn_weightTb": "L0_ffn_weight",
    "L1_attn_w_qTb": "L1_attn_w_q", "L1_attn_w_kTb": "L1_attn_w_k",
    "L1_attn_w_vTb": "L1_attn_w_v", "L1_attn_w_oTb": "L1_attn_w_o",
    "L1_ffn_weightTb": "L1_ffn_weight",
}


def kernel(**inputs):
    ins = _prep_inputs(inputs)
    for tb, src in TB_SRC.items():
        ins[tb] = _blocked_T(ins[src])
    x0 = ins["L0_attn_inputs"].reshape(8192, 4096)
    x0p = np.pad(x0, ((1, 1), (0, 0)))
    cores = list(range(NCORES))

    nc = build_fused()
    maps = []
    for c in cores:
        m = {nm: ins[nm] for nm in FUSED_IN}
        m["x0s"] = np.ascontiguousarray(x0p[c * 1024:c * 1024 + 1026])
        maps.append(m)
    res = run_bass_kernel_spmd(nc, maps, core_ids=cores)
    outs = [res.results[c]["out"] for c in cores]

    return np.concatenate(outs, axis=0).reshape(1, 8192, 4096)


# version 12
# speedup vs baseline: 1.0235x; 1.0235x over previous
import numpy as np
from contextlib import ExitStack

import concourse.bass as bass
import concourse.mybir as mybir
from concourse.bass_utils import run_bass_kernel_spmd

F32 = mybir.dt.float32
BF16 = mybir.dt.bfloat16
I32 = mybir.dt.int32
AF = mybir.ActivationFunctionType
ALU = mybir.AluOpType
AX = mybir.AxisListType

NCORES = 8
TOK = 1024
D = 4096
NEG = -1.0e9


class Sem:
    def __init__(self, handle):
        self.h = handle
        self.n = 0


class Prog:
    def __init__(self, nc, ctx):
        self.nc = nc
        self.ctx = ctx
        names = ["dW", "dW2", "dX", "dXs", "dS", "dO", "dOg", "pe", "act",
                 "dve", "pool", "cc", "st0", "st1", "st2", "st3",
                 "kv0", "kv1", "vv0", "vv1", "dWp"]
        self.s = {nm: Sem(ctx.enter_context(nc.semaphore(nm))) for nm in names}

    def sb(self, name, shape, dt):
        return self.ctx.enter_context(self.nc.sbuf_tensor(name, shape, dt))

    def ps(self, name, shape, dt=F32):
        return self.ctx.enter_context(self.nc.psum_tensor(name, shape, dt))


def row_ap(handle, n, off=0):
    return bass.AP(handle, off, [[1, 1], [1, n]])


def emit_fence(p):
    nc = p.nc
    dma_sems = {"dW", "dW2", "dX", "dXs", "dS", "dO", "dOg", "st0", "st1",
                "st2", "st3", "kv0", "kv1", "vv0", "vv1"}
    targets = [(s.h, (16 if nm in dma_sems else 1) * s.n)
               for nm, s in p.s.items() if s.n > 0 and nm not in ("cc", "dWp")]
    with nc.Block() as blk:
        for eng_name in ["sync", "gpsimd", "scalar", "vector", "tensor"]:
            def body(eng, targets=targets):
                for h, v in targets:
                    eng.wait_ge(h, v)
            getattr(blk, eng_name)(body)



def emit_setup_core(p, C):
    nc = p.nc
    C["ident_bf"] = p.sb("c_ident_bf", [128, 128], BF16)
    C["ident_f"] = p.sb("c_ident_f", [128, 128], F32)
    C["mask"] = p.sb("c_mask", [128, 128], F32)
    C["ones_bf"] = p.sb("c_ones_bf", [1, 128], BF16)
    C["eps"] = p.sb("c_eps", [128, 1], F32)
    C["convb_fm"] = p.sb("c_convb_fm", [128, 24], F32)
    C["bq_fm"] = p.sb("c_bq_fm", [128, 8], F32)
    C["bk_fm"] = p.sb("c_bk_fm", [128, 8], F32)
    C["bv_row"] = p.sb("c_bv_row", [1, 1024], BF16)
    C["bo_fm"] = p.sb("c_bo_fm", [128, 8], F32)
    C["projb_row"] = p.sb("c_projb_row", [1, 4096], BF16)
    _setup_ctx = ExitStack()
    lp = _Sub(p, _setup_ctx)
    with p.nc.Block() as blk:
        @blk.gpsimd
        def _(g):
            g.memset(C["ident_bf"][:, :], 0.0)
            g.memset(C["ident_f"][:, :], 0.0)
            g.memset(C["mask"][:, :], 0.0)
            g.drain()
            g.affine_select(out=C["ident_bf"][:, :], in_=C["ident_bf"][:, :],
                            compare_op=ALU.not_equal, fill=1.0, base=0,
                            pattern=[[-1, 128]], channel_multiplier=1)
            g.affine_select(out=C["ident_f"][:, :], in_=C["ident_f"][:, :],
                            compare_op=ALU.not_equal, fill=1.0, base=0,
                            pattern=[[-1, 128]], channel_multiplier=1)
            g.affine_select(out=C["mask"][:, :], in_=C["mask"][:, :],
                            compare_op=ALU.is_ge, fill=NEG, base=0,
                            pattern=[[-1, 128]], channel_multiplier=1)
            g.memset(C["ones_bf"][:, :], 1.0)
            g.memset(C["eps"][:, :], 1e-5).then_inc(p.s["pool"].h, 1)
    p.s["pool"].n += 1
    emit_fence(p)


def emit_setup_l0a(p, t_in, C):
    nc = p.nc
    _setup_ctx = ExitStack()
    lp = _Sub(p, _setup_ctx)
    stage = lp.sb("c_stage", [48, 128], F32)
    psA = lp.ps("c_psA", [128, 128], F32)

    dS, pe, dve, act = p.s["dS"], p.s["pe"], p.s["dve"], p.s["act"]
    b_dS, b_pe, b_dve, b_act = dS.n, pe.n, dve.n, act.n

    with nc.Block() as blk:
        @blk.gpsimd
        def _(g):
            g.dma_start(stage[0:24, :],
                        t_in["L0_attn_conv_b"].ap().rearrange("(a b) -> a b", b=128)).then_inc(dS.h, 16)
            g.dma_start(stage[24:32, :],
                        t_in["L0_attn_bq"].ap().rearrange("(a c) e -> a (c e)", c=2)).then_inc(dS.h, 16)
            g.dma_start(stage[32:40, :],
                        t_in["L0_attn_bk"].ap().rearrange("(a c) e -> a (c e)", c=2)).then_inc(dS.h, 16)
            g.dma_start(stage[40:48, :],
                        t_in["L0_attn_bo"].ap().rearrange("(a b) -> a b", b=128)).then_inc(dS.h, 16)
            g.dma_start(C["bv_row"][0:1, :], row_ap(t_in["L0_attn_bv"], 1024)).then_inc(dS.h, 16)
            g.dma_start(C["projb_row"][0:1, :], row_ap(t_in["L0_attn_proj_b"], 4096)).then_inc(dS.h, 16)

        @blk.tensor
        def _(t):
            t.wait_ge(dS.h, 16 * (b_dS + 6))
            t.transpose(psA[:, 0:48], stage[0:48, :], C["ident_f"][0:48, 0:48]).then_inc(pe.h, 1)

        @blk.vector
        def _(v):
            v.wait_ge(pe.h, b_pe + 1)
            v.tensor_copy(C["convb_fm"][:, 0:24], psA[:, 0:24])
            v.tensor_copy(C["bq_fm"][:, 0:8], psA[:, 24:32])
            v.tensor_copy(C["bk_fm"][:, 0:8], psA[:, 32:40])
            v.tensor_copy(C["bo_fm"][:, 0:8], psA[:, 40:48]).then_inc(dve.h, 1)

        @blk.scalar
        def _(s):
            s.wait_ge(dve.h, b_dve + 1)
            s.mul(C["bq_fm"][:, 0:8], C["bq_fm"][:, 0:8], 0.125).then_inc(act.h, 1)

    dS.n = b_dS + 6
    pe.n = b_pe + 1
    dve.n = b_dve + 1
    act.n = b_act + 1
    emit_fence(p)
    _setup_ctx.close()



def emit_ln_to_fm(p, C, x_dram, row_starts, out_fm, col0, name):
    nc = p.nc
    NT = len(row_starts)
    NDVE, NACT, NPE = 36, 2, 32
    with ExitStack() as lctx:
        lp = _Sub(p, lctx)
        x_t = [lp.sb(f"{name}_x{i}", [128, 4096], F32) for i in range(2)]
        sums = lp.sb(f"{name}_sums", [128, 2], F32)
        ssq = lp.sb(f"{name}_ssq", [128, 2], F32)
        mean = lp.sb(f"{name}_mean", [128, 2], F32)
        m2 = lp.sb(f"{name}_m2", [128, 2], F32)
        var = lp.sb(f"{name}_var", [128, 2], F32)
        lnv = lp.sb(f"{name}_lnv", [128, 2], F32)
        rstd = lp.sb(f"{name}_rstd", [128, 2], F32)
        sqd = lp.sb(f"{name}_sqd", [128, 4096], BF16)
        ln_tm = [lp.sb(f"{name}_tm{i}", [128, 4096], BF16) for i in range(2)]
        fmst = [lp.sb(f"{name}_fm{i}", [128, 4096], BF16) for i in range(2)]
        psT = [lp.ps(f"{name}_psT{i}", [128, 128], BF16) for i in range(2)]

        dX, dO, pe, act, dve = p.s["dX"], p.s["dOg"], p.s["pe"], p.s["act"], p.s["dve"]
        W = out_fm.shape[1]
        b = dict(dX=dX.n, dO=dO.n, pe=pe.n, act=act.n, dve=dve.n)

        def d_n(it):
            return b["dve"] + it * NDVE

        def a_n(it):
            return b["act"] + it * NACT

        def p_n(it):
            return b["pe"] + it * NPE

        def tp_evac(tj):
            return d_n(tj // 32) + 4 + (tj % 32) + 1

        with nc.Block() as blk:
            @blk.sync
            def _(sy):
                for it, r0 in enumerate(row_starts):
                    if it >= 2:
                        sy.wait_ge(dve.h, d_n(it - 2) + 4)
                    sy.dma_start(x_t[it % 2][:, :],
                                 x_dram[r0:r0 + 128, :]).then_inc(dX.h, 16)

            @blk.vector
            def _(v):
                for it in range(NT):
                    j = it % 2
                    v.wait_ge(dX.h, 16 * (b["dX"] + it + 1))
                    if it >= 2:
                        v.wait_ge(act.h, a_n(it - 2) + 2)
                    v.reduce_sum(sums[:, j:j + 1], x_t[j][:, :],
                                 axis=AX.X).then_inc(dve.h, 1)
                    v.scalar_tensor_tensor(sqd[:, :], x_t[j][:, :], 1.0, x_t[j][:, :],
                                           op0=ALU.mult, op1=ALU.mult,
                                           accum_out=ssq[:, j:j + 1]).then_inc(dve.h, 1)
                    v.wait_ge(act.h, a_n(it) + 1)
                    v.drain()
                    v.scalar_tensor_tensor(var[:, j:j + 1], ssq[:, j:j + 1], 1.0 / 4096,
                                           m2[:, j:j + 1], op0=ALU.mult,
                                           op1=ALU.subtract).then_inc(dve.h, 1)
                    v.wait_ge(act.h, a_n(it) + 2)
                    if it >= 2:
                        v.wait_ge(pe.h, p_n(it - 2) + 32)
                    v.tensor_scalar(ln_tm[j][:, :], x_t[j][:, :], mean[:, j:j + 1],
                                    rstd[:, j:j + 1], op0=ALU.subtract,
                                    op1=ALU.mult).then_inc(dve.h, 1)
                    if it >= 2:
                        v.wait_ge(dO.h, 16 * (b["dO"] + it - 1))
                    for jc in range(32):
                        v.wait_ge(pe.h, p_n(it) + jc + 1)
                        v.tensor_copy(fmst[j][:, jc * 128:(jc + 1) * 128],
                                      psT[(it * 32 + jc) % 2][:, :]).then_inc(dve.h, 1)

            @blk.scalar
            def _(s):
                for it in range(NT):
                    j = it % 2
                    s.wait_ge(dve.h, d_n(it) + 1)
                    if it >= 2:
                        s.wait_ge(dve.h, d_n(it - 2) + 4)
                    s.mul(mean[:, j:j + 1], sums[:, j:j + 1], 1.0 / 4096)
                    s.drain()
                    s.activation(m2[:, j:j + 1], mean[:, j:j + 1],
                                 AF.Square).then_inc(act.h, 1)
                    s.wait_ge(dve.h, d_n(it) + 3)
                    s.activation(lnv[:, j:j + 1], var[:, j:j + 1], AF.Ln,
                                 bias=C["eps"][:, 0:1])
                    s.drain()
                    s.activation(rstd[:, j:j + 1], lnv[:, j:j + 1], AF.Exp,
                                 scale=-0.5).then_inc(act.h, 1)

            @blk.tensor
            def _(t):
                for it in range(NT):
                    j = it % 2
                    t.wait_ge(dve.h, d_n(it) + 4)
                    for jc in range(32):
                        tj = it * 32 + jc
                        if tj >= 2:
                            t.wait_ge(dve.h, tp_evac(tj - 2))
                        t.transpose(psT[tj % 2][:, :],
                                    ln_tm[j][:, jc * 128:(jc + 1) * 128],
                                    C["ident_bf"][:, :]).then_inc(pe.h, 1)

            @blk.gpsimd
            def _(g):
                for it, r0 in enumerate(row_starts):
                    g.wait_ge(dve.h, d_n(it) + 36)
                    dst = bass.AP(out_fm, col0 + r0,
                                  [[W, 128], [128 * W, 32], [1, 128]])
                    g.dma_start(dst, fmst[it % 2][:, :]).then_inc(dO.h, 16)

        dX.n = b["dX"] + NT
        dve.n = b["dve"] + NT * NDVE
        act.n = b["act"] + NT * NACT
        pe.n = b["pe"] + NT * NPE
        dO.n = b["dO"] + NT


def emit_zero_cols(p, out_fm, c0, name):
    nc = p.nc
    W = out_fm.shape[1]
    nz = W - c0
    with ExitStack() as lctx:
        lp = _Sub(p, lctx)
        z = lp.sb(f"{name}_z", [128, 32 * nz], BF16)
        dO = p.s["dOg"]
        b = dO.n
        with nc.Block() as blk:
            @blk.gpsimd
            def _(g):
                g.memset(z[:, :], 0.0)
                g.drain()
                g.dma_start(bass.AP(out_fm, c0, [[W, 128], [128 * W, 32], [1, nz]]),
                            z[:, :]).then_inc(dO.h, 16)
        dO.n = b + 1


class _Sub:

    def __init__(self, p, ctx):
        self.nc = p.nc
        self.ctx = ctx

    def sb(self, name, shape, dt):
        return self.ctx.enter_context(self.nc.sbuf_tensor(name, shape, dt))

    def ps(self, name, shape, dt=F32):
        return self.ctx.enter_context(self.nc.psum_tensor(name, shape, dt))



def emit_conv(p, C, ln0_fm, conv_w, conv_fm, name="cv"):
    nc = p.nc
    LW = ln0_fm.shape[1]
    with ExitStack() as lctx:
        lp = _Sub(p, lctx)
        ln_sb = lp.sb(f"{name}_ln", [128, 32 * LW], BF16)
        w_sb = [lp.sb(f"{name}_w{i}", [128, 1536], BF16) for i in range(2)]
        out_sb = lp.sb(f"{name}_out", [128, 8 * 512], BF16)
        pso = [lp.ps(f"{name}_ps{i}", [128, 1024], F32) for i in range(4)]

        dW, dW2, dX, dO, pe, act = (p.s["dW"], p.s["dW2"], p.s["dX"], p.s["dO"],
                                     p.s["pe"], p.s["act"])
        b = dict(dW=dW.n, dW2=dW2.n, dX=dX.n, dO=dO.n, pe=pe.n, act=act.n)
        NIT = 6 * 32
        pe_done = [(i + 1) * 24 for i in range(NIT)]

        def w_src(ocg, jc):
            off = jc * 128 * 3072 + ocg * 512
            return bass.AP(conv_w, off, [[3072, 128], [4096 * 3072, 3], [1, 512]])

        with nc.Block() as blk:
            @blk.sync
            def _(sy):
                for jc in range(32):
                    sy.dma_start(ln_sb[:, jc * LW:(jc + 1) * LW],
                                 ln0_fm[jc * 128:(jc + 1) * 128, :]).then_inc(dX.h, 16)

            @blk.gpsimd
            def _(g):
                for ocg in range(6):
                    for jc in range(32):
                        i = ocg * 32 + jc
                        if i >= 2:
                            g.wait_ge(pe.h, b["pe"] + pe_done[i - 2])
                        sl = dW if i % 2 == 0 else dW2
                        g.dma_start(w_sb[i % 2][:, :], w_src(ocg, jc)).then_inc(sl.h, 16)

            @blk.tensor
            def _(t):
                t.wait_ge(dX.h, 16 * (b["dX"] + 32))
                for ocg in range(6):
                    for jc in range(32):
                        i = ocg * 32 + jc
                        sl, bn = (dW, b["dW"]) if i % 2 == 0 else (dW2, b["dW2"])
                        t.wait_ge(sl.h, 16 * (bn + i // 2 + 1))
                        if jc == 0 and ocg > 0:
                            t.wait_ge(act.h, b["act"] + ocg * 4)
                        for k in range(3):
                            for oc in range(4):
                                for th in range(2):
                                    t.matmul(
                                        pso[oc][:, th * 512:(th + 1) * 512],
                                        w_sb[i % 2][:, k * 512 + oc * 128:k * 512 + oc * 128 + 128],
                                        ln_sb[:, jc * LW + th * 512 + k:jc * LW + th * 512 + k + 512],
                                        start=(jc == 0 and k == 0),
                                        stop=(jc == 31 and k == 2),
                                    ).then_inc(pe.h, 1)

            @blk.scalar
            def _(s):
                for ocg in range(6):
                    s.wait_ge(pe.h, b["pe"] + (ocg + 1) * 32 * 24)
                    if ocg > 0:
                        s.wait_ge(dO.h, 16 * (b["dO"] + ocg * 8))
                    for oc in range(4):
                        of = ocg * 4 + oc
                        s.activation(
                            out_sb[:, oc * 1024:(oc + 1) * 1024],
                            pso[oc][:, :], AF.Identity,
                            bias=C["convb_fm"][:, of:of + 1],
                        ).then_inc(act.h, 1)

            @blk.sync
            def _(sy):
                for ocg in range(6):
                    for oc in range(4):
                        for th in range(2):
                            of = ocg * 4 + oc
                            sy.wait_ge(act.h, b["act"] + ocg * 4 + oc + 1)
                            sy.dma_start(
                                bass.AP(conv_fm, of * 128 * 1024 + th * 512,
                                        [[1024, 128], [1, 512]]),
                                out_sb[:, (oc * 2 + th) * 512:(oc * 2 + th + 1) * 512],
                            ).then_inc(dO.h, 16)

        dX.n = b["dX"] + 32
        dW.n = b["dW"] + NIT // 2
        dW2.n = b["dW2"] + NIT // 2
        pe.n = b["pe"] + NIT * 24
        act.n = b["act"] + 24
        dO.n = b["dO"] + 48
    emit_fence(p)



def emit_l0_qkv(p, C, conv_fm, wq, wk, wv, qh_fm, kh_fm, vh_tm, name="qkv",
                conv_buf=None):
    nc = p.nc
    with ExitStack() as lctx:
        lp = _Sub(p, lctx)
        conv_sb = conv_buf if conv_buf is not None else lp.sb(f"{name}_conv", [128, 24 * 1024], BF16)
        wq_sb = lp.sb(f"{name}_wq", [128, 8 * 1024], BF16)
        wk_sb = lp.sb(f"{name}_wk", [128, 8 * 1024], BF16)
        wv_sb = lp.sb(f"{name}_wv", [128, 8 * 1024], BF16)
        ost = lp.sb(f"{name}_ost", [128, 8 * 512], BF16)
        pso = [lp.ps(f"{name}_ps{i}", [128, 512], F32) for i in range(4)]

        dW, dX, dO, pe, act, dve = (p.s["dW"], p.s["dX"], p.s["dO"], p.s["pe"],
                                    p.s["act"], p.s["dve"])
        st = [p.s[f"st{i}"] for i in range(4)]
        b = dict(dW=dW.n, dX=dX.n, dO=dO.n, pe=pe.n, act=act.n, dve=dve.n,
                 st=[x.n for x in st])

        with nc.Block() as blk:
            @blk.sync
            def _(sy):
                for c in range(24):
                    sy.dma_start(conv_sb[:, c * 1024:(c + 1) * 1024],
                                 conv_fm[c * 128:(c + 1) * 128, :]).then_inc(dX.h, 16)

            @blk.gpsimd
            def _(g):
                for wi, w in enumerate([wq, wk, wv]):
                    dst = [wq_sb, wk_sb, wv_sb][wi]
                    for jc in range(8):
                        g.dma_start(dst[:, jc * 1024:(jc + 1) * 1024],
                                    bass.AP(w, jc * 128 * 1024, [[1024, 128], [1, 1024]])
                                    ).then_inc(dW.h, 16)

        with nc.Block() as blk:
            @blk.tensor
            def _(t):
                t.wait_ge(dX.h, 16 * (b["dX"] + 24))
                t.wait_ge(dW.h, 16 * (b["dW"] + 24))
                gi = 0
                for wi, wsb in enumerate([wq_sb, wk_sb]):
                    for hc in range(8):
                        if gi >= 2:
                            t.wait_ge(act.h, b["act"] + (gi - 2 + 1) * 2)
                        for th in range(2):
                            for jc in range(8):
                                t.matmul(pso[(gi % 2) * 2 + th][:, :],
                                         wsb[:, jc * 1024 + hc * 128:jc * 1024 + hc * 128 + 128],
                                         conv_sb[:, (wi * 8 + jc) * 1024 + th * 512:(wi * 8 + jc) * 1024 + th * 512 + 512],
                                         start=(jc == 0), stop=(jc == 7)).then_inc(pe.h, 1)
                        gi += 1

            @blk.scalar
            def _(s):
                gi = 0
                for wi in range(2):
                    bias = C["bq_fm"] if wi == 0 else C["bk_fm"]
                    scale = 0.125 if wi == 0 else 1.0
                    for hc in range(8):
                        t0 = b["pe"] + (gi + 1) * 16
                        s.wait_ge(pe.h, t0)
                        if gi >= 4:
                            sl = gi % 4
                            s.wait_ge(st[sl].h, 16 * (b["st"][sl] + 2 * (gi // 4)))
                        for th in range(2):
                            s.activation(ost[:, (gi % 4) * 1024 + th * 512:(gi % 4) * 1024 + (th + 1) * 512],
                                         pso[(gi % 2) * 2 + th][:, :], AF.Identity,
                                         bias=bias[:, hc:hc + 1],
                                         scale=scale).then_inc(act.h, 1)
                        gi += 1

            @blk.sync
            def _(sy):
                gi = 0
                for wi in range(2):
                    out = qh_fm if wi == 0 else kh_fm
                    for hc in range(8):
                        for th in range(2):
                            sy.wait_ge(act.h, b["act"] + gi * 2 + th + 1)
                            sy.dma_start(
                                bass.AP(out, hc * 128 * 1024 + th * 512,
                                        [[1024, 128], [1, 512]]),
                                ost[:, (gi % 4) * 1024 + th * 512:(gi % 4) * 1024 + (th + 1) * 512],
                            ).then_inc(st[gi % 4].h, 16)
                        gi += 1

        b2 = dict(pe=b["pe"] + 256, act=b["act"] + 32, dO=b["dO"],
                  st=[x + 8 for x in b["st"]])

        with nc.Block() as blk:
            @blk.tensor
            def _(t):
                gi = 0
                for tt in range(8):
                    for hh in range(2):
                        if gi >= 2:
                            t.wait_ge(dve.h, b["dve"] + (gi - 2 + 1))
                        ps = pso[gi % 2]
                        t.matmul(ps[:, :], C["ones_bf"][0:1, :],
                                 C["bv_row"][0:1, hh * 512:(hh + 1) * 512],
                                 start=True, stop=False).then_inc(pe.h, 1)
                        for jc in range(8):
                            t.matmul(ps[:, :],
                                     conv_sb[:, (16 + jc) * 1024 + tt * 128:(16 + jc) * 1024 + tt * 128 + 128],
                                     wv_sb[:, jc * 1024 + hh * 512:jc * 1024 + hh * 512 + 512],
                                     start=False, stop=(jc == 7)).then_inc(pe.h, 1)
                        gi += 1

            @blk.vector
            def _(v):
                gi = 0
                for tt in range(8):
                    for hh in range(2):
                        v.wait_ge(pe.h, b2["pe"] + (gi + 1) * 9)
                        if gi < 4:
                            for sl2 in range(4):
                                v.wait_ge(st[sl2].h, 16 * b2["st"][sl2])
                        else:
                            sl = gi % 4
                            v.wait_ge(st[sl].h, 16 * (b2["st"][sl] + gi // 4))
                        v.tensor_copy(ost[:, (gi % 4) * 512:((gi % 4) + 1) * 512],
                                      pso[gi % 2][:, :]).then_inc(dve.h, 1)
                        gi += 1

            @blk.sync
            def _(sy):
                gi = 0
                for tt in range(8):
                    for hh in range(2):
                        sy.wait_ge(dve.h, b["dve"] + gi + 1)
                        sy.dma_start(
                            bass.AP(vh_tm, tt * 128 * 1024 + hh * 512,
                                    [[1024, 128], [1, 512]]),
                            ost[:, (gi % 4) * 512:((gi % 4) + 1) * 512],
                        ).then_inc(st[gi % 4].h, 16)
                        gi += 1

        dX.n = b["dX"] + 24
        dW.n = b["dW"] + 24
        pe.n = b2["pe"] + 16 * 9
        act.n = b2["act"]
        dve.n = b["dve"] + 16
        for i in range(4):
            st[i].n = b2["st"][i] + 4
    emit_fence(p)



def emit_l0_battn(p, C, qh_fm, kh_fm, vh_tm, o_fm, name="ba"):
    nc = p.nc
    with ExitStack() as lctx:
        lp = _Sub(p, lctx)
        qh_sb = lp.sb(f"{name}_qh", [128, 8 * 1024], BF16)
        kh_sb = lp.sb(f"{name}_kh", [128, 8 * 1024], BF16)
        vh_sb = lp.sb(f"{name}_vh", [128, 8 * 1024], BF16)
        o_sb = lp.sb(f"{name}_o", [128, 8 * 1024], BF16)
        s_sb = lp.sb(f"{name}_s", [128, 2 * 128], F32)
        p_bf = lp.sb(f"{name}_p", [128, 2 * 128], BF16)
        pT_sb = lp.sb(f"{name}_pT", [128, 2 * 128], BF16)
        o_tm = lp.sb(f"{name}_otm", [128, 2 * 64], BF16)
        mx = lp.sb(f"{name}_mx", [128, 2], F32)
        negmx = lp.sb(f"{name}_negmx", [128, 2], F32)
        lsum = lp.sb(f"{name}_l", [128, 2], F32)
        rl = lp.sb(f"{name}_rl", [128, 2], F32)
        ps_s = [lp.ps(f"{name}_ps_s{i}", [128, 128], F32) for i in range(2)]
        ps_pT = [lp.ps(f"{name}_ps_pT{i}", [128, 128], BF16) for i in range(2)]
        ps_o = [lp.ps(f"{name}_ps_o{i}", [128, 64], F32) for i in range(2)]
        ps_oT = [lp.ps(f"{name}_ps_oT{i}", [128, 128], BF16) for i in range(2)]

        dX, dO, pe, act, dve = (p.s["dX"], p.s["dO"], p.s["pe"], p.s["act"],
                                p.s["dve"])
        b = dict(dX=dX.n, dO=dO.n, pe=pe.n, act=act.n, dve=dve.n)

        with nc.Block() as blk:
            @blk.sync
            def _(sy):
                for src, dst in [(qh_fm, qh_sb), (kh_fm, kh_sb), (vh_tm, vh_sb)]:
                    for c in range(8):
                        sy.dma_start(dst[:, c * 1024:(c + 1) * 1024],
                                     src[c * 128:(c + 1) * 128, :]).then_inc(dX.h, 16)

        N = 128
        score_pe, tp_pe, pv_pe, to_pe = {}, {}, {}, {}
        cnt = b["pe"]
        for k in range(N + 4):
            if k < N:
                cnt += 1; score_pe[k] = cnt
            if 0 <= k - 2 < N:
                cnt += 1; tp_pe[k - 2] = cnt
            if 0 <= k - 3 < N:
                cnt += 1; pv_pe[k - 3] = cnt
            if 0 <= k - 4 < N:
                cnt += 1; to_pe[k - 4] = cnt
        add_d, max_d, ptc_d, rec_d, scl_d, oc_d = {}, {}, {}, {}, {}, {}
        cnt = b["dve"]
        for k in range(N + 4):
            if k < N:
                add_d[k] = cnt + 1; max_d[k] = cnt + 2; cnt += 2
            if 0 <= k - 2 < N:
                cnt += 1; ptc_d[k - 2] = cnt
            if 0 <= k - 3 < N:
                rec_d[k - 3] = cnt + 1; scl_d[k - 3] = cnt + 2; cnt += 2
            if 0 <= k - 4 < N:
                cnt += 1; oc_d[k - 4] = cnt
        ngm_a, exp_a = {}, {}
        cnt = b["act"]
        for k in range(N + 4):
            if 0 <= k - 1 < N:
                ngm_a[k - 1] = cnt + 1; exp_a[k - 1] = cnt + 2; cnt += 2

        def loc(it):
            bb, h = it // 16, it % 16
            return bb, h, h // 2, (h % 2) * 64

        with nc.Block() as blk:
            @blk.tensor
            def _(t):
                t.wait_ge(dX.h, 16 * (b["dX"] + 24))
                for k in range(N + 4):
                    if k < N:
                        it = k
                        bb, h, hc, hp = loc(it)
                        j = it % 2
                        if it >= 2:
                            t.wait_ge(dve.h, add_d[it - 2])
                        t.matmul(ps_s[j][:, :],
                                 qh_sb[hp:hp + 64, hc * 1024 + bb * 128:hc * 1024 + bb * 128 + 128],
                                 kh_sb[hp:hp + 64, hc * 1024 + bb * 128:hc * 1024 + bb * 128 + 128],
                                 start=True, stop=True).then_inc(pe.h, 1)
                    if 0 <= k - 2 < N:
                        it = k - 2
                        j = it % 2
                        t.wait_ge(act.h, exp_a[it])
                        if it >= 2:
                            t.wait_ge(dve.h, ptc_d[it - 2])
                        t.transpose(ps_pT[j][:, :], p_bf[:, j * 128:(j + 1) * 128],
                                    C["ident_bf"][:, :]).then_inc(pe.h, 1)
                    if 0 <= k - 3 < N:
                        it = k - 3
                        bb, h, hc, hp = loc(it)
                        j = it % 2
                        t.wait_ge(dve.h, ptc_d[it])
                        if it >= 2:
                            t.wait_ge(dve.h, scl_d[it - 2])
                        t.matmul(ps_o[j][:, :], pT_sb[:, j * 128:(j + 1) * 128],
                                 vh_sb[:, bb * 1024 + h * 64:bb * 1024 + h * 64 + 64],
                                 start=True, stop=True).then_inc(pe.h, 1)
                    if 0 <= k - 4 < N:
                        it = k - 4
                        j = it % 2
                        t.wait_ge(dve.h, scl_d[it])
                        if it >= 2:
                            t.wait_ge(dve.h, oc_d[it - 2])
                        t.transpose(ps_oT[j][0:64, :], o_tm[:, j * 64:j * 64 + 64],
                                    C["ident_bf"][:, :]).then_inc(pe.h, 1)

            @blk.vector
            def _(v):
                for k in range(N + 4):
                    if k < N:
                        it = k
                        j = it % 2
                        v.wait_ge(pe.h, score_pe[it])
                        if it >= 2:
                            v.wait_ge(act.h, exp_a[it - 2])
                        v.tensor_add(s_sb[:, j * 128:(j + 1) * 128], ps_s[j][:, :],
                                     C["mask"][:, :]).then_inc(dve.h, 1)
                        v.drain()
                        v.reduce_max(mx[:, j:j + 1], s_sb[:, j * 128:(j + 1) * 128],
                                     axis=AX.X).then_inc(dve.h, 1)
                    if 0 <= k - 2 < N:
                        it = k - 2
                        j = it % 2
                        v.wait_ge(pe.h, tp_pe[it])
                        if it >= 2:
                            v.wait_ge(pe.h, pv_pe[it - 2])
                        v.tensor_copy(pT_sb[:, j * 128:(j + 1) * 128],
                                      ps_pT[j][:, :]).then_inc(dve.h, 1)
                    if 0 <= k - 3 < N:
                        it = k - 3
                        j = it % 2
                        v.wait_ge(act.h, exp_a[it])
                        v.reciprocal(rl[:, j:j + 1], lsum[:, j:j + 1]).then_inc(dve.h, 1)
                        v.drain()
                        v.wait_ge(pe.h, pv_pe[it])
                        if it >= 2:
                            v.wait_ge(pe.h, to_pe[it - 2])
                        v.tensor_scalar(o_tm[:, j * 64:j * 64 + 64], ps_o[j][:, 0:64],
                                        rl[:, j:j + 1], None,
                                        op0=ALU.mult).then_inc(dve.h, 1)
                    if 0 <= k - 4 < N:
                        it = k - 4
                        bb, h, hc, hp = loc(it)
                        j = it % 2
                        v.wait_ge(pe.h, to_pe[it])
                        v.tensor_copy(o_sb[hp:hp + 64, hc * 1024 + bb * 128:hc * 1024 + bb * 128 + 128],
                                      ps_oT[j][0:64, :]).then_inc(dve.h, 1)

            @blk.scalar
            def _(s):
                for k in range(N + 4):
                    if 0 <= k - 1 < N:
                        it = k - 1
                        j = it % 2
                        s.wait_ge(dve.h, max_d[it])
                        if it >= 2:
                            s.wait_ge(dve.h, rec_d[it - 2])
                            s.wait_ge(pe.h, tp_pe[it - 2])
                        s.mul(negmx[:, j:j + 1], mx[:, j:j + 1], -1.0).then_inc(act.h, 1)
                        s.drain()
                        s.activation(p_bf[:, j * 128:(j + 1) * 128],
                                     s_sb[:, j * 128:(j + 1) * 128], AF.Exp,
                                     bias=negmx[:, j:j + 1],
                                     accum_out=lsum[:, j:j + 1]).then_inc(act.h, 1)

        b2 = dict(pe=b["pe"] + 128 * 4, act=b["act"] + 128 * 2,
                  dve=b["dve"] + 128 * 6)

        with nc.Block() as blk:
            @blk.sync
            def _(sy):
                sy.wait_ge(dve.h, b2["dve"])
                for c in range(8):
                    sy.dma_start(
                        bass.AP(o_fm, c * 128 * 1024, [[1024, 128], [1, 1024]]),
                        o_sb[:, c * 1024:(c + 1) * 1024]).then_inc(dO.h, 16)

        dX.n = b["dX"] + 24
        pe.n = b2["pe"]
        act.n = b2["act"]
        dve.n = b2["dve"]
        dO.n = b["dO"] + 8
    emit_fence(p)



def emit_l0_out(p, C, o_fm, wo, proj_w, x0s, x1, name="l0o", pw_buf=None):
    nc = p.nc
    with ExitStack() as lctx:
        lp = _Sub(p, lctx)
        o_sb = lp.sb(f"{name}_o", [128, 8 * 1024], BF16)
        wo_sb = lp.sb(f"{name}_wo", [128, 8 * 1024], BF16)
        ap_sb = lp.sb(f"{name}_ap", [128, 8 * 1024], BF16)
        pw_sb = pw_buf if pw_buf is not None else lp.sb(f"{name}_pw", [128, 8 * 4096], BF16)
        x0_t = lp.sb(f"{name}_x0", [128, 4096], F32)
        x1_sb = lp.sb(f"{name}_x1", [128, 4096], F32)
        pso = [lp.ps(f"{name}_ps{i}", [128, 512], F32) for i in range(4)]

        dW, dW2, dX, dO, pe, act, dve = (p.s["dW"], p.s["dW2"], p.s["dX"],
                                         p.s["dO"], p.s["pe"], p.s["act"],
                                         p.s["dve"])
        b = dict(dW=dW.n, dW2=dW2.n, dX=dX.n, dO=dO.n, pe=pe.n, act=act.n,
                 dve=dve.n)

        with nc.Block() as blk:
            @blk.sync
            def _(sy):
                for c in range(8):
                    sy.dma_start(o_sb[:, c * 1024:(c + 1) * 1024],
                                 o_fm[c * 128:(c + 1) * 128, :]).then_inc(dX.h, 16)

            @blk.gpsimd
            def _(g):
                for c in range(8):
                    g.dma_start(wo_sb[:, c * 1024:(c + 1) * 1024],
                                bass.AP(wo, c * 128 * 1024, [[1024, 128], [1, 1024]])
                                ).then_inc(dW.h, 16)
                for c in range(8):
                    g.dma_start(pw_sb[:, c * 4096:(c + 1) * 4096],
                                bass.AP(proj_w, c * 128 * 4096, [[4096, 128], [1, 4096]])
                                ).then_inc(dW2.h, 16)

        with nc.Block() as blk:
            @blk.tensor
            def _(t):
                t.wait_ge(dX.h, 16 * (b["dX"] + 8))
                t.wait_ge(dW.h, 16 * (b["dW"] + 8))
                for oc in range(8):
                    if oc >= 2:
                        t.wait_ge(act.h, b["act"] + (oc - 2 + 1) * 2)
                    for th in range(2):
                        for hc in range(8):
                            t.matmul(pso[(oc % 2) * 2 + th][:, :],
                                     wo_sb[:, hc * 1024 + oc * 128:hc * 1024 + oc * 128 + 128],
                                     o_sb[:, hc * 1024 + th * 512:hc * 1024 + th * 512 + 512],
                                     start=(hc == 0), stop=(hc == 7)).then_inc(pe.h, 1)

            @blk.scalar
            def _(s):
                for oc in range(8):
                    s.wait_ge(pe.h, b["pe"] + (oc + 1) * 16)
                    for th in range(2):
                        s.activation(ap_sb[:, oc * 1024 + th * 512:oc * 1024 + th * 512 + 512],
                                     pso[(oc % 2) * 2 + th][:, :], AF.Identity,
                                     bias=C["bo_fm"][:, oc:oc + 1]).then_inc(act.h, 1)

        b2 = dict(pe=b["pe"] + 128, act=b["act"] + 16)

        b_dW3 = dW.n + 8
        with nc.Block() as blk:
            @blk.gpsimd
            def _(g):
                for tt in range(8):
                    if tt > 0:
                        g.wait_ge(dve.h, b["dve"] + tt * 8)
                    g.dma_start(x0_t[:, :], x0s[1 + tt * 128:1 + tt * 128 + 128, :]).then_inc(dW.h, 16)

            @blk.tensor
            def _(t):
                t.wait_ge(act.h, b2["act"])
                t.wait_ge(dW2.h, 16 * (b["dW2"] + 8))
                for tt in range(8):
                    for og in range(8):
                        gi = tt * 8 + og
                        if gi >= 2:
                            t.wait_ge(dve.h, b["dve"] + gi - 1)
                        ps = pso[gi % 2]
                        t.matmul(ps[:, :], C["ones_bf"][0:1, :],
                                 C["projb_row"][0:1, og * 512:(og + 1) * 512],
                                 start=True, stop=False).then_inc(pe.h, 1)
                        for fc in range(8):
                            t.matmul(ps[:, :],
                                     ap_sb[:, fc * 1024 + tt * 128:fc * 1024 + tt * 128 + 128],
                                     pw_sb[:, fc * 4096 + og * 512:fc * 4096 + og * 512 + 512],
                                     start=False, stop=(fc == 7)).then_inc(pe.h, 1)

            @blk.vector
            def _(v):
                for tt in range(8):
                    v.wait_ge(dW.h, 16 * (b_dW3 + tt + 1))
                    if tt > 0:
                        v.wait_ge(dO.h, 16 * (b["dO"] + tt))
                    for og in range(8):
                        gi = tt * 8 + og
                        v.wait_ge(pe.h, b2["pe"] + (gi + 1) * 9)
                        v.tensor_add(x1_sb[:, og * 512:(og + 1) * 512],
                                     pso[gi % 2][:, :],
                                     x0_t[:, og * 512:(og + 1) * 512]).then_inc(dve.h, 1)

            @blk.sync
            def _(sy):
                for tt in range(8):
                    sy.wait_ge(dve.h, b["dve"] + (tt + 1) * 8)
                    sy.dma_start(x1[tt * 128:(tt + 1) * 128, :],
                                 x1_sb[:, :]).then_inc(dO.h, 16)

        dX.n = b["dX"] + 8
        dW.n = b["dW"] + 16
        dW2.n = b["dW2"] + 8
        pe.n = b2["pe"] + 8 * 8 * 9
        act.n = b2["act"]
        dve.n = b["dve"] + 64
        dO.n = b["dO"] + 8
    emit_fence(p)




def emit_l0_ffn(p, C, t_in, x1, ln1_fm, x2, cc_in, cc_out, dbg_w=None,
                rg=None, name="ff", wTb=None, wpref_sb=None, ln_buf=None):
    nc = p.nc
    ALPHA = 1.0 / 16.0
    CAP = 3072.0
    with ExitStack() as lctx:
        lp = _Sub(p, lctx)
        ln_sb = ln_buf if ln_buf is not None else lp.sb(f"{name}_ln", [128, 32 * 1024], BF16)
        gw1n = lp.sb(f"{name}_gw1n", [32, 4096], BF16)
        gw1T = lp.sb(f"{name}_gw1T", [128, 1024], BF16)
        gw2n = lp.sb(f"{name}_gw2n", [8, 32], F32)
        gw2T = lp.sb(f"{name}_gw2T", [32, 8], F32)
        gb1_row = lp.sb(f"{name}_gb1r", [1, 32], BF16)
        gb2_row = lp.sb(f"{name}_gb2r", [1, 8], F32)
        ones_f = lp.sb(f"{name}_onesf", [1, 128], F32)
        ones_col = lp.sb(f"{name}_onescol", [128, 1], F32)
        grow = lp.sb(f"{name}_grow", [1, 64], F32)
        gam_bc = lp.sb(f"{name}_gam", [128, 32], F32)
        bet_bc = lp.sb(f"{name}_bet", [128, 32], F32)
        h_t = lp.sb(f"{name}_h", [128, 32], F32)
        sqd = lp.sb(f"{name}_sqd", [128, 32], F32)
        sums = lp.sb(f"{name}_sums", [128, 1], F32)
        ssq = lp.sb(f"{name}_ssq", [128, 1], F32)
        mean = lp.sb(f"{name}_mean", [128, 1], F32)
        msq = lp.sb(f"{name}_msq", [128, 1], F32)
        var = lp.sb(f"{name}_var", [128, 1], F32)
        lnv = lp.sb(f"{name}_lnv", [128, 1], F32)
        rstd = lp.sb(f"{name}_rstd", [128, 1], F32)
        hn = lp.sb(f"{name}_hn", [128, 32], F32)
        t1g = lp.sb(f"{name}_t1g", [128, 32], F32)
        h2 = lp.sb(f"{name}_h2", [128, 32], F32)
        hr = lp.sb(f"{name}_hr", [128, 32], F32)
        hT = lp.sb(f"{name}_hT", [32, 128], F32)
        gates = lp.sb(f"{name}_gates", [128, 8], F32)
        gm1 = lp.sb(f"{name}_gm1", [128, 1], F32)
        neggm1 = lp.sb(f"{name}_ngm1", [128, 1], F32)
        gmasked = lp.sb(f"{name}_gmsk", [128, 8], F32)
        gm2 = lp.sb(f"{name}_gm2", [128, 1], F32)
        gdexp = lp.sb(f"{name}_gdx", [128, 1], F32)
        gden = lp.sb(f"{name}_gden", [128, 1], F32)
        grden = lp.sb(f"{name}_grd", [128, 1], F32)
        eqb_all = lp.sb(f"{name}_eqb", [128, 8 * 16], F32)
        g1_all = lp.sb(f"{name}_g1", [128, 8], F32)
        g2_all = lp.sb(f"{name}_g2", [128, 8], F32)
        cnt_sb = lp.sb(f"{name}_cnt", [16, 1], F32)
        cnt_k = lp.sb(f"{name}_cntk", [1, 8], F32)
        cnt_k1 = lp.sb(f"{name}_cntk1", [1, 8], F32)
        alw = lp.sb(f"{name}_alw", [1, 16], F32)
        a1_bc = lp.sb(f"{name}_a1bc", [128, 8], F32)
        a2_bc = lp.sb(f"{name}_a2bc", [128, 8], F32)
        u1 = lp.sb(f"{name}_u1", [128, 8], F32)
        u2 = lp.sb(f"{name}_u2", [128, 8], F32)
        u1b = lp.sb(f"{name}_u1b", [128, 8], F32)
        u2b = lp.sb(f"{name}_u2b", [128, 8], F32)
        w_all = lp.sb(f"{name}_w", [128, 64], F32)
        ps_ctxF = ExitStack()
        lpf = _Sub(p, ps_ctxF)
        ps_h = lpf.ps(f"{name}_psh", [128, 512], F32)
        ps_hT = lpf.ps(f"{name}_psht", [128, 128], F32)
        ps_g = lpf.ps(f"{name}_psg", [128, 8], F32)
        ps_c = lpf.ps(f"{name}_psc", [16, 1], F32)
        ps_gw = lpf.ps(f"{name}_psgw", [128, 128], BF16)

        dW, dW2, dX, dXs, dO, pe, act, dve, pool, cc = (
            p.s["dW"], p.s["dW2"], p.s["dX"], p.s["dXs"], p.s["dO"], p.s["pe"],
            p.s["act"], p.s["dve"], p.s["pool"], p.s["cc"])
        st = [p.s[f"st{i}"] for i in range(4)]
        b = dict(dW=dW.n, dX=dX.n, pe=pe.n, dve=dve.n)

        with nc.Block() as blk:
            @blk.sync
            def _(sy):
                for c in range(32):
                    sy.dma_start(ln_sb[:, c * 1024:(c + 1) * 1024],
                                 ln1_fm[c * 128:(c + 1) * 128, :]).then_inc(dX.h, 16)

            @blk.gpsimd
            def _(g):
                g.memset(ones_f[:, :], 1.0)
                g.memset(ones_col[:, :], 1.0)
                g.dma_start(gw1n[:, :], t_in["L0_ffn_gw1"].ap()).then_inc(dW.h, 16)
                g.dma_start(gw2n[:, :], t_in["L0_ffn_gw2"].ap()).then_inc(dW.h, 16)
                g.dma_start(gb1_row[0:1, :], row_ap(t_in["L0_ffn_gb1"], 32)).then_inc(dW.h, 16)
                g.dma_start(gb2_row[0:1, :], row_ap(t_in["L0_ffn_gb2"], 8)).then_inc(dW.h, 16)
                g.dma_start(grow[0:1, 0:32], row_ap(t_in["L0_ffn_ln_gamma"], 32)).then_inc(dW.h, 16)
                g.dma_start(grow[0:1, 32:64], row_ap(t_in["L0_ffn_ln_beta"], 32)).then_inc(dW.h, 16)

            @blk.tensor
            def _(t):
                t.wait_ge(dW.h, 16 * (b["dW"] + 6))
                for jc in range(32):
                    if jc >= 4:
                        t.wait_ge(dve.h, b["dve"] + (jc - 4) + 1)
                    t.transpose(ps_gw[:, (jc % 4) * 32:(jc % 4) * 32 + 32],
                                gw1n[:, jc * 128:(jc + 1) * 128],
                                C["ident_bf"][0:32, 0:32]).then_inc(pe.h, 1)
                t.transpose(ps_g[0:32, 0:8], gw2n[:, :],
                            C["ident_f"][0:8, 0:8]).then_inc(pe.h, 1)
                t.matmul(ps_h[:, 0:64], ones_f[0:1, :], grow[0:1, :],
                         start=True, stop=True).then_inc(pe.h, 1)

            @blk.vector
            def _(v):
                for jc in range(32):
                    v.wait_ge(pe.h, b["pe"] + jc + 1)
                    v.tensor_copy(gw1T[:, jc * 32:(jc + 1) * 32],
                                  ps_gw[:, (jc % 4) * 32:(jc % 4) * 32 + 32]).then_inc(dve.h, 1)
                v.wait_ge(pe.h, b["pe"] + 33)
                v.tensor_copy(gw2T[0:32, :], ps_g[0:32, 0:8]).then_inc(dve.h, 1)
                v.wait_ge(pe.h, b["pe"] + 34)
                v.tensor_copy(gam_bc[:, :], ps_h[:, 0:32])
                v.tensor_copy(bet_bc[:, :], ps_h[:, 32:64]).then_inc(dve.h, 1)

        dX.n = b["dX"] + 32
        dW.n = b["dW"] + 6
        pe.n = b["pe"] + 34
        dve.n = b["dve"] + 34
        emit_fence(p)

        NPE, NDVE, NACT = 37, 19, 4
        b2 = dict(pe=pe.n, act=act.n, dve=dve.n)

        def d_n(it):
            return b2["dve"] + it * NDVE

        def a_n(it):
            return b2["act"] + it * NACT

        def p_n(it):
            return b2["pe"] + it * NPE

        with nc.Block() as blk:
            @blk.tensor
            def _(t):
                for it in range(8):
                    if it > 0:
                        t.wait_ge(dve.h, d_n(it))
                    t.matmul(ps_h[:, 0:32], C["ones_bf"][0:1, :],
                             gb1_row[0:1, :], start=True, stop=False).then_inc(pe.h, 1)
                    for jc in range(32):
                        t.matmul(ps_h[:, 0:32],
                                 ln_sb[:, jc * 1024 + it * 128:jc * 1024 + it * 128 + 128],
                                 gw1T[:, jc * 32:(jc + 1) * 32],
                                 start=False, stop=(jc == 31)).then_inc(pe.h, 1)
                    t.wait_ge(dve.h, d_n(it) + 8)
                    t.transpose(ps_hT[0:32, 0:128], hr[:, 0:32],
                                C["ident_f"][:, :]).then_inc(pe.h, 1)
                    t.wait_ge(dve.h, d_n(it) + 9)
                    t.matmul(ps_g[:, 0:8], ones_f[0:1, :], gb2_row[0:1, :],
                             start=True, stop=False).then_inc(pe.h, 1)
                    t.matmul(ps_g[:, 0:8], hT[0:32, :], gw2T[0:32, :],
                             start=False, stop=True).then_inc(pe.h, 1)
                    t.wait_ge(dve.h, d_n(it) + 15)
                    t.matmul(ps_c[0:16, 0:1], eqb_all[:, it * 16:(it + 1) * 16],
                             ones_col[:, 0:1], start=(it == 0), stop=(it == 7)).then_inc(pe.h, 1)

            @blk.vector
            def _(v):
                for it in range(8):
                    v.wait_ge(pe.h, p_n(it) + 33)
                    v.tensor_copy(h_t[:, :], ps_h[:, 0:32]).then_inc(dve.h, 1)
                    v.drain()
                    v.reduce_sum(sums[:, :], h_t[:, :], axis=AX.X).then_inc(dve.h, 1)
                    v.scalar_tensor_tensor(sqd[:, :], h_t[:, :], 1.0, h_t[:, :],
                                           op0=ALU.mult, op1=ALU.mult,
                                           accum_out=ssq[:, 0:1]).then_inc(dve.h, 1)
                    v.wait_ge(act.h, a_n(it) + 1)
                    v.drain()
                    v.scalar_tensor_tensor(var[:, :], ssq[:, :], 1.0 / 32, msq[:, :],
                                           op0=ALU.mult, op1=ALU.subtract).then_inc(dve.h, 1)
                    v.wait_ge(act.h, a_n(it) + 2)
                    v.tensor_scalar(hn[:, :], h_t[:, :], mean[:, 0:1], rstd[:, 0:1],
                                    op0=ALU.subtract, op1=ALU.mult).then_inc(dve.h, 1)
                    v.drain()
                    v.tensor_mul(t1g[:, :], hn[:, :], gam_bc[:, :]).then_inc(dve.h, 1)
                    v.drain()
                    v.tensor_add(h2[:, :], t1g[:, :], bet_bc[:, :]).then_inc(dve.h, 1)
                    v.drain()
                    v.tensor_relu(hr[:, :], h2[:, :]).then_inc(dve.h, 1)
                    v.wait_ge(pe.h, p_n(it) + 34)
                    v.tensor_copy(hT[0:32, :], ps_hT[0:32, 0:128]).then_inc(dve.h, 1)
                    v.wait_ge(pe.h, p_n(it) + 36)
                    v.tensor_copy(gates[:, :], ps_g[:, 0:8]).then_inc(dve.h, 1)
                    v.drain()
                    v.reduce_max(gm1[:, :], gates[:, :], axis=AX.X).then_inc(dve.h, 1)
                    v.drain()
                    v.tensor_scalar(eqb_all[:, it * 16:it * 16 + 8], gates[:, :],
                                    gm1[:, 0:1], None, op0=ALU.is_ge).then_inc(dve.h, 1)
                    v.drain()
                    v.scalar_tensor_tensor(gmasked[:, :], eqb_all[:, it * 16:it * 16 + 8],
                                           -1.0e30, gates[:, :], op0=ALU.mult,
                                           op1=ALU.add).then_inc(dve.h, 1)
                    v.drain()
                    v.reduce_max(gm2[:, :], gmasked[:, :], axis=AX.X).then_inc(dve.h, 1)
                    v.drain()
                    v.tensor_scalar(eqb_all[:, it * 16 + 8:it * 16 + 16], gmasked[:, :],
                                    gm2[:, 0:1], None, op0=ALU.is_ge).then_inc(dve.h, 1)
                    v.wait_ge(act.h, a_n(it) + 4)
                    v.tensor_scalar(gden[:, :], gdexp[:, :], 1.0, None,
                                    op0=ALU.add).then_inc(dve.h, 1)
                    v.drain()
                    v.reciprocal(grden[:, :], gden[:, :]).then_inc(dve.h, 1)
                    v.drain()
                    v.tensor_scalar(g1_all[:, it:it + 1], grden[:, :], ALPHA, None,
                                    op0=ALU.mult).then_inc(dve.h, 1)
                    v.scalar_tensor_tensor(g2_all[:, it:it + 1], gdexp[:, :], ALPHA,
                                           grden[:, :], op0=ALU.mult,
                                           op1=ALU.mult).then_inc(dve.h, 1)

            @blk.scalar
            def _(s):
                for it in range(8):
                    s.wait_ge(dve.h, d_n(it) + 2)
                    s.mul(mean[:, :], sums[:, :], 1.0 / 32)
                    s.drain()
                    s.activation(msq[:, :], mean[:, :], AF.Square).then_inc(act.h, 1)
                    s.wait_ge(dve.h, d_n(it) + 4)
                    s.activation(lnv[:, :], var[:, :], AF.Ln, bias=C["eps"][:, 0:1])
                    s.drain()
                    s.activation(rstd[:, :], lnv[:, :], AF.Exp, scale=-0.5).then_inc(act.h, 1)
                    s.wait_ge(dve.h, d_n(it) + 11)
                    s.mul(neggm1[:, :], gm1[:, :], -1.0).then_inc(act.h, 1)
                    s.drain()
                    s.wait_ge(dve.h, d_n(it) + 14)
                    s.activation(gdexp[:, :], gm2[:, :], AF.Exp,
                                 bias=neggm1[:, 0:1]).then_inc(act.h, 1)

        pe.n = b2["pe"] + 8 * NPE
        dve.n = b2["dve"] + 8 * NDVE
        act.n = b2["act"] + 8 * NACT
        emit_fence(p)

        b3 = dict(pe=pe.n, dve=dve.n, cc=cc.n, dW=dW.n)
        with nc.Block() as blk:
            @blk.vector
            def _(v):
                v.tensor_copy(cnt_sb[:, :], ps_c[0:16, 0:1]).then_inc(dve.h, 1)

            @blk.gpsimd
            def _(g):
                g.wait_ge(dve.h, b3["dve"] + 1)
                g.dma_start(cc_in[:, :], cnt_sb[:, :]).then_inc(dW.h, 16)
                g.wait_ge(dW.h, 16 * (b3["dW"] + 1))
                g.collective_compute(
                    "AllReduce", ALU.add,
                    replica_groups=rg if rg is not None else [list(range(NCORES))],
                    ins=[cc_in.ap()], outs=[cc_out.ap()]).then_inc(cc.h, 1)
                g.wait_ge(cc.h, b3["cc"] + 1)
                g.dma_start(cnt_k[0:1, 0:8],
                            bass.AP(cc_out, 0, [[1, 1], [1, 8]])).then_inc(dW.h, 16)
                g.dma_start(cnt_k1[0:1, 0:8],
                            bass.AP(cc_out, 8, [[1, 1], [1, 8]])).then_inc(dW.h, 16)

            @blk.vector
            def _(v):
                v.wait_ge(dW.h, 16 * (b3["dW"] + 3))
                v.tensor_scalar(alw[0:1, 0:8], cnt_k[0:1, 0:8], CAP, None,
                                op0=ALU.is_le).then_inc(dve.h, 1)
                v.tensor_scalar(alw[0:1, 8:16], cnt_k1[0:1, 0:8], CAP, None,
                                op0=ALU.is_le).then_inc(dve.h, 1)

            @blk.tensor
            def _(t):
                t.wait_ge(dve.h, b3["dve"] + 3)
                t.matmul(ps_hT[:, 0:8], ones_f[0:1, :], alw[0:1, 0:8],
                         start=True, stop=True).then_inc(pe.h, 1)
                t.matmul(ps_g[:, 0:8], ones_f[0:1, :], alw[0:1, 8:16],
                         start=True, stop=True).then_inc(pe.h, 1)

            @blk.vector
            def _(v):
                v.wait_ge(pe.h, b3["pe"] + 2)
                v.tensor_copy(a1_bc[:, :], ps_hT[:, 0:8])
                v.tensor_copy(a2_bc[:, :], ps_g[:, 0:8]).then_inc(dve.h, 1)
                v.drain()
                for it in range(8):
                    v.tensor_scalar(u1[:, :], eqb_all[:, it * 16:it * 16 + 8],
                                    g1_all[:, it:it + 1], None, op0=ALU.mult)
                    v.tensor_scalar(u2[:, :], eqb_all[:, it * 16 + 8:it * 16 + 16],
                                    g2_all[:, it:it + 1], None, op0=ALU.mult)
                    v.drain()
                    v.tensor_mul(u1b[:, :], u1[:, :], a1_bc[:, :])
                    v.tensor_mul(u2b[:, :], u2[:, :], a2_bc[:, :])
                    v.drain()
                    v.tensor_add(w_all[:, it * 8:(it + 1) * 8], u1b[:, :],
                                 u2b[:, :]).then_inc(dve.h, 1)
                    v.drain()

        pe.n = b3["pe"] + 2
        dve.n = b3["dve"] + 3 + 1 + 8
        dW.n = b3["dW"] + 3
        cc.n = b3["cc"] + 1
        if dbg_w is not None:
            bw = p.s["dO"].n
            with nc.Block() as blk:
                @blk.sync
                def _(sy):
                    sy.wait_ge(dve.h, dve.n)
                    sy.dma_start(dbg_w[:, :], w_all[:, :]).then_inc(p.s["dO"].h, 16)
            p.s["dO"].n = bw + 1
        emit_fence(p)

        ps_ctxF.close()

        A_all = lp.sb(f"{name}_Aall", [128, 32 * 128], BF16)
        B_all = lp.sb(f"{name}_Ball", [128, 4096], BF16)
        tmpw = lp.sb(f"{name}_tmpw", [128, 128], BF16)
        loraT = lp.sb(f"{name}_loraT", [128, 8 * 128], BF16)
        ps_ctxH = ExitStack()
        lph = _Sub(p, ps_ctxH)
        ps_l = lph.ps(f"{name}_psl", [128, 128], F32)
        ps_lT = lph.ps(f"{name}_pslT", [128, 128], BF16)

        b4 = dict(pe=pe.n, dve=dve.n, dW=dW.n)
        with nc.Block() as blk:
            @blk.gpsimd
            def _(g):
                for e in range(8):
                    src = bass.AP(t_in["L0_ffn_lora_A"], e * 65536,
                                  [[16, 128], [2048, 32], [1, 16]])
                    dst = bass.AP(A_all, e * 16, [[4096, 128], [128, 32], [1, 16]])
                    g.dma_start(dst, src).then_inc(dW.h, 16)
                g.dma_start(B_all[:, :],
                            bass.AP(t_in["L0_ffn_lora_B"], 0, [[4096, 128], [1, 4096]])
                            ).then_inc(dW.h, 16)

            @blk.tensor
            def _(t):
                t.wait_ge(dW.h, 16 * (b4["dW"] + 9))
                for it in range(8):
                    if it > 0:
                        t.wait_ge(dve.h, b4["dve"] + (it - 1) * 9 + 8)
                    for jc in range(32):
                        t.matmul(ps_l[:, :],
                                 ln_sb[:, jc * 1024 + it * 128:jc * 1024 + it * 128 + 128],
                                 A_all[:, jc * 128:(jc + 1) * 128],
                                 start=(jc == 0), stop=(jc == 31)).then_inc(pe.h, 1)
                    t.wait_ge(dve.h, b4["dve"] + it * 9 + 8)
                    t.transpose(ps_lT[:, :], tmpw[:, :],
                                C["ident_bf"][:, :]).then_inc(pe.h, 1)

            @blk.vector
            def _(v):
                for it in range(8):
                    v.wait_ge(pe.h, b4["pe"] + it * 33 + 32)
                    for e in range(8):
                        v.tensor_scalar(tmpw[:, e * 16:(e + 1) * 16],
                                        ps_l[:, e * 16:(e + 1) * 16],
                                        w_all[:, it * 8 + e:it * 8 + e + 1], None,
                                        op0=ALU.mult).then_inc(dve.h, 1)
                    v.wait_ge(pe.h, b4["pe"] + it * 33 + 33)
                    v.tensor_copy(loraT[:, it * 128:(it + 1) * 128],
                                  ps_lT[:, :]).then_inc(dve.h, 1)

        pe.n = b4["pe"] + 8 * 33
        dve.n = b4["dve"] + 8 * 9
        dW.n = b4["dW"] + 9
        emit_fence(p)

        ps_ctxH.close()

        wT = [wpref_sb, lp.sb(f"{name}_wT1", [128, 32 * 512], BF16)]
        x1_t = lp.sb(f"{name}_x1t", [128, 512], F32)
        x2_sb = lp.sb(f"{name}_x2sb", [128, 2 * 512], F32)
        ps_m = [lp.ps(f"{name}_psm{i}", [128, 512], F32) for i in range(2)]

        b5 = dict(pe=pe.n, dve=dve.n, dW=dW.n, dXs=dXs.n,
                  st=[x.n for x in st])
        NPE_OG = 8 * 33

        def x2_evac(gi):
            return b5["dve"] + gi + 1

        with nc.Block() as blk:
            @blk.gpsimd
            def _(g):
                for og in range(8):
                    if og >= 2:
                        g.wait_ge(pe.h, b5["pe"] + (og - 1) * NPE_OG)
                    g.dma_start(wT[og % 2][:, :],
                                bass.AP(wTb, og * 128 * 16384,
                                        [[16384, 128], [1, 16384]])).then_inc(dW.h, 16)

            @blk.scalar
            def _(s):
                for og in range(8):
                    for tt in range(8):
                        gi = og * 8 + tt
                        if gi > 0:
                            s.wait_ge(dve.h, x2_evac(gi - 1))
                        s.dma_start(x1_t[:, :],
                                    x1[tt * 128:(tt + 1) * 128, og * 512:(og + 1) * 512]
                                    ).then_inc(dXs.h, 16)

            @blk.tensor
            def _(t):
                for og in range(8):
                    t.wait_ge(dW.h, 16 * (b5["dW"] + og + 1))
                    for tt in range(8):
                        gi = og * 8 + tt
                        if gi >= 2:
                            t.wait_ge(dve.h, x2_evac(gi - 2))
                        ps = ps_m[gi % 2]
                        t.matmul(ps[:, :], loraT[:, tt * 128:(tt + 1) * 128],
                                 B_all[:, og * 512:(og + 1) * 512],
                                 start=True, stop=False).then_inc(pe.h, 1)
                        for dc in range(32):
                            t.matmul(ps[:, :],
                                     ln_sb[:, dc * 1024 + tt * 128:dc * 1024 + tt * 128 + 128],
                                     wT[og % 2][:, dc * 512:(dc + 1) * 512],
                                     start=False, stop=(dc == 31)).then_inc(pe.h, 1)

            @blk.vector
            def _(v):
                for og in range(8):
                    for tt in range(8):
                        gi = og * 8 + tt
                        v.wait_ge(pe.h, b5["pe"] + (gi + 1) * 33)
                        v.wait_ge(dXs.h, 16 * (b5["dXs"] + gi + 1))
                        sl = gi % 2
                        v.wait_ge(st[sl].h, 16 * (b5["st"][sl] + gi // 2))
                        v.tensor_add(x2_sb[:, sl * 512:(sl + 1) * 512],
                                     ps_m[gi % 2][:, :], x1_t[:, :]).then_inc(dve.h, 1)

            @blk.sync
            def _(sy):
                for og in range(8):
                    for tt in range(8):
                        gi = og * 8 + tt
                        sy.wait_ge(dve.h, x2_evac(gi))
                        sl = gi % 2
                        sy.dma_start(x2[tt * 128:(tt + 1) * 128, og * 512:(og + 1) * 512],
                                     x2_sb[:, sl * 512:(sl + 1) * 512]).then_inc(st[sl].h, 16)

        pe.n = b5["pe"] + 8 * NPE_OG
        dve.n = b5["dve"] + 64
        dW.n = b5["dW"] + 8
        dXs.n = b5["dXs"] + 64
        for i in range(2):
            st[i].n = b5["st"][i] + 32
    emit_fence(p)
    return b5["pe"] + 7 * NPE_OG




def emit_l1_qkv(p, C, t_in, ln2_fm, q_dram, kv_c, kn_c, qnm_row, name="l1q", sk=()):
    nc = p.nc
    with ExitStack() as lctx:
        lp = _Sub(p, lctx)
        ln_sb = lp.sb(f"{name}_ln", [128, 32 * 1024], BF16)
        wnat = lp.sb(f"{name}_wnat", [128, 4 * 4096], BF16)
        wT = lp.sb(f"{name}_wT", [128, 32 * 512], BF16)
        ost = lp.sb(f"{name}_ost", [128, 4 * 512], BF16)
        k2_sb = lp.sb(f"{name}_k2", [128, 512], F32)
        kn_tm = lp.sb(f"{name}_kntm", [128, 8 * 32], F32)
        kn_T = lp.sb(f"{name}_knT", [32, 8 * 128], F32)
        q2_sb = lp.sb(f"{name}_q2", [128, 512], F32)
        ones_colf = lp.sb(f"{name}_onescf", [128, 1], F32)
        qtmp = lp.sb(f"{name}_qtmp", [1, 1], F32)
        ps_m = [lp.ps(f"{name}_psm{i}", [128, 512], F32) for i in range(2)]
        ps_w = [lp.ps(f"{name}_psw{i}", [128, 128], BF16) for i in range(2)]
        ps_qn = lp.ps(f"{name}_psqn", [1, 512], F32)
        ps_kn = lp.ps(f"{name}_pskn", [32, 128], F32)

        dW, dX, dO, pe, act, dve, pool = (p.s["dW"], p.s["dX"], p.s["dO"],
                                          p.s["pe"], p.s["act"], p.s["dve"],
                                          p.s["pool"])
        st = [p.s[f"st{i}"] for i in range(4)]
        b = dict(dW=dW.n, dX=dX.n, dO=dO.n, pe=pe.n, act=act.n, dve=dve.n,
                 pool=pool.n, st=[x.n for x in st])

        NPE_SEG = [392, 384, 384]
        NDVE_SEG = [148, 160, 136]
        NACT_SEG = [8, 16, 0]
        PE_OFF, DVE_OFF, ACT_OFF = [0], [0], [0]
        for s_ in range(24):
            pr = s_ // 8
            PE_OFF.append(PE_OFF[-1] + NPE_SEG[pr])
            DVE_OFF.append(DVE_OFF[-1] + NDVE_SEG[pr])
            ACT_OFF.append(ACT_OFF[-1] + NACT_SEG[pr])

        def tp_evac(kg):
            s_, k_ = kg // 128, kg % 128
            return b["dve"] + DVE_OFF[s_] + k_ + 1

        def grp_consumed(s_, gr):
            pr = s_ // 8
            if pr == 0:
                dv = b["dve"] + DVE_OFF[s_] + 128 + _q_dve_off(gr) + 1
                ac = b["act"] + ACT_OFF[s_] + gr + 1
                return dv, ac
            if pr == 1:
                dv = b["dve"] + DVE_OFF[s_] + 128 + gr * 4 + 4
                ac = b["act"] + ACT_OFF[s_] + gr * 2 + 2
                return dv, ac
            return b["dve"] + DVE_OFF[s_] + 128 + gr + 1, None

        def _q_dve_off(gr):
            return sum(3 if g % 2 == 1 else 2 for g in range(gr))

        wsrc = [t_in["L1_attn_w_q"], t_in["L1_attn_w_k"], t_in["L1_attn_w_v"]]

        with nc.Block() as blk:
            @blk.sync
            def _(sy):
                for c in range(32):
                    sy.dma_start(ln_sb[:, c * 1024:(c + 1) * 1024],
                                 ln2_fm[c * 128:(c + 1) * 128, :]).then_inc(dX.h, 16)

            @blk.gpsimd
            def _(g):
                g.memset(ones_colf[:, :], 1.0).then_inc(pool.h, 1)
                for s_ in range(24):
                    pr, eg = s_ // 8, s_ % 8
                    if s_ > 0:
                        g.wait_ge(pe.h, b["pe"] + PE_OFF[s_ - 1] + 128)
                    for sub in range(4):
                        g.dma_start(wnat[:, sub * 4096:(sub + 1) * 4096],
                                    bass.AP(wsrc[pr], (eg * 512 + sub * 128) * 4096,
                                            [[4096, 128], [1, 4096]])).then_inc(dW.h, 16)

            @blk.tensor
            def _(t):
                t.wait_ge(dX.h, 16 * (b["dX"] + 32))
                for s_ in range(24):
                    pr, eg = s_ // 8, s_ % 8
                    t.wait_ge(dW.h, 16 * (b["dW"] + (s_ + 1) * 4))
                    for j in range(128):
                        kg = s_ * 128 + j
                        if kg >= 2:
                            t.wait_ge(dve.h, tp_evac(kg - 2))
                        dc, sub = j // 4, j % 4
                        t.transpose(ps_w[kg % 2][:, :],
                                    wnat[:, sub * 4096 + dc * 128:sub * 4096 + dc * 128 + 128],
                                    C["ident_bf"][:, :]).then_inc(pe.h, 1)
                    t.wait_ge(dve.h, b["dve"] + DVE_OFF[s_] + 128)
                    for gr in range(8):
                        gi = s_ * 8 + gr
                        if gi >= 2:
                            s2, g2 = (gi - 2) // 8, (gi - 2) % 8
                            dv, ac = grp_consumed(s2, g2)
                            t.wait_ge(dve.h, dv)
                            if ac is not None:
                                t.wait_ge(act.h, ac)
                        ps = ps_m[gi % 2]
                        if pr == 0:
                            ec, th = gr // 2, gr % 2
                            for dc in range(32):
                                t.matmul(ps[:, :],
                                         wT[:, dc * 512 + ec * 128:dc * 512 + ec * 128 + 128],
                                         ln_sb[:, dc * 1024 + th * 512:dc * 1024 + th * 512 + 512],
                                         start=(dc == 0), stop=(dc == 31)).then_inc(pe.h, 1)
                            t.wait_ge(act.h, b["act"] + ACT_OFF[s_] + gr + 1)
                            if gr > 0:
                                t.wait_ge(dve.h, b["dve"] + DVE_OFF[s_] + 128 + _q_dve_off(gr - 1) + 2)
                            if "qn_mm" in sk:
                                t.matmul(ps_w[0][0:1, 0:128], ones_colf[:, 0:1],
                                         q2_sb[:, 0:128].bitcast(BF16)[:, 0:128],
                                         start=True, stop=True).then_inc(pe.h, 1) if False else t.nop().then_inc(pe.h, 1)
                            else:
                                t.matmul(ps_qn[0:1, :], ones_colf[:, 0:1], q2_sb[:, :],
                                         start=True, stop=True).then_inc(pe.h, 1)
                        else:
                            tt = gr
                            for dc in range(32):
                                t.matmul(ps[:, :],
                                         ln_sb[:, dc * 1024 + tt * 128:dc * 1024 + tt * 128 + 128],
                                         wT[:, dc * 512:(dc + 1) * 512],
                                         start=(dc == 0), stop=(dc == 31)).then_inc(pe.h, 1)
                t.wait_ge(dve.h, b["dve"] + DVE_OFF[16])
                for tt in range(8):
                    if tt > 0:
                        t.wait_ge(dve.h, b["dve"] + DVE_OFF[24] + tt)
                    t.transpose(ps_kn[0:32, :], kn_tm[:, tt * 32:(tt + 1) * 32],
                                C["ident_f"][:, :]).then_inc(pe.h, 1)

            @blk.scalar
            def _(s):
                for s_ in range(24):
                    pr, eg = s_ // 8, s_ % 8
                    if pr == 0:
                        for gr in range(8):
                            gi = s_ * 8 + gr
                            s.wait_ge(pe.h, b["pe"] + PE_OFF[s_] + 128 + gr * 33 + 32)
                            if gi > 0:
                                s2, g2 = (gi - 1) // 8, (gi - 1) % 8
                                s.wait_ge(pe.h, b["pe"] + PE_OFF[s2] + 128 + g2 * 33 + 33)
                            if "qn_sq" in sk:
                                s.nop().then_inc(act.h, 1)
                            else:
                                s.wait_ge(dve.h, b["dve"] + DVE_OFF[s_] + 128 + _q_dve_off(gr) + 1)
                                s.activation(q2_sb[:, :], ps_m[gi % 2][:, :],
                                             AF.Square).then_inc(act.h, 1)
                    elif pr == 1:
                        for gr in range(8):
                            gi = s_ * 8 + gr
                            tt = gr
                            s.wait_ge(pe.h, b["pe"] + PE_OFF[s_] + 128 + gr * 32 + 32)
                            sl = gi % 4
                            s.wait_ge(st[sl].h, 16 * (b["st"][sl] + max(0, (gi - 4) // 4 + 1)))
                            s.activation(ost[:, sl * 512:(sl + 1) * 512],
                                         ps_m[gi % 2][:, :], AF.Identity).then_inc(act.h, 1)
                            if gi > 0:
                                s.wait_ge(dve.h, b["dve"] + DVE_OFF[(gi - 1) // 8] + 128 + ((gi - 1) % 8) * 4 + 4)
                            if "kn_sq" in sk:
                                s.nop().then_inc(act.h, 1)
                            else:
                                s.activation(k2_sb[:, :], ps_m[gi % 2][:, :],
                                             AF.Square).then_inc(act.h, 1)

            @blk.vector
            def _(v):
                for s_ in range(24):
                    pr, eg = s_ // 8, s_ % 8
                    for j in range(128):
                        kg = s_ * 128 + j
                        v.wait_ge(pe.h, b["pe"] + PE_OFF[s_] + j + 1)
                        dc, sub = j // 4, j % 4
                        v.tensor_copy(wT[:, dc * 512 + sub * 128:dc * 512 + sub * 128 + 128],
                                      ps_w[kg % 2][:, :]).then_inc(dve.h, 1)
                    for gr in range(8):
                        gi = s_ * 8 + gr
                        if pr == 0:
                            ec, th = gr // 2, gr % 2
                            h = eg * 4 + ec
                            v.wait_ge(pe.h, b["pe"] + PE_OFF[s_] + 128 + gr * 33 + 32)
                            sl = gi % 4
                            v.wait_ge(st[sl].h, 16 * (b["st"][sl] + max(0, (gi - 4) // 4 + 1)))
                            v.tensor_scalar(ost[:, sl * 512:(sl + 1) * 512],
                                            ps_m[gi % 2][:, :], 0.08838834764831845,
                                            None, op0=ALU.mult).then_inc(dve.h, 1)
                            v.wait_ge(pe.h, b["pe"] + PE_OFF[s_] + 128 + gr * 33 + 33)
                            if "qn_red" in sk:
                                v.engine_nop().then_inc(dve.h, 1)
                                if th == 1:
                                    v.engine_nop().then_inc(dve.h, 1)
                            elif th == 0:
                                v.reduce_max(qnm_row[0:1, h:h + 1], ps_qn[0:1, :],
                                             axis=AX.X).then_inc(dve.h, 1)
                            else:
                                v.reduce_max(qtmp[0:1, 0:1], ps_qn[0:1, :],
                                             axis=AX.X).then_inc(dve.h, 1)
                                v.drain()
                                v.tensor_max(qnm_row[0:1, h:h + 1],
                                             qnm_row[0:1, h:h + 1],
                                             qtmp[0:1, 0:1]).then_inc(dve.h, 1)
                        elif pr == 1:
                            tt = gr
                            v.wait_ge(act.h, b["act"] + ACT_OFF[s_] + gr * 2 + 2)
                            for jj in range(4):
                                h = eg * 4 + jj
                                if "kn_red" in sk:
                                    v.engine_nop().then_inc(dve.h, 1)
                                else:
                                    v.tensor_reduce(kn_tm[:, tt * 32 + h:tt * 32 + h + 1],
                                                    k2_sb[:, jj * 128:(jj + 1) * 128],
                                                    axis=AX.X, op=ALU.add).then_inc(dve.h, 1)
                        else:
                            tt = gr
                            v.wait_ge(pe.h, b["pe"] + PE_OFF[s_] + 128 + gr * 32 + 32)
                            sl = gi % 4
                            v.wait_ge(st[sl].h, 16 * (b["st"][sl] + max(0, (gi - 4) // 4 + 1)))
                            v.tensor_copy(ost[:, sl * 512:(sl + 1) * 512],
                                          ps_m[gi % 2][:, :]).then_inc(dve.h, 1)
                for tt in range(8):
                    v.wait_ge(pe.h, b["pe"] + PE_OFF[24] + tt + 1)
                    v.tensor_copy(kn_T[0:32, tt * 128:(tt + 1) * 128],
                                  ps_kn[0:32, :]).then_inc(dve.h, 1)

            @blk.sync
            def _(sy):
                for s_ in range(24):
                    pr, eg = s_ // 8, s_ % 8
                    for gr in range(8):
                        gi = s_ * 8 + gr
                        sl = gi % 4
                        if pr == 0:
                            ec, th = gr // 2, gr % 2
                            h = eg * 4 + ec
                            sy.wait_ge(dve.h, b["dve"] + DVE_OFF[s_] + 128 + _q_dve_off(gr) + 1)
                            dst = bass.AP(q_dram, h * 128 * 1024 + th * 512,
                                          [[1024, 128], [1, 512]])
                        elif pr == 1:
                            tt = gr
                            sy.wait_ge(act.h, b["act"] + ACT_OFF[s_] + gr * 2 + 1)
                            dst = bass.AP(kv_c, tt * 128 * 4096 + eg * 512,
                                          [[4096, 128], [1, 512]])
                        else:
                            tt = gr
                            sy.wait_ge(dve.h, b["dve"] + DVE_OFF[s_] + 128 + gr + 1)
                            dst = bass.AP(kv_c, (1024 + tt * 128) * 4096 + eg * 512,
                                          [[4096, 128], [1, 512]])
                        sy.dma_start(dst, ost[:, sl * 512:(sl + 1) * 512]).then_inc(st[sl].h, 16)
                sy.wait_ge(dve.h, b["dve"] + DVE_OFF[24] + 8)
                sy.dma_start(kn_c[0:32, :], kn_T[0:32, :]).then_inc(dO.h, 16)

        dX.n = b["dX"] + 32
        dW.n = b["dW"] + 96
        pe.n = b["pe"] + PE_OFF[24] + 8
        act.n = b["act"] + ACT_OFF[24]
        dve.n = b["dve"] + DVE_OFF[24] + 8
        dO.n = b["dO"] + 1
        for i in range(4):
            st[i].n = b["st"][i] + 48
        pool.n = b["pool"] + 1
    emit_fence(p)




import math as _math
_KBIAS = float(_math.log(1.02) - 0.5 * _math.log(128.0))


def emit_l1_gather(p, C, kv_c, kvg, kn_c, kng, qnm_row, negc_bc, rg=None,
                   name="l1g"):
    nc = p.nc
    rg = rg if rg is not None else [list(range(NCORES))]
    NR = len(rg[0])
    with ExitStack() as lctx:
        lp = _Sub(p, lctx)
        knga = lp.sb(f"{name}_knga", [32, NR * 1024], F32)
        knm = lp.sb(f"{name}_knm", [32, 1], F32)
        qnm_col = lp.sb(f"{name}_qnc", [32, 1], F32)
        prod = lp.sb(f"{name}_prod", [32, 1], F32)
        lnp = lp.sb(f"{name}_lnp", [32, 1], F32)
        c_col = lp.sb(f"{name}_c", [32, 1], F32)
        negc = lp.sb(f"{name}_negc", [32, 1], F32)
        negc_row = lp.sb(f"{name}_negcr", [1, 32], F32)
        kb = lp.sb(f"{name}_kb", [32, 1], F32)
        ones_f = lp.sb(f"{name}_onesf", [1, 128], F32)
        ps_a = lp.ps(f"{name}_psa", [128, 128], F32)

        dW, pe, act, dve, pool, cc = (p.s["dW"], p.s["pe"], p.s["act"],
                                      p.s["dve"], p.s["pool"], p.s["cc"])
        b = dict(dW=dW.n, pe=pe.n, act=act.n, dve=dve.n, pool=pool.n, cc=cc.n)

        with nc.Block() as blk:
            @blk.gpsimd
            def _(g):
                g.memset(kb[:, :], _KBIAS)
                g.memset(ones_f[:, :], 1.0).then_inc(pool.h, 1)
                g.collective_compute(
                    "AllGather", ALU.bypass, replica_groups=rg,
                    ins=[kv_c.ap()], outs=[kvg.ap()]).then_inc(cc.h, 1)
                g.collective_compute(
                    "AllGather", ALU.bypass, replica_groups=rg,
                    ins=[kn_c.ap()], outs=[kng.ap()]).then_inc(cc.h, 1)
                g.wait_ge(cc.h, b["cc"] + 2)
                g.dma_start(knga[0:32, :],
                            bass.AP(kng, 0, [[1024, 32], [32 * 1024, NR], [1, 1024]])
                            ).then_inc(dW.h, 16)

            @blk.vector
            def _(v):
                v.wait_ge(dW.h, 16 * (b["dW"] + 1))
                v.reduce_max(knm[0:32, :], knga[0:32, :], axis=AX.X).then_inc(dve.h, 1)

            @blk.tensor
            def _(t):
                t.wait_ge(p.s["pool"].h, b["pool"] + 1)
                t.transpose(ps_a[0:32, 0:1], qnm_row[0:1, 0:32],
                            C["ident_f"][0:1, 0:1]).then_inc(pe.h, 1)

            @blk.vector
            def _(v):
                v.wait_ge(pe.h, b["pe"] + 1)
                v.tensor_copy(qnm_col[0:32, :], ps_a[0:32, 0:1]).then_inc(dve.h, 1)
                v.drain()
                v.tensor_mul(prod[0:32, :], qnm_col[0:32, :],
                             knm[0:32, :]).then_inc(dve.h, 1)

            @blk.scalar
            def _(s):
                s.wait_ge(dve.h, b["dve"] + 3)
                s.activation(lnp[0:32, :], prod[0:32, :], AF.Ln)
                s.drain()
                s.activation(c_col[0:32, :], lnp[0:32, :], AF.Exp, scale=0.5,
                             bias=kb[0:32, 0:1])
                s.drain()
                s.mul(negc[0:32, :], c_col[0:32, :], -1.0).then_inc(act.h, 1)

            @blk.tensor
            def _(t):
                t.wait_ge(act.h, b["act"] + 1)
                t.transpose(ps_a[0:1, 0:32], negc[0:32, 0:1],
                            C["ident_f"][0:32, 0:32]).then_inc(pe.h, 1)

            @blk.vector
            def _(v):
                v.wait_ge(pe.h, b["pe"] + 2)
                v.tensor_copy(negc_row[0:1, :], ps_a[0:1, 0:32]).then_inc(dve.h, 1)

            @blk.tensor
            def _(t):
                t.wait_ge(dve.h, b["dve"] + 4)
                t.matmul(ps_a[:, 0:32], ones_f[0:1, :], negc_row[0:1, :],
                         start=True, stop=True).then_inc(pe.h, 1)

            @blk.vector
            def _(v):
                v.wait_ge(pe.h, b["pe"] + 3)
                v.tensor_copy(negc_bc[:, :], ps_a[:, 0:32]).then_inc(dve.h, 1)

        dW.n = b["dW"] + 1
        pe.n = b["pe"] + 3
        act.n = b["act"] + 1
        dve.n = b["dve"] + 5
        cc.n = b["cc"] + 2
        pool.n = b["pool"] + 1
    emit_fence(p)



def emit_l1_qkv_t2(p, C, wTb_k, wTb_v, wTb_q, ln2_fm, k_c, v_c, q_dram,
                   kn_c, qnm_row, kg, vg, kng, wpref=None, pref_cnt=None,
                   ln_buf=None, name="l1q"):
    nc = p.nc
    rg = [list(range(NCORES))]
    with ExitStack() as lctx:
        lp = _Sub(p, lctx)
        ln_sb = ln_buf if ln_buf is not None else lp.sb(f"{name}_ln", [128, 32 * 1024], BF16)
        wT = [lp.sb(f"{name}_wT{i}", [128, 32 * 512], BF16) for i in range(2)]
        ost = lp.sb(f"{name}_ost", [128, 4 * 512], BF16)
        k2_sb = lp.sb(f"{name}_k2", [128, 512], F32)
        kn_row = lp.sb(f"{name}_knr", [1, 2 * 512], F32)
        q2_sb = lp.sb(f"{name}_q2", [128, 512], F32)
        ones_colf = lp.sb(f"{name}_onescf", [128, 1], F32)
        qtmp = lp.sb(f"{name}_qtmp", [1, 1], F32)
        ps_m = [lp.ps(f"{name}_psm{i}", [128, 512], F32) for i in range(2)]
        ps_qn = lp.ps(f"{name}_psqn", [1, 512], F32)
        ps_kn = lp.ps(f"{name}_pskn", [1, 512], F32)

        dW, dX, dO, pe, act, dve, pool, cc = (
            p.s["dW"], p.s["dX"], p.s["dO"], p.s["pe"], p.s["act"],
            p.s["dve"], p.s["pool"], p.s["cc"])
        st = [p.s[f"st{i}"] for i in range(4)]
        b = dict(dW=dW.n, dX=dX.n, dO=dO.n, pe=pe.n, act=act.n, dve=dve.n,
                 pool=pool.n, cc=cc.n, st=[x.n for x in st])

        NPE_SEG = [264, 256, 264]
        NDVE_SEG = [16, 8, 20]
        NACT_SEG = [8, 0, 8]
        PE_OFF, DVE_OFF, ACT_OFF = [0], [0], [0]
        for s_ in range(24):
            pr = s_ // 8
            PE_OFF.append(PE_OFF[-1] + NPE_SEG[pr])
            DVE_OFF.append(DVE_OFF[-1] + NDVE_SEG[pr])
            ACT_OFF.append(ACT_OFF[-1] + NACT_SEG[pr])

        def _q_dve_off(gr):
            return sum(3 if g % 2 == 1 else 2 for g in range(gr))

        def grp_consumed(s_, gr):
            pr = s_ // 8
            if pr == 0:
                return (b["dve"] + DVE_OFF[s_] + gr * 2 + 1,
                        b["act"] + ACT_OFF[s_] + gr + 1)
            if pr == 1:
                return b["dve"] + DVE_OFF[s_] + gr + 1, None
            return (b["dve"] + DVE_OFF[s_] + _q_dve_off(gr) + 1,
                    b["act"] + ACT_OFF[s_] + gr + 1)

        wsrc = [wTb_k, wTb_v, wTb_q]

        def wbuf(s_):
            return wpref if (s_ == 0 and wpref is not None) else wT[s_ % 2]

        with nc.Block() as blk:
            @blk.sync
            def _(sy):
                for c in range(32):
                    sy.dma_start(ln_sb[:, c * 1024:(c + 1) * 1024],
                                 ln2_fm[c * 128:(c + 1) * 128, :]).then_inc(dX.h, 16)

            @blk.gpsimd
            def _(g):
                g.memset(ones_colf[:, :], 1.0).then_inc(pool.h, 1)
                for s_ in range(24):
                    pr, eg = s_ // 8, s_ % 8
                    if not (s_ == 0 and wpref is not None):
                        if s_ >= 2:
                            g.wait_ge(pe.h, b["pe"] + PE_OFF[s_ - 1])
                        g.dma_start(wT[s_ % 2][:, :],
                                    bass.AP(wsrc[pr], eg * 128 * 16384,
                                            [[16384, 128], [1, 16384]])).then_inc(dW.h, 16)
                    if s_ == 9:
                        for i in range(4):
                            g.wait_ge(st[i].h, 16 * (b["st"][i] + 16))
                        g.collective_compute(
                            "AllGather", ALU.bypass, replica_groups=rg,
                            ins=[k_c.ap()], outs=[kg.ap()]).then_inc(cc.h, 1)
                        g.wait_ge(dO.h, 16 * (b["dO"] + 64))
                        g.collective_compute(
                            "AllGather", ALU.bypass, replica_groups=rg,
                            ins=[kn_c.ap()], outs=[kng.ap()]).then_inc(cc.h, 1)
                    if s_ == 17:
                        for i in range(4):
                            g.wait_ge(st[i].h, 16 * (b["st"][i] + 32))
                        g.collective_compute(
                            "AllGather", ALU.bypass, replica_groups=rg,
                            ins=[v_c.ap()], outs=[vg.ap()]).then_inc(cc.h, 1)

            @blk.tensor
            def _(t):
                t.wait_ge(dX.h, 16 * (b["dX"] + 32))
                t.wait_ge(pool.h, b["pool"] + 1)
                skip0 = 1 if wpref is not None else 0
                for s_ in range(24):
                    pr, eg = s_ // 8, s_ % 8
                    if s_ == 0 and wpref is not None:
                        t.wait_ge(p.s["dWp"].h, 16 * pref_cnt)
                    else:
                        t.wait_ge(dW.h, 16 * (b["dW"] + s_ + 1 - skip0))
                    for gr in range(8):
                        gi = s_ * 8 + gr
                        if gi >= 2:
                            s2, g2 = (gi - 2) // 8, (gi - 2) % 8
                            dv, ac = grp_consumed(s2, g2)
                            t.wait_ge(dve.h, dv)
                            if ac is not None:
                                t.wait_ge(act.h, ac)
                        ps = ps_m[gi % 2]
                        if pr == 0:
                            ec, th = gr // 2, gr % 2
                            for dc in range(32):
                                t.matmul(ps[:, :],
                                         wbuf(s_)[:, dc * 512 + ec * 128:dc * 512 + ec * 128 + 128],
                                         ln_sb[:, dc * 1024 + th * 512:dc * 1024 + th * 512 + 512],
                                         start=(dc == 0), stop=(dc == 31)).then_inc(pe.h, 1)
                            t.wait_ge(act.h, b["act"] + ACT_OFF[s_] + gr + 1)
                            if gi > 0:
                                t.wait_ge(dve.h, b["dve"] + DVE_OFF[(gi - 1) // 8] + ((gi - 1) % 8) * 2 + 2)
                            t.matmul(ps_kn[0:1, :], ones_colf[:, 0:1], k2_sb[:, :],
                                     start=True, stop=True).then_inc(pe.h, 1)
                        elif pr == 1:
                            tt = gr
                            for dc in range(32):
                                t.matmul(ps[:, :],
                                         ln_sb[:, dc * 1024 + tt * 128:dc * 1024 + tt * 128 + 128],
                                         wbuf(s_)[:, dc * 512:(dc + 1) * 512],
                                         start=(dc == 0), stop=(dc == 31)).then_inc(pe.h, 1)
                        else:
                            ec, th = gr // 2, gr % 2
                            for dc in range(32):
                                t.matmul(ps[:, :],
                                         wbuf(s_)[:, dc * 512 + ec * 128:dc * 512 + ec * 128 + 128],
                                         ln_sb[:, dc * 1024 + th * 512:dc * 1024 + th * 512 + 512],
                                         start=(dc == 0), stop=(dc == 31)).then_inc(pe.h, 1)
                            t.wait_ge(act.h, b["act"] + ACT_OFF[s_] + gr + 1)
                            qgi = (s_ - 16) * 8 + gr
                            if qgi > 0:
                                t.wait_ge(dve.h, b["dve"] + DVE_OFF[16 + (qgi - 1) // 8] + _q_dve_off((qgi - 1) % 8) + 2)
                            t.matmul(ps_qn[0:1, :], ones_colf[:, 0:1], q2_sb[:, :],
                                     start=True, stop=True).then_inc(pe.h, 1)

            @blk.scalar
            def _(s):
                for s_ in range(24):
                    pr, eg = s_ // 8, s_ % 8
                    if pr == 1:
                        continue
                    for gr in range(8):
                        gi = s_ * 8 + gr
                        s.wait_ge(pe.h, b["pe"] + PE_OFF[s_] + gr * 33 + 32)
                        if pr == 0:
                            if gi > 0:
                                s2, g2 = (gi - 1) // 8, (gi - 1) % 8
                                s.wait_ge(pe.h, b["pe"] + PE_OFF[s2] + g2 * 33 + 33)
                            s.wait_ge(dve.h, b["dve"] + DVE_OFF[s_] + gr * 2 + 1)
                            s.activation(k2_sb[:, :], ps_m[gi % 2][:, :],
                                         AF.Square).then_inc(act.h, 1)
                        else:
                            qgi = (s_ - 16) * 8 + gr
                            if qgi > 0:
                                s2, g2 = 16 + (qgi - 1) // 8, (qgi - 1) % 8
                                s.wait_ge(pe.h, b["pe"] + PE_OFF[s2] + g2 * 33 + 33)
                            s.wait_ge(dve.h, b["dve"] + DVE_OFF[s_] + _q_dve_off(gr) + 1)
                            s.activation(q2_sb[:, :], ps_m[gi % 2][:, :],
                                         AF.Square).then_inc(act.h, 1)

            @blk.vector
            def _(v):
                for s_ in range(24):
                    pr, eg = s_ // 8, s_ % 8
                    for gr in range(8):
                        gi = s_ * 8 + gr
                        if pr == 0:
                            ec, th = gr // 2, gr % 2
                            kgi = gi
                            v.wait_ge(pe.h, b["pe"] + PE_OFF[s_] + gr * 33 + 32)
                            sl = gi % 4
                            v.wait_ge(st[sl].h, 16 * (b["st"][sl] + max(0, (gi - 4) // 4 + 1)))
                            v.tensor_copy(ost[:, sl * 512:(sl + 1) * 512],
                                          ps_m[gi % 2][:, :]).then_inc(dve.h, 1)
                            v.wait_ge(pe.h, b["pe"] + PE_OFF[s_] + gr * 33 + 33)
                            if kgi >= 2:
                                v.wait_ge(dO.h, 16 * (b["dO"] + kgi - 1))
                            v.tensor_copy(kn_row[0:1, (kgi % 2) * 512:(kgi % 2 + 1) * 512],
                                          ps_kn[0:1, :]).then_inc(dve.h, 1)
                        elif pr == 1:
                            tt = gr
                            v.wait_ge(pe.h, b["pe"] + PE_OFF[s_] + gr * 32 + 32)
                            sl = gi % 4
                            v.wait_ge(st[sl].h, 16 * (b["st"][sl] + max(0, (gi - 4) // 4 + 1)))
                            v.tensor_copy(ost[:, sl * 512:(sl + 1) * 512],
                                          ps_m[gi % 2][:, :]).then_inc(dve.h, 1)
                        else:
                            ec, th = gr // 2, gr % 2
                            h = eg * 4 + ec
                            v.wait_ge(pe.h, b["pe"] + PE_OFF[s_] + gr * 33 + 32)
                            sl = gi % 4
                            v.wait_ge(st[sl].h, 16 * (b["st"][sl] + max(0, (gi - 4) // 4 + 1)))
                            v.tensor_scalar(ost[:, sl * 512:(sl + 1) * 512],
                                            ps_m[gi % 2][:, :], 0.08838834764831845,
                                            None, op0=ALU.mult).then_inc(dve.h, 1)
                            v.wait_ge(pe.h, b["pe"] + PE_OFF[s_] + gr * 33 + 33)
                            if th == 0:
                                v.reduce_max(qnm_row[0:1, h:h + 1], ps_qn[0:1, :],
                                             axis=AX.X).then_inc(dve.h, 1)
                            else:
                                v.reduce_max(qtmp[0:1, 0:1], ps_qn[0:1, :],
                                             axis=AX.X).then_inc(dve.h, 1)
                                v.drain()
                                v.tensor_max(qnm_row[0:1, h:h + 1],
                                             qnm_row[0:1, h:h + 1],
                                             qtmp[0:1, 0:1]).then_inc(dve.h, 1)

            @blk.sync
            def _(sy):
                for s_ in range(24):
                    pr, eg = s_ // 8, s_ % 8
                    for gr in range(8):
                        gi = s_ * 8 + gr
                        sl = gi % 4
                        if pr == 0:
                            ec, th = gr // 2, gr % 2
                            h = eg * 4 + ec
                            sy.wait_ge(dve.h, b["dve"] + DVE_OFF[s_] + gr * 2 + 1)
                            dst = bass.AP(k_c, h * 128 * 1024 + th * 512,
                                          [[1024, 128], [1, 512]])
                            sy.dma_start(dst, ost[:, sl * 512:(sl + 1) * 512]).then_inc(st[sl].h, 16)
                            sy.wait_ge(dve.h, b["dve"] + DVE_OFF[s_] + gr * 2 + 2)
                            sy.dma_start(
                                bass.AP(kn_c, h * 1024 + th * 512, [[1, 1], [1, 512]]),
                                kn_row[0:1, (gi % 2) * 512:(gi % 2 + 1) * 512]).then_inc(dO.h, 16)
                        elif pr == 1:
                            tt = gr
                            sy.wait_ge(dve.h, b["dve"] + DVE_OFF[s_] + gr + 1)
                            dst = bass.AP(v_c, tt * 128 * 4096 + eg * 512,
                                          [[4096, 128], [1, 512]])
                            sy.dma_start(dst, ost[:, sl * 512:(sl + 1) * 512]).then_inc(st[sl].h, 16)
                        else:
                            ec, th = gr // 2, gr % 2
                            h = eg * 4 + ec
                            sy.wait_ge(dve.h, b["dve"] + DVE_OFF[s_] + _q_dve_off(gr) + 1)
                            dst = bass.AP(q_dram, h * 128 * 1024 + th * 512,
                                          [[1024, 128], [1, 512]])
                            sy.dma_start(dst, ost[:, sl * 512:(sl + 1) * 512]).then_inc(st[sl].h, 16)

        dX.n = b["dX"] + 32
        dW.n = b["dW"] + 24 - (1 if wpref is not None else 0)
        pe.n = b["pe"] + PE_OFF[24]
        act.n = b["act"] + ACT_OFF[24]
        dve.n = b["dve"] + DVE_OFF[24]
        dO.n = b["dO"] + 64
        cc.n = b["cc"] + 3
        for i in range(4):
            st[i].n = b["st"][i] + 48
        pool.n = b["pool"] + 1
    emit_fence(p)
    return b["pe"] + PE_OFF[1]


def emit_l1_negc(p, C, kng, qnm_row, negc_bc, cc_base, name="l1n"):
    nc = p.nc
    NR = NCORES
    with ExitStack() as lctx:
        lp = _Sub(p, lctx)
        knga = lp.sb(f"{name}_knga", [32, NR * 1024], F32)
        knm = lp.sb(f"{name}_knm", [32, 1], F32)
        qnm_col = lp.sb(f"{name}_qnc", [32, 1], F32)
        prod = lp.sb(f"{name}_prod", [32, 1], F32)
        lnp = lp.sb(f"{name}_lnp", [32, 1], F32)
        c_col = lp.sb(f"{name}_c", [32, 1], F32)
        negc = lp.sb(f"{name}_negc", [32, 1], F32)
        negc_row = lp.sb(f"{name}_negcr", [1, 32], F32)
        kb = lp.sb(f"{name}_kb", [32, 1], F32)
        ones_f = lp.sb(f"{name}_onesf", [1, 128], F32)
        ps_a = lp.ps(f"{name}_psa", [128, 128], F32)

        dW, pe, act, dve, pool, cc = (p.s["dW"], p.s["pe"], p.s["act"],
                                      p.s["dve"], p.s["pool"], p.s["cc"])
        b = dict(dW=dW.n, pe=pe.n, act=act.n, dve=dve.n, pool=pool.n)

        with nc.Block() as blk:
            @blk.gpsimd
            def _(g):
                g.memset(kb[:, :], _KBIAS)
                g.memset(ones_f[:, :], 1.0).then_inc(pool.h, 1)
                g.wait_ge(cc.h, cc_base + 2)
                g.dma_start(knga[0:32, :],
                            bass.AP(kng, 0, [[1024, 32], [32 * 1024, NR], [1, 1024]])
                            ).then_inc(dW.h, 16)

            @blk.vector
            def _(v):
                v.wait_ge(dW.h, 16 * (b["dW"] + 1))
                v.reduce_max(knm[0:32, :], knga[0:32, :], axis=AX.X).then_inc(dve.h, 1)

            @blk.tensor
            def _(t):
                t.wait_ge(p.s["pool"].h, b["pool"] + 1)
                t.transpose(ps_a[0:32, 0:1], qnm_row[0:1, 0:32],
                            C["ident_f"][0:1, 0:1]).then_inc(pe.h, 1)

            @blk.vector
            def _(v):
                v.wait_ge(pe.h, b["pe"] + 1)
                v.tensor_copy(qnm_col[0:32, :], ps_a[0:32, 0:1]).then_inc(dve.h, 1)
                v.drain()
                v.tensor_mul(prod[0:32, :], qnm_col[0:32, :],
                             knm[0:32, :]).then_inc(dve.h, 1)

            @blk.scalar
            def _(s):
                s.wait_ge(dve.h, b["dve"] + 3)
                s.activation(lnp[0:32, :], prod[0:32, :], AF.Ln)
                s.drain()
                s.activation(c_col[0:32, :], lnp[0:32, :], AF.Exp, scale=0.5,
                             bias=kb[0:32, 0:1])
                s.drain()
                s.mul(negc[0:32, :], c_col[0:32, :], -1.0).then_inc(act.h, 1)

            @blk.tensor
            def _(t):
                t.wait_ge(act.h, b["act"] + 1)
                t.transpose(ps_a[0:1, 0:32], negc[0:32, 0:1],
                            C["ident_f"][0:32, 0:32]).then_inc(pe.h, 1)

            @blk.vector
            def _(v):
                v.wait_ge(pe.h, b["pe"] + 2)
                v.tensor_copy(negc_row[0:1, :], ps_a[0:1, 0:32]).then_inc(dve.h, 1)

            @blk.tensor
            def _(t):
                t.wait_ge(dve.h, b["dve"] + 4)
                t.matmul(ps_a[:, 0:32], ones_f[0:1, :], negc_row[0:1, :],
                         start=True, stop=True).then_inc(pe.h, 1)

            @blk.vector
            def _(v):
                v.wait_ge(pe.h, b["pe"] + 3)
                v.tensor_copy(negc_bc[:, :], ps_a[:, 0:32]).then_inc(dve.h, 1)

        dW.n = b["dW"] + 1
        pe.n = b["pe"] + 3
        act.n = b["act"] + 1
        dve.n = b["dve"] + 5
        pool.n = b["pool"] + 1
    return dve.n



def emit_l1_flash2(p, C, q_dram, kg, vg, negc_bc, o2_tm, NR=8,
                   cc_base=None, negc_dve=None, a_pref=None, name="l1f"):
    nc = p.nc
    cch = p.s["cc"].h
    NKC = NR * 8
    NKP = NKC // 2
    with ExitStack() as lctx:
        lp = _Sub(p, lctx)
        kh_sb = lp.sb(f"{name}_kh", [128, 2 * NR * 1024], BF16)
        vh_sb = lp.sb(f"{name}_vh", [128, 2 * NKC * 130], BF16)
        q_sb = lp.sb(f"{name}_q", [128, 2 * 1024], BF16)
        pT_sb = lp.sb(f"{name}_pT", [128, 2 * 1024], BF16)
        rl = lp.sb(f"{name}_rl", [128, 8], F32)
        o_st = lp.sb(f"{name}_ost", [128, 4 * 128], BF16)
        ps_s2 = [lp.ps(f"{name}_pss{i}", [128, 1024], F32) for i in range(2)]
        ps_o = [lp.ps(f"{name}_pso{i}", [128, 129], F32) for i in range(4)]

        pe, act, dve, pool = (p.s["pe"], p.s["act"], p.s["dve"], p.s["pool"])
        dX = p.s["dX"]
        st = [p.s[f"st{i}"] for i in range(4)]
        kv = [p.s["kv0"], p.s["kv1"]]
        vv = [p.s["vv0"], p.s["vv1"]]
        b = dict(pe=pe.n, act=act.n, dve=dve.n, pool=pool.n,
                 st=[x.n for x in st], kv=[x.n for x in kv],
                 vv=[x.n for x in vv])

        PPE = 5 * NKC
        NPASS = 64

        def base(kp):
            return 2 * kp + 8 * max(0, kp - 1)

        def s_end(pp, kc):
            return b["pe"] + pp * PPE + base(kc // 2) + (kc % 2) + 1

        def pv_end_g(kpg):
            pp, j = kpg // NKP, kpg % NKP
            return b["pe"] + pp * PPE + (base(j + 1) + 10 if j < NKP - 1 else PPE)

        def exp_n(kpg):
            return b["act"] + kpg + 1

        with nc.Block() as blk:
            @blk.gpsimd
            def _(g):
                if negc_dve is not None:
                    g.wait_ge(dve.h, negc_dve)
                g.memset(vh_sb[:, :], 1.0).then_inc(pool.h, 1)
                g.wait_ge(pool.h, b["pool"] + 1)
                if cc_base is not None:
                    g.wait_ge(cch, cc_base + 3)
                for h in range(32):
                    sl = h % 2
                    if h >= 2:
                        g.wait_ge(pe.h, b["pe"] + (h - 1) * 2 * PPE)
                    for r in range(NR):
                        src = bass.AP(vg, r * 1024 * 4096 + h * 128,
                                      [[4096, 128], [128 * 4096, 8], [1, 128]])
                        dst = bass.AP(vh_sb, sl * NKC * 130 + r * 8 * 130,
                                      [[2 * NKC * 130, 128], [130, 8], [1, 128]])
                        g.dma_start(dst, src).then_inc(vv[h % 2].h, 16)

            @blk.scalar
            def _(s):
                if negc_dve is not None:
                    s.wait_ge(dve.h, negc_dve)
                for pp in range(NPASS):
                    h = pp // 2
                    for kp in range(NKP):
                        kpg = pp * NKP + kp
                        s.wait_ge(pe.h, s_end(pp, 2 * kp + 1))
                        if kpg >= 2:
                            s.wait_ge(pe.h, pv_end_g(kpg - 2))
                        s.activation(pT_sb[:, (kpg % 2) * 1024:(kpg % 2 + 1) * 1024],
                                     ps_s2[kpg % 2][:, :], AF.Exp,
                                     bias=negc_bc[:, h:h + 1]).then_inc(act.h, 1)

            @blk.tensor
            def _(t):
                t.wait_ge(pool.h, b["pool"] + 1)
                for pp in range(NPASS):
                    h, qH = pp // 2, pp % 2
                    sl = h % 2
                    if qH == 0:
                        t.wait_ge(kv[sl].h, 16 * (b["kv"][sl] + (h // 2 + 1) * (NR + 1)))
                        t.wait_ge(vv[sl].h, 16 * (b["vv"][sl] + (h // 2 + 1) * NR))
                    for kp in range(NKP):
                        kpg = pp * NKP + kp
                        for half in range(2):
                            kc = 2 * kp + half
                            if kpg >= 2 and half == 0:
                                t.wait_ge(act.h, exp_n(kpg - 2))
                            t.matmul(ps_s2[kpg % 2][:, half * 512:(half + 1) * 512],
                                     kh_sb[:, sl * NR * 1024 + kc * 128:sl * NR * 1024 + kc * 128 + 128],
                                     q_sb[:, sl * 1024 + qH * 512:sl * 1024 + qH * 512 + 512],
                                     start=True, stop=True).then_inc(pe.h, 1)
                        if kp >= 1:
                            t.wait_ge(act.h, exp_n(kpg - 1))
                            if kp == 1:
                                t.wait_ge(dve.h, b["dve"] + pp * 8)
                            for kcq in (2 * kp - 2, 2 * kp - 1):
                                pslot = ((kpg - 1) % 2) * 1024 + (kcq % 2) * 512
                                for qq in range(4):
                                    t.matmul(ps_o[qq][:, 0:129],
                                             pT_sb[:, pslot + qq * 128:pslot + qq * 128 + 128],
                                             vh_sb[:, sl * NKC * 130 + kcq * 130:sl * NKC * 130 + kcq * 130 + 129],
                                             start=(kcq == 0), stop=False).then_inc(pe.h, 1)
                    t.wait_ge(act.h, exp_n(pp * NKP + NKP - 1))
                    for kcq in (NKC - 2, NKC - 1):
                        pslot = ((NKP - 1) % 2) * 1024 + (kcq % 2) * 512
                        for qq in range(4):
                            t.matmul(ps_o[qq][:, 0:129],
                                     pT_sb[:, pslot + qq * 128:pslot + qq * 128 + 128],
                                     vh_sb[:, sl * NKC * 130 + kcq * 130:sl * NKC * 130 + kcq * 130 + 129],
                                     start=False, stop=(kcq == NKC - 1)).then_inc(pe.h, 1)

            @blk.vector
            def _(v):
                for pp in range(NPASS):
                    h, qH = pp // 2, pp % 2
                    v.wait_ge(pe.h, b["pe"] + (pp + 1) * PPE)
                    for qq in range(4):
                        qs = qH * 4 + qq
                        rc = (pp % 2) * 4 + qq
                        v.reciprocal(rl[:, rc:rc + 1], ps_o[qq][:, 128:129]).then_inc(dve.h, 1)
                        v.drain()
                        sl2 = qs % 4
                        v.wait_ge(st[sl2].h, 16 * (b["st"][sl2] + pp))
                        v.tensor_scalar(o_st[:, sl2 * 128:(sl2 + 1) * 128],
                                        ps_o[qq][:, 0:128],
                                        rl[:, rc:rc + 1], None, op0=ALU.mult).then_inc(dve.h, 1)

            @blk.sync
            def _(sy):
                if negc_dve is not None:
                    sy.wait_ge(dve.h, negc_dve)
                if cc_base is not None:
                    sy.wait_ge(cch, cc_base + 1)
                for pp in range(NPASS + 1):
                    if pp < NPASS and pp % 2 == 0:
                        h = pp // 2
                        sl = h % 2
                        if h >= 2:
                            sy.wait_ge(pe.h, b["pe"] + (h - 1) * 2 * PPE)
                        for r in range(NR):
                            sy.dma_start(
                                kh_sb[:, sl * NR * 1024 + r * 1024:sl * NR * 1024 + (r + 1) * 1024],
                                bass.AP(kg, (r * 4096 + h * 128) * 1024,
                                        [[1024, 128], [1, 1024]])
                                ).then_inc(kv[sl].h, 16)
                        sy.dma_start(q_sb[:, sl * 1024:(sl + 1) * 1024],
                                     bass.AP(q_dram, h * 128 * 1024,
                                             [[1024, 128], [1, 1024]])
                                     ).then_inc(kv[sl].h, 16)
                    if pp >= 1:
                        ppp = pp - 1
                        h, qH = ppp // 2, ppp % 2
                        for qq in range(4):
                            qs = qH * 4 + qq
                            sy.wait_ge(dve.h, b["dve"] + ppp * 8 + qq * 2 + 2)
                            sl2 = qs % 4
                            sy.dma_start(
                                bass.AP(o2_tm, qs * 128 * 4096 + h * 128,
                                        [[4096, 128], [1, 128]]),
                                o_st[:, sl2 * 128:(sl2 + 1) * 128]).then_inc(st[sl2].h, 16)
                        if a_pref is not None and ppp % 2 == 1:
                            c = ppp // 2
                            for i in range(4):
                                sy.wait_ge(st[i].h, 16 * (b["st"][i] + pp))
                            sy.dma_start(
                                a_pref[:, c * 1024:(c + 1) * 1024],
                                bass.AP(o2_tm, c * 128, [[4096, 1024], [1, 128]]),
                                transpose=True).then_inc(dX.h, 16)

        pe.n = b["pe"] + NPASS * PPE
        if a_pref is not None:
            dX.n += 32
        act.n = b["act"] + NPASS * NKP
        dve.n = b["dve"] + NPASS * 8
        pool.n = b["pool"] + 1
        for i2 in range(2):
            kv[i2].n = b["kv"][i2] + 16 * (NR + 1)
            vv[i2].n = b["vv"][i2] + 16 * NR
        for i2 in range(4):
            st[i2].n = b["st"][i2] + 64
    emit_fence(p)



def emit_big_linear(p, C, name, act_mode, act_dram, w_dram, xin, xout,
                    n_extra=0, emit_extra=None):
    nc = p.nc
    with ExitStack() as lctx:
        lp = _Sub(p, lctx)
        a_sb = lp.sb(f"{name}_a", [128, 32 * 1024], BF16)
        wnat = lp.sb(f"{name}_wnat", [128, 4 * 4096], BF16)
        wT = lp.sb(f"{name}_wT", [128, 32 * 512], BF16)
        xi_t = lp.sb(f"{name}_xit", [128, 512], F32)
        xo_sb = lp.sb(f"{name}_xosb", [128, 2 * 512], F32)
        ps_m = [lp.ps(f"{name}_psm{i}", [128, 512], F32) for i in range(2)]
        ps_w = [lp.ps(f"{name}_psw{i}", [128, 128], BF16) for i in range(2)]

        dW, dX, dXs, pe, act, dve = (p.s["dW"], p.s["dX"], p.s["dXs"],
                                     p.s["pe"], p.s["act"], p.s["dve"])
        st = [p.s[f"st{i}"] for i in range(4)]
        b = dict(dW=dW.n, dX=dX.n, dXs=dXs.n, pe=pe.n, dve=dve.n,
                 st=[x.n for x in st])
        G = 32 + n_extra
        NPE_OG = 128 + 8 * G
        NDVE_OG = 128 + 8

        def wt_evac(k):
            return b["dve"] + (k // 128) * NDVE_OG + (k % 128) + 1

        def xo_evac(gi):
            return b["dve"] + (gi // 8) * NDVE_OG + 128 + (gi % 8) + 1

        with nc.Block() as blk:
            @blk.sync
            def _(sy):
                if skip_a_loads:
                    pass
                elif act_mode == "fm":
                    for c in range(32):
                        sy.dma_start(a_sb[:, c * 1024:(c + 1) * 1024],
                                     act_dram[c * 128:(c + 1) * 128, :]).then_inc(dX.h, 16)
                else:
                    for c in range(32):
                        sy.dma_start(a_sb[:, c * 1024:(c + 1) * 1024],
                                     bass.AP(act_dram, c * 128, [[4096, 1024], [1, 128]]),
                                     transpose=True).then_inc(dX.h, 16)

            @blk.gpsimd
            def _(g):
                for og in range(8):
                    if og > 0:
                        g.wait_ge(pe.h, b["pe"] + (og - 1) * NPE_OG + 128)
                    for sub in range(4):
                        g.dma_start(wnat[:, sub * 4096:(sub + 1) * 4096],
                                    bass.AP(w_dram, (og * 512 + sub * 128) * 4096,
                                            [[4096, 128], [1, 4096]])).then_inc(dW.h, 16)

            @blk.scalar
            def _(s):
                for og in range(8):
                    for tt in range(8):
                        gi = og * 8 + tt
                        if gi > 0:
                            s.wait_ge(dve.h, xo_evac(gi - 1))
                        s.dma_start(xi_t[:, :],
                                    xin[tt * 128:(tt + 1) * 128, og * 512:(og + 1) * 512]
                                    ).then_inc(dXs.h, 16)

            @blk.tensor
            def _(t):
                t.wait_ge(dX.h, 16 * (b["dX"] + 32))
                for og in range(8):
                    t.wait_ge(dW.h, 16 * (b["dW"] + (og + 1) * 4))
                    for j in range(128):
                        k = og * 128 + j
                        if k >= 2:
                            t.wait_ge(dve.h, wt_evac(k - 2))
                        dc, sub = j // 4, j % 4
                        t.transpose(ps_w[k % 2][:, :],
                                    wnat[:, sub * 4096 + dc * 128:sub * 4096 + dc * 128 + 128],
                                    C["ident_bf"][:, :]).then_inc(pe.h, 1)
                    t.wait_ge(dve.h, b["dve"] + og * NDVE_OG + 128)
                    for tt in range(8):
                        gi = og * 8 + tt
                        if gi >= 2:
                            t.wait_ge(dve.h, xo_evac(gi - 2))
                        ps = ps_m[gi % 2]
                        if emit_extra is not None:
                            emit_extra(t, p, ps, og, tt)
                        for dc in range(32):
                            t.matmul(ps[:, :],
                                     a_sb[:, dc * 1024 + tt * 128:dc * 1024 + tt * 128 + 128],
                                     wT[:, dc * 512:(dc + 1) * 512],
                                     start=(dc == 0 and n_extra == 0),
                                     stop=(dc == 31)).then_inc(pe.h, 1)

            @blk.vector
            def _(v):
                for og in range(8):
                    for j in range(128):
                        v.wait_ge(pe.h, b["pe"] + og * NPE_OG + j + 1)
                        dc, sub = j // 4, j % 4
                        v.tensor_copy(wT[:, dc * 512 + sub * 128:dc * 512 + sub * 128 + 128],
                                      ps_w[(og * 128 + j) % 2][:, :]).then_inc(dve.h, 1)
                    for tt in range(8):
                        gi = og * 8 + tt
                        v.wait_ge(pe.h, b["pe"] + og * NPE_OG + 128 + (tt + 1) * G)
                        v.wait_ge(dXs.h, 16 * (b["dXs"] + gi + 1))
                        sl = gi % 2
                        v.wait_ge(st[sl].h, 16 * (b["st"][sl] + gi // 2))
                        v.tensor_add(xo_sb[:, sl * 512:(sl + 1) * 512],
                                     ps_m[gi % 2][:, :], xi_t[:, :]).then_inc(dve.h, 1)

            @blk.sync
            def _(sy):
                for og in range(8):
                    for tt in range(8):
                        gi = og * 8 + tt
                        sy.wait_ge(dve.h, xo_evac(gi))
                        sl = gi % 2
                        sy.dma_start(xout[tt * 128:(tt + 1) * 128, og * 512:(og + 1) * 512],
                                     xo_sb[:, sl * 512:(sl + 1) * 512]).then_inc(st[sl].h, 16)

        pe.n = b["pe"] + 8 * NPE_OG
        dve.n = b["dve"] + 8 * NDVE_OG
        dW.n = b["dW"] + 32
        dX.n = b["dX"] + 32
        dXs.n = b["dXs"] + 64
        for i in range(2):
            st[i].n = b["st"][i] + 32
    emit_fence(p)




def emit_l1_ffn_pre_w(p, C, t_in, bias_row, A_effT, B_effT, name="l1p"):
    nc = p.nc
    with ExitStack() as lctx:
        lp = _Sub(p, lctx)
        lsi_i = lp.sb(f"{name}_lsii", [1, 4], I32)
        lsi_f = lp.sb(f"{name}_lsif", [1, 4], F32)
        le0 = lp.sb(f"{name}_le0", [1, 4], F32)
        cnt = lp.sb(f"{name}_cnt", [1, 1], F32)
        use = lp.sb(f"{name}_use", [1, 1], F32)
        ai = lp.sb(f"{name}_ai", [1, 1], F32)
        ai2 = lp.sb(f"{name}_ai2", [1, 1], F32)
        ea = lp.sb(f"{name}_ea", [1, 1], F32)
        m_row = lp.sb(f"{name}_mrow", [1, 3], F32)
        m_bc = lp.sb(f"{name}_mbc", [128, 3], F32)
        ones_f = lp.sb(f"{name}_onesf", [1, 128], F32)
        An = lp.sb(f"{name}_An", [16, 3 * 4096], BF16)
        Aeffn = lp.sb(f"{name}_Aeffn", [16, 4096], BF16)
        Bn = lp.sb(f"{name}_Bn", [128, 3 * 512], BF16)
        Beffn = lp.sb(f"{name}_Beffn", [128, 32 * 16], BF16)
        ps_a = lp.ps(f"{name}_psa", [128, 128], F32)
        ps_b = [lp.ps(f"{name}_psb{i}", [128, 128], BF16) for i in range(2)]

        dW, dX, pe, act, dve, pool = (p.s["dW"], p.s["dX"], p.s["pe"],
                                      p.s["act"], p.s["dve"], p.s["pool"])
        b = dict(dW=dW.n, dX=dX.n, pe=pe.n, act=act.n, dve=dve.n, pool=pool.n)

        with nc.Block() as blk:
            @blk.gpsimd
            def _(g):
                g.memset(ones_f[:, :], 1.0).then_inc(pool.h, 1)
                g.dma_start(lsi_i[0:1, :],
                            bass.AP(t_in["L1_ffn_lora_start_indices"], 0,
                                    [[1, 1], [1, 4]])).then_inc(dW.h, 16)
                g.dma_start(bias_row[0:1, :],
                            row_ap(t_in["L1_ffn_bias"], 4096)).then_inc(dW.h, 16)
                for a in range(3):
                    g.dma_start(An[0:16, a * 4096:(a + 1) * 4096],
                                bass.AP(t_in["L1_ffn_lora_A"], a * 16 * 4096,
                                        [[4096, 16], [1, 4096]])).then_inc(dW.h, 16)
                    g.dma_start(Bn[0:128, a * 512:(a + 1) * 512],
                                bass.AP(t_in["L1_ffn_lora_B"], a * 4096 * 16,
                                        [[16, 128], [2048, 32], [1, 16]])).then_inc(dW.h, 16)

            @blk.vector
            def _(v):
                v.wait_ge(dW.h, 16 * (b["dW"] + 8))
                v.tensor_copy(lsi_f[0:1, :], lsi_i[0:1, :]).then_inc(dve.h, 1)
                v.drain()
                v.tensor_scalar(le0[0:1, :], lsi_f[0:1, :], 0.0, None,
                                op0=ALU.is_le).then_inc(dve.h, 1)
                v.drain()
                v.reduce_sum(cnt[0:1, :], le0[0:1, :], axis=AX.X).then_inc(dve.h, 1)
                v.drain()
                v.tensor_scalar(use[0:1, :], cnt[0:1, :], 2.0, None,
                                op0=ALU.is_ge).then_inc(dve.h, 1)
                v.tensor_scalar(ai[0:1, :], cnt[0:1, :], 2.0, 0.0,
                                op0=ALU.subtract, op1=ALU.max).then_inc(dve.h, 1)
                v.drain()
                v.tensor_scalar(ai2[0:1, :], ai[0:1, :], 2.0, None,
                                op0=ALU.min).then_inc(dve.h, 1)
                v.drain()
                for a in range(3):
                    v.tensor_scalar(ea[0:1, :], ai2[0:1, :], float(a), None,
                                    op0=ALU.is_equal).then_inc(dve.h, 1)
                    v.drain()
                    v.tensor_mul(m_row[0:1, a:a + 1], ea[0:1, :],
                                 use[0:1, :]).then_inc(dve.h, 1)
                    v.drain()

            @blk.tensor
            def _(t):
                t.wait_ge(dve.h, b["dve"] + 12)
                t.wait_ge(pool.h, b["pool"] + 1)
                t.matmul(ps_a[:, 0:3], ones_f[0:1, :], m_row[0:1, :],
                         start=True, stop=True).then_inc(pe.h, 1)

            @blk.vector
            def _(v):
                v.wait_ge(pe.h, b["pe"] + 1)
                v.tensor_copy(m_bc[:, :], ps_a[:, 0:3]).then_inc(dve.h, 1)
                v.drain()
                v.tensor_scalar(Aeffn[0:16, :], An[0:16, 0:4096],
                                m_bc[0:16, 0:1], None, op0=ALU.mult).then_inc(dve.h, 1)
                v.tensor_scalar(Beffn[:, :], Bn[:, 0:512],
                                m_bc[:, 0:1], None, op0=ALU.mult).then_inc(dve.h, 1)
                v.drain()
                for a in range(1, 3):
                    v.scalar_tensor_tensor(Aeffn[0:16, :], An[0:16, a * 4096:(a + 1) * 4096],
                                           m_bc[0:16, a:a + 1], Aeffn[0:16, :],
                                           op0=ALU.mult, op1=ALU.add).then_inc(dve.h, 1)
                    v.scalar_tensor_tensor(Beffn[:, :], Bn[:, a * 512:(a + 1) * 512],
                                           m_bc[:, a:a + 1], Beffn[:, :],
                                           op0=ALU.mult, op1=ALU.add).then_inc(dve.h, 1)
                    v.drain()

            @blk.tensor
            def _(t):
                t.wait_ge(dve.h, b["dve"] + 19)
                for jc in range(32):
                    if jc >= 2:
                        t.wait_ge(dve.h, b["dve"] + 19 + (jc - 2) + 1)
                    t.transpose(ps_b[jc % 2][:, 0:16], Aeffn[0:16, jc * 128:(jc + 1) * 128],
                                C["ident_bf"][0:16, 0:16]).then_inc(pe.h, 1)
                for jc in range(32):
                    t.wait_ge(dve.h, b["dve"] + 19 + (32 + jc - 2) + 1)
                    t.transpose(ps_b[jc % 2][0:16, 0:128], Beffn[:, jc * 16:(jc + 1) * 16],
                                C["ident_bf"][:, :]).then_inc(pe.h, 1)

            @blk.vector
            def _(v):
                for jc in range(32):
                    v.wait_ge(pe.h, b["pe"] + 1 + jc + 1)
                    v.tensor_copy(A_effT[:, jc * 16:(jc + 1) * 16],
                                  ps_b[jc % 2][:, 0:16]).then_inc(dve.h, 1)
                for jc in range(32):
                    v.wait_ge(pe.h, b["pe"] + 33 + jc + 1)
                    v.tensor_copy(B_effT[0:16, jc * 128:(jc + 1) * 128],
                                  ps_b[jc % 2][0:16, 0:128]).then_inc(dve.h, 1)

        pe.n = b["pe"] + 65
        dve.n = b["dve"] + 19 + 64
        dW.n = b["dW"] + 8
        pool.n = b["pool"] + 1
    emit_fence(p)


def emit_l1_ffn_tmp(p, C, ln3_fm, A_effT, tmpT, ln_buf=None, name="l1t"):
    nc = p.nc
    with ExitStack() as lctx:
        lp = _Sub(p, lctx)
        ln_sb = ln_buf if ln_buf is not None else lp.sb(f"{name}_ln", [128, 32 * 1024], BF16)
        tmp_bf = lp.sb(f"{name}_tmpbf", [128, 16], BF16)
        ps_t = lp.ps(f"{name}_pst", [128, 16], F32)
        ps_b = [lp.ps(f"{name}_psb{i}", [128, 128], BF16) for i in range(2)]
        dX, pe, dve = p.s["dX"], p.s["pe"], p.s["dve"]
        b0 = dict(dX=dX.n)
        with nc.Block() as blk:
            @blk.sync
            def _(sy):
                for c in range(32):
                    sy.dma_start(ln_sb[:, c * 1024:(c + 1) * 1024],
                                 ln3_fm[c * 128:(c + 1) * 128, :]).then_inc(dX.h, 16)
        dX.n = b0["dX"] + 32

        b2 = dict(pe=pe.n, dve=dve.n)
        with nc.Block() as blk:
            @blk.tensor
            def _(t):
                t.wait_ge(dX.h, 16 * (b0["dX"] + 32))
                for it in range(8):
                    if it > 0:
                        t.wait_ge(dve.h, b2["dve"] + (it - 1) * 2 + 1)
                    for dc in range(32):
                        t.matmul(ps_t[:, 0:16],
                                 ln_sb[:, dc * 1024 + it * 128:dc * 1024 + it * 128 + 128],
                                 A_effT[:, dc * 16:(dc + 1) * 16],
                                 start=(dc == 0), stop=(dc == 31)).then_inc(pe.h, 1)
                    t.wait_ge(dve.h, b2["dve"] + it * 2 + 1)
                    if it >= 2:
                        t.wait_ge(dve.h, b2["dve"] + (it - 2) * 2 + 2)
                    t.transpose(ps_b[it % 2][0:16, 0:128], tmp_bf[:, 0:16],
                                C["ident_bf"][:, :]).then_inc(pe.h, 1)

            @blk.vector
            def _(v):
                for it in range(8):
                    v.wait_ge(pe.h, b2["pe"] + it * 33 + 32)
                    v.tensor_copy(tmp_bf[:, 0:16], ps_t[:, 0:16]).then_inc(dve.h, 1)
                    v.wait_ge(pe.h, b2["pe"] + it * 33 + 33)
                    v.tensor_copy(tmpT[0:16, it * 128:(it + 1) * 128],
                                  ps_b[it % 2][0:16, 0:128]).then_inc(dve.h, 1)

        pe.n = b2["pe"] + 8 * 33
        dve.n = b2["dve"] + 16
    emit_fence(p)



def emit_prefetch(p, wpref, wTb, free_pe=None):
    nc = p.nc
    dWp, pe = p.s["dWp"], p.s["pe"]
    with nc.Block() as blk:
        @blk.gpsimd
        def _(g):
            if free_pe is not None:
                g.wait_ge(pe.h, free_pe)
            g.dma_start(wpref[:, :],
                        bass.AP(wTb, 0, [[16384, 128], [1, 16384]])).then_inc(dWp.h, 16)
    dWp.n += 1
    return dWp.n


def emit_big_linear_t(p, C, name, act_mode, act_dram, wTb, xin, xout,
                      n_extra=0, emit_extra=None, wpref=None, pref_cnt=None,
                      a_buf=None, skip_a_loads=False):
    nc = p.nc
    with ExitStack() as lctx:
        lp = _Sub(p, lctx)
        a_sb = a_buf if a_buf is not None else lp.sb(f"{name}_a", [128, 32 * 1024], BF16)
        wT = [lp.sb(f"{name}_wT{i}", [128, 32 * 512], BF16) for i in range(2)]
        xi_t = lp.sb(f"{name}_xit", [128, 512], F32)
        xo_sb = lp.sb(f"{name}_xosb", [128, 2 * 512], F32)
        ps_m = [lp.ps(f"{name}_psm{i}", [128, 512], F32) for i in range(2)]

        dW, dX, dXs, pe, act, dve = (p.s["dW"], p.s["dX"], p.s["dXs"],
                                     p.s["pe"], p.s["act"], p.s["dve"])
        dWp = p.s["dWp"]
        st = [p.s[f"st{i}"] for i in range(4)]
        b = dict(dW=dW.n, dX=dX.n, dXs=dXs.n, pe=pe.n, dve=dve.n,
                 st=[x.n for x in st])
        G = 32 + n_extra
        NPE_OG = 8 * G
        skip0 = 1 if wpref is not None else 0

        def wbuf(og):
            return wpref if (og == 0 and wpref is not None) else wT[og % 2]

        def xo_evac(gi):
            return b["dve"] + gi + 1

        with nc.Block() as blk:
            @blk.sync
            def _(sy):
                if skip_a_loads:
                    pass
                elif act_mode == "fm":
                    for c in range(32):
                        sy.dma_start(a_sb[:, c * 1024:(c + 1) * 1024],
                                     act_dram[c * 128:(c + 1) * 128, :]).then_inc(dX.h, 16)
                else:
                    for c in range(32):
                        sy.dma_start(a_sb[:, c * 1024:(c + 1) * 1024],
                                     bass.AP(act_dram, c * 128, [[4096, 1024], [1, 128]]),
                                     transpose=True).then_inc(dX.h, 16)

            @blk.gpsimd
            def _(g):
                for og in range(8):
                    if og == 0 and wpref is not None:
                        continue
                    if og >= 2:
                        g.wait_ge(pe.h, b["pe"] + (og - 1) * NPE_OG)
                    g.dma_start(wT[og % 2][:, :],
                                bass.AP(wTb, og * 128 * 16384,
                                        [[16384, 128], [1, 16384]])).then_inc(dW.h, 16)

            @blk.scalar
            def _(s):
                for og in range(8):
                    for tt in range(8):
                        gi = og * 8 + tt
                        if gi > 0:
                            s.wait_ge(dve.h, xo_evac(gi - 1))
                        s.dma_start(xi_t[:, :],
                                    xin[tt * 128:(tt + 1) * 128, og * 512:(og + 1) * 512]
                                    ).then_inc(dXs.h, 16)

            @blk.tensor
            def _(t):
                if not skip_a_loads:
                    t.wait_ge(dX.h, 16 * (b["dX"] + 32))
                for og in range(8):
                    if og == 0 and wpref is not None:
                        t.wait_ge(dWp.h, 16 * pref_cnt)
                    else:
                        t.wait_ge(dW.h, 16 * (b["dW"] + og + 1 - skip0))
                    for tt in range(8):
                        gi = og * 8 + tt
                        if gi >= 2:
                            t.wait_ge(dve.h, xo_evac(gi - 2))
                        ps = ps_m[gi % 2]
                        if emit_extra is not None:
                            emit_extra(t, p, ps, og, tt)
                        for dc in range(32):
                            t.matmul(ps[:, :],
                                     a_sb[:, dc * 1024 + tt * 128:dc * 1024 + tt * 128 + 128],
                                     wbuf(og)[:, dc * 512:(dc + 1) * 512],
                                     start=(dc == 0 and n_extra == 0),
                                     stop=(dc == 31)).then_inc(pe.h, 1)

            @blk.vector
            def _(v):
                for og in range(8):
                    for tt in range(8):
                        gi = og * 8 + tt
                        v.wait_ge(pe.h, b["pe"] + (gi + 1) * G)
                        v.wait_ge(dXs.h, 16 * (b["dXs"] + gi + 1))
                        sl = gi % 2
                        v.wait_ge(st[sl].h, 16 * (b["st"][sl] + gi // 2))
                        v.tensor_add(xo_sb[:, sl * 512:(sl + 1) * 512],
                                     ps_m[gi % 2][:, :], xi_t[:, :]).then_inc(dve.h, 1)

            @blk.sync
            def _(sy):
                for og in range(8):
                    for tt in range(8):
                        gi = og * 8 + tt
                        sy.wait_ge(dve.h, xo_evac(gi))
                        sl = gi % 2
                        sy.dma_start(xout[tt * 128:(tt + 1) * 128, og * 512:(og + 1) * 512],
                                     xo_sb[:, sl * 512:(sl + 1) * 512]).then_inc(st[sl].h, 16)

        pe.n = b["pe"] + 8 * NPE_OG
        dve.n = b["dve"] + 64
        dW.n = b["dW"] + 8 - skip0
        dX.n = b["dX"] + (0 if skip_a_loads else 32)
        dXs.n = b["dXs"] + 64
        for i in range(2):
            st[i].n = b["st"][i] + 32
    emit_fence(p)
    return b["pe"] + NPE_OG


def emit_l1_qkv_t(p, C, wTb_q, wTb_k, wTb_v, ln2_fm, q_dram, kv_c, kn_c,
                  qnm_row, name="l1q"):
    nc = p.nc
    with ExitStack() as lctx:
        lp = _Sub(p, lctx)
        ln_sb = lp.sb(f"{name}_ln", [128, 32 * 1024], BF16)
        wT = [lp.sb(f"{name}_wT{i}", [128, 32 * 512], BF16) for i in range(2)]
        ost = lp.sb(f"{name}_ost", [128, 4 * 512], BF16)
        k2_sb = lp.sb(f"{name}_k2", [128, 512], F32)
        kn_tm = lp.sb(f"{name}_kntm", [128, 8 * 32], F32)
        kn_T = lp.sb(f"{name}_knT", [32, 8 * 128], F32)
        q2_sb = lp.sb(f"{name}_q2", [128, 512], F32)
        ones_colf = lp.sb(f"{name}_onescf", [128, 1], F32)
        qtmp = lp.sb(f"{name}_qtmp", [1, 1], F32)
        ps_m = [lp.ps(f"{name}_psm{i}", [128, 512], F32) for i in range(2)]
        ps_qn = lp.ps(f"{name}_psqn", [1, 512], F32)
        ps_kn = lp.ps(f"{name}_pskn", [32, 128], F32)

        dW, dX, dO, pe, act, dve, pool = (p.s["dW"], p.s["dX"], p.s["dO"],
                                          p.s["pe"], p.s["act"], p.s["dve"],
                                          p.s["pool"])
        st = [p.s[f"st{i}"] for i in range(4)]
        b = dict(dW=dW.n, dX=dX.n, dO=dO.n, pe=pe.n, act=act.n, dve=dve.n,
                 pool=pool.n, st=[x.n for x in st])

        NPE_SEG = [264, 256, 256]
        NDVE_SEG = [20, 32, 8]
        NACT_SEG = [8, 16, 0]
        PE_OFF, DVE_OFF, ACT_OFF = [0], [0], [0]
        for s_ in range(24):
            pr = s_ // 8
            PE_OFF.append(PE_OFF[-1] + NPE_SEG[pr])
            DVE_OFF.append(DVE_OFF[-1] + NDVE_SEG[pr])
            ACT_OFF.append(ACT_OFF[-1] + NACT_SEG[pr])

        def grp_consumed(s_, gr):
            pr = s_ // 8
            if pr == 0:
                dv = b["dve"] + DVE_OFF[s_] + _q_dve_off(gr) + 1
                ac = b["act"] + ACT_OFF[s_] + gr + 1
                return dv, ac
            if pr == 1:
                dv = b["dve"] + DVE_OFF[s_] + gr * 4 + 4
                ac = b["act"] + ACT_OFF[s_] + gr * 2 + 2
                return dv, ac
            return b["dve"] + DVE_OFF[s_] + gr + 1, None

        def _q_dve_off(gr):
            return sum(3 if g % 2 == 1 else 2 for g in range(gr))

        wsrc = [wTb_q, wTb_k, wTb_v]

        with nc.Block() as blk:
            @blk.sync
            def _(sy):
                for c in range(32):
                    sy.dma_start(ln_sb[:, c * 1024:(c + 1) * 1024],
                                 ln2_fm[c * 128:(c + 1) * 128, :]).then_inc(dX.h, 16)

            @blk.gpsimd
            def _(g):
                g.memset(ones_colf[:, :], 1.0).then_inc(pool.h, 1)
                for s_ in range(24):
                    pr, eg = s_ // 8, s_ % 8
                    if s_ >= 2:
                        g.wait_ge(pe.h, b["pe"] + PE_OFF[s_ - 1])
                    g.dma_start(wT[s_ % 2][:, :],
                                bass.AP(wsrc[pr], eg * 128 * 16384,
                                        [[16384, 128], [1, 16384]])).then_inc(dW.h, 16)

            @blk.tensor
            def _(t):
                t.wait_ge(dX.h, 16 * (b["dX"] + 32))
                for s_ in range(24):
                    pr, eg = s_ // 8, s_ % 8
                    t.wait_ge(dW.h, 16 * (b["dW"] + s_ + 1))
                    for gr in range(8):
                        gi = s_ * 8 + gr
                        if gi >= 2:
                            s2, g2 = (gi - 2) // 8, (gi - 2) % 8
                            dv, ac = grp_consumed(s2, g2)
                            t.wait_ge(dve.h, dv)
                            if ac is not None:
                                t.wait_ge(act.h, ac)
                        ps = ps_m[gi % 2]
                        if pr == 0:
                            ec, th = gr // 2, gr % 2
                            for dc in range(32):
                                t.matmul(ps[:, :],
                                         wT[s_ % 2][:, dc * 512 + ec * 128:dc * 512 + ec * 128 + 128],
                                         ln_sb[:, dc * 1024 + th * 512:dc * 1024 + th * 512 + 512],
                                         start=(dc == 0), stop=(dc == 31)).then_inc(pe.h, 1)
                            t.wait_ge(act.h, b["act"] + ACT_OFF[s_] + gr + 1)
                            if gr > 0:
                                t.wait_ge(dve.h, b["dve"] + DVE_OFF[s_] + _q_dve_off(gr - 1) + 2)
                            t.matmul(ps_qn[0:1, :], ones_colf[:, 0:1], q2_sb[:, :],
                                     start=True, stop=True).then_inc(pe.h, 1)
                        else:
                            tt = gr
                            for dc in range(32):
                                t.matmul(ps[:, :],
                                         ln_sb[:, dc * 1024 + tt * 128:dc * 1024 + tt * 128 + 128],
                                         wT[s_ % 2][:, dc * 512:(dc + 1) * 512],
                                         start=(dc == 0), stop=(dc == 31)).then_inc(pe.h, 1)
                t.wait_ge(dve.h, b["dve"] + DVE_OFF[16])
                for tt in range(8):
                    if tt > 0:
                        t.wait_ge(dve.h, b["dve"] + DVE_OFF[24] + tt)
                    t.transpose(ps_kn[0:32, :], kn_tm[:, tt * 32:(tt + 1) * 32],
                                C["ident_f"][:, :]).then_inc(pe.h, 1)

            @blk.scalar
            def _(s):
                for s_ in range(24):
                    pr, eg = s_ // 8, s_ % 8
                    if pr == 0:
                        for gr in range(8):
                            gi = s_ * 8 + gr
                            s.wait_ge(pe.h, b["pe"] + PE_OFF[s_] + gr * 33 + 32)
                            if gi > 0:
                                s2, g2 = (gi - 1) // 8, (gi - 1) % 8
                                s.wait_ge(pe.h, b["pe"] + PE_OFF[s2] + g2 * 33 + 33)
                            s.wait_ge(dve.h, b["dve"] + DVE_OFF[s_] + _q_dve_off(gr) + 1)
                            s.activation(q2_sb[:, :], ps_m[gi % 2][:, :],
                                         AF.Square).then_inc(act.h, 1)
                    elif pr == 1:
                        for gr in range(8):
                            gi = s_ * 8 + gr
                            s.wait_ge(pe.h, b["pe"] + PE_OFF[s_] + gr * 32 + 32)
                            sl = gi % 4
                            s.wait_ge(st[sl].h, 16 * (b["st"][sl] + max(0, (gi - 4) // 4 + 1)))
                            s.activation(ost[:, sl * 512:(sl + 1) * 512],
                                         ps_m[gi % 2][:, :], AF.Identity).then_inc(act.h, 1)
                            if gi > 64:
                                s.wait_ge(dve.h, b["dve"] + DVE_OFF[(gi - 1) // 8] + ((gi - 1) % 8) * 4 + 4)
                            s.activation(k2_sb[:, :], ps_m[gi % 2][:, :],
                                         AF.Square).then_inc(act.h, 1)

            @blk.vector
            def _(v):
                for s_ in range(24):
                    pr, eg = s_ // 8, s_ % 8
                    for gr in range(8):
                        gi = s_ * 8 + gr
                        if pr == 0:
                            ec, th = gr // 2, gr % 2
                            h = eg * 4 + ec
                            v.wait_ge(pe.h, b["pe"] + PE_OFF[s_] + gr * 33 + 32)
                            sl = gi % 4
                            v.wait_ge(st[sl].h, 16 * (b["st"][sl] + max(0, (gi - 4) // 4 + 1)))
                            v.tensor_scalar(ost[:, sl * 512:(sl + 1) * 512],
                                            ps_m[gi % 2][:, :], 0.08838834764831845,
                                            None, op0=ALU.mult).then_inc(dve.h, 1)
                            v.wait_ge(pe.h, b["pe"] + PE_OFF[s_] + gr * 33 + 33)
                            if th == 0:
                                v.reduce_max(qnm_row[0:1, h:h + 1], ps_qn[0:1, :],
                                             axis=AX.X).then_inc(dve.h, 1)
                            else:
                                v.reduce_max(qtmp[0:1, 0:1], ps_qn[0:1, :],
                                             axis=AX.X).then_inc(dve.h, 1)
                                v.drain()
                                v.tensor_max(qnm_row[0:1, h:h + 1],
                                             qnm_row[0:1, h:h + 1],
                                             qtmp[0:1, 0:1]).then_inc(dve.h, 1)
                        elif pr == 1:
                            tt = gr
                            v.wait_ge(act.h, b["act"] + ACT_OFF[s_] + gr * 2 + 2)
                            for jj in range(4):
                                h = eg * 4 + jj
                                v.tensor_reduce(kn_tm[:, tt * 32 + h:tt * 32 + h + 1],
                                                k2_sb[:, jj * 128:(jj + 1) * 128],
                                                axis=AX.X, op=ALU.add).then_inc(dve.h, 1)
                        else:
                            tt = gr
                            v.wait_ge(pe.h, b["pe"] + PE_OFF[s_] + gr * 32 + 32)
                            sl = gi % 4
                            v.wait_ge(st[sl].h, 16 * (b["st"][sl] + max(0, (gi - 4) // 4 + 1)))
                            v.tensor_copy(ost[:, sl * 512:(sl + 1) * 512],
                                          ps_m[gi % 2][:, :]).then_inc(dve.h, 1)
                for tt in range(8):
                    v.wait_ge(pe.h, b["pe"] + PE_OFF[24] + tt + 1)
                    v.tensor_copy(kn_T[0:32, tt * 128:(tt + 1) * 128],
                                  ps_kn[0:32, :]).then_inc(dve.h, 1)

            @blk.sync
            def _(sy):
                for s_ in range(24):
                    pr, eg = s_ // 8, s_ % 8
                    for gr in range(8):
                        gi = s_ * 8 + gr
                        sl = gi % 4
                        if pr == 0:
                            ec, th = gr // 2, gr % 2
                            h = eg * 4 + ec
                            sy.wait_ge(dve.h, b["dve"] + DVE_OFF[s_] + _q_dve_off(gr) + 1)
                            dst = bass.AP(q_dram, h * 128 * 1024 + th * 512,
                                          [[1024, 128], [1, 512]])
                        elif pr == 1:
                            tt = gr
                            sy.wait_ge(act.h, b["act"] + ACT_OFF[s_] + gr * 2 + 1)
                            dst = bass.AP(kv_c, tt * 128 * 4096 + eg * 512,
                                          [[4096, 128], [1, 512]])
                        else:
                            tt = gr
                            sy.wait_ge(dve.h, b["dve"] + DVE_OFF[s_] + gr + 1)
                            dst = bass.AP(kv_c, (1024 + tt * 128) * 4096 + eg * 512,
                                          [[4096, 128], [1, 512]])
                        sy.dma_start(dst, ost[:, sl * 512:(sl + 1) * 512]).then_inc(st[sl].h, 16)
                sy.wait_ge(dve.h, b["dve"] + DVE_OFF[24] + 8)
                sy.dma_start(kn_c[0:32, :], kn_T[0:32, :]).then_inc(dO.h, 16)

        dX.n = b["dX"] + 32
        dW.n = b["dW"] + 24
        pe.n = b["pe"] + PE_OFF[24] + 8
        act.n = b["act"] + ACT_OFF[24]
        dve.n = b["dve"] + DVE_OFF[24] + 8
        dO.n = b["dO"] + 1
        for i in range(4):
            st[i].n = b["st"][i] + 48
        pool.n = b["pool"] + 1
    emit_fence(p)



LN0_STARTS = [0, 128, 256, 384, 512, 640, 768, 896, 898]
LN_STARTS = [0, 128, 256, 384, 512, 640, 768, 896]

IN_SPECS = {
    "L0_attn_conv_w": [3, 4096, 3072], "L0_attn_conv_b": [3072],
    "L0_attn_wq": [1024, 16, 64], "L0_attn_bq": [16, 64],
    "L0_attn_wk": [1024, 16, 64], "L0_attn_bk": [16, 64],
    "L0_attn_wv": [1024, 16, 64], "L0_attn_bv": [16, 64],
    "L0_attn_wo": [16, 64, 1024], "L0_attn_bo": [1024],
    "L0_attn_proj_w": [1024, 4096], "L0_attn_proj_b": [4096],
    "L0_ffn_weight": [4096, 4096],
    "L0_ffn_lora_A": [8, 4096, 16], "L0_ffn_lora_B": [8, 16, 4096],
    "L0_ffn_gw1": [32, 4096], "L0_ffn_gb1": [32],
    "L0_ffn_ln_gamma": [32], "L0_ffn_ln_beta": [32],
    "L0_ffn_gw2": [8, 32], "L0_ffn_gb2": [8],
    "L1_attn_w_q": [4096, 4096], "L1_attn_w_k": [4096, 4096],
    "L1_attn_w_v": [4096, 4096], "L1_attn_w_o": [4096, 4096],
    "L1_ffn_weight": [4096, 4096], "L1_ffn_bias": [4096],
    "L1_ffn_lora_A": [3, 16, 4096], "L1_ffn_lora_B": [3, 4096, 16],
    "L1_ffn_lora_start_indices": [4],
    "L0_ffn_weightTb": [1024, 16384],
    "L1_attn_w_qTb": [1024, 16384], "L1_attn_w_kTb": [1024, 16384],
    "L1_attn_w_vTb": [1024, 16384], "L1_attn_w_oTb": [1024, 16384],
    "L1_ffn_weightTb": [1024, 16384],
}

FUSED_IN = [nm for nm in IN_SPECS
            if nm not in ("L0_ffn_weight", "L1_attn_w_q", "L1_attn_w_k",
                          "L1_attn_w_v", "L1_attn_w_o", "L1_ffn_weight")]


def declare_inputs(nc, names):
    t = {}
    for nm in names:
        dt = I32 if nm == "L1_ffn_lora_start_indices" else F32
        t[nm] = nc.dram_tensor(nm, IN_SPECS[nm], dt, kind="ExternalInput")
    return t


def build_l0a(upto="conv", dbg=()):
    nc = bass.Bass("TRN2", target_bir_lowering=False)
    ctx = ExitStack()
    t_in = declare_inputs(nc, [
        "L0_attn_conv_w", "L0_attn_conv_b", "L0_attn_wq", "L0_attn_bq",
        "L0_attn_wk", "L0_attn_bk", "L0_attn_wv", "L0_attn_bv",
        "L0_attn_wo", "L0_attn_bo", "L0_attn_proj_w", "L0_attn_proj_b",
    ])
    x0s = nc.dram_tensor("x0s", [1026, 4096], F32, kind="ExternalInput")

    def dram(nm, shape, dt):
        kind = "ExternalOutput" if nm in dbg else "Internal"
        return nc.dram_tensor(nm, shape, dt, kind=kind)

    ln0_fm = dram("ln0_fm", [4096, 1040], BF16)
    conv_fm = dram("conv_fm", [3072, 1024], BF16)
    qh_fm = dram("qh_fm", [1024, 1024], BF16)
    kh_fm = dram("kh_fm", [1024, 1024], BF16)
    vh_tm = dram("vh_tm", [1024, 1024], BF16)
    o_fm = dram("o_fm", [1024, 1024], BF16)
    x1 = dram("x1", [1024, 4096], F32)

    p = Prog(nc, ctx)
    C = {}
    emit_setup_core(p, C)
    emit_setup_l0a(p, t_in, C)
    emit_ln_to_fm(p, C, x0s, LN0_STARTS, ln0_fm, 0, "ln0")
    emit_zero_cols(p, ln0_fm, 1026, "ln0pad")
    emit_fence(p)
    if upto == "ln0":
        ctx.close()
        return nc
    emit_conv(p, C, ln0_fm, t_in["L0_attn_conv_w"], conv_fm)
    if upto == "conv":
        ctx.close()
        return nc
    emit_l0_qkv(p, C, conv_fm, t_in["L0_attn_wq"], t_in["L0_attn_wk"],
                t_in["L0_attn_wv"], qh_fm, kh_fm, vh_tm, conv_buf=big_a)
    if upto == "qkv":
        ctx.close()
        return nc
    emit_l0_battn(p, C, qh_fm, kh_fm, vh_tm, o_fm)
    if upto == "battn":
        ctx.close()
        return nc
    emit_l0_out(p, C, o_fm, t_in["L0_attn_wo"], t_in["L0_attn_proj_w"], x0s, x1,
                pw_buf=big_a)
    ctx.close()
    return nc


def build_l0f(upto="x2", dbg=(), rg=None):
    nc = bass.Bass("TRN2", target_bir_lowering=False)
    ctx = ExitStack()
    t_in = declare_inputs(nc, [
        "L0_ffn_weight", "L0_ffn_lora_A", "L0_ffn_lora_B", "L0_ffn_gw1",
        "L0_ffn_gb1", "L0_ffn_ln_gamma", "L0_ffn_ln_beta", "L0_ffn_gw2",
        "L0_ffn_gb2",
    ])
    x1 = nc.dram_tensor("x1", [1024, 4096], F32, kind="ExternalInput")

    def dram(nm, shape, dt):
        kind = "ExternalOutput" if nm in dbg else "Internal"
        return nc.dram_tensor(nm, shape, dt, kind=kind)

    ln1_fm = dram("ln1_fm", [4096, 1024], BF16)
    x2 = dram("x2", [1024, 4096], F32)
    dbg_w = dram("w_dbg", [128, 64], F32) if "w_dbg" in dbg else None
    cc_in = nc.dram_tensor("cc_in", [16, 1], F32)
    cc_out = nc.dram_tensor("cc_out", [16, 1], F32)

    p = Prog(nc, ctx)
    C = {}
    emit_setup_core(p, C)
    emit_ln_to_fm(p, C, x1, LN_STARTS, ln1_fm, 0, "ln1")
    emit_fence(p)
    emit_l0_ffn(p, C, t_in, x1, ln1_fm, x2, cc_in, cc_out, dbg_w=dbg_w, rg=rg)
    ctx.close()
    return nc


def build_l1a(dbg=(), rg=None, upto="x3", skq=()):
    nc = bass.Bass("TRN2", target_bir_lowering=False)
    ctx = ExitStack()
    t_in = declare_inputs(nc, [
        "L1_attn_w_q", "L1_attn_w_k", "L1_attn_w_v", "L1_attn_w_o",
    ])
    x2 = nc.dram_tensor("x2", [1024, 4096], F32, kind="ExternalInput")
    NR = len(rg[0]) if rg is not None else NCORES

    def dram(nm, shape, dt):
        kind = "ExternalOutput" if nm in dbg else "Internal"
        return nc.dram_tensor(nm, shape, dt, kind=kind)

    ln2_fm = dram("ln2_fm", [4096, 1024], BF16)
    q_dram = dram("q_dram", [4096, 1024], BF16)
    kv_c = dram("kv_c", [2048, 4096], BF16)
    kvg = dram("kvg", [NR * 2048, 4096], BF16)
    kn_c = dram("kn_c", [32, 1024], F32)
    kng = dram("kng", [NR * 32, 1024], F32)
    o2_tm = dram("o2_tm", [1024, 4096], BF16)
    x3 = dram("x3", [1024, 4096], F32)

    p = Prog(nc, ctx)
    C = {}
    emit_setup_core(p, C)
    qnm_row = p.sb("l1_qnm_row", [1, 32], F32)
    negc_bc = p.sb("l1_negc_bc", [128, 32], F32)
    emit_ln_to_fm(p, C, x2, LN_STARTS, ln2_fm, 0, "ln2")
    emit_fence(p)
    emit_l1_qkv(p, C, t_in, ln2_fm, q_dram, kv_c, kn_c, qnm_row, sk=skq)
    if upto == "qkv":
        ctx.close()
        return nc
    emit_l1_gather(p, C, kv_c, kvg, kn_c, kng, qnm_row, negc_bc, rg=rg)
    if upto == "gather":
        ctx.close()
        return nc
    emit_l1_flash(p, C, q_dram, kvg, negc_bc, o2_tm, NR=NR)
    if upto == "flash":
        ctx.close()
        return nc
    emit_big_linear(p, C, "oproj", "tm_transpose", o2_tm,
                    t_in["L1_attn_w_o"], x2, x3)
    ctx.close()
    return nc


def build_l1ffn(dbg=()):
    nc = bass.Bass("TRN2", target_bir_lowering=False)
    ctx = ExitStack()
    t_in = declare_inputs(nc, [
        "L1_ffn_weight", "L1_ffn_bias", "L1_ffn_lora_A", "L1_ffn_lora_B",
        "L1_ffn_lora_start_indices",
    ])
    x3 = nc.dram_tensor("x3", [1024, 4096], F32, kind="ExternalInput")
    out = nc.dram_tensor("out", [1024, 4096], F32, kind="ExternalOutput")

    def dram(nm, shape, dt):
        kind = "ExternalOutput" if nm in dbg else "Internal"
        return nc.dram_tensor(nm, shape, dt, kind=kind)

    ln3_fm = dram("ln3_fm", [4096, 1024], BF16)

    p = Prog(nc, ctx)
    C = {}
    emit_setup_core(p, C)
    bias_row = p.sb("l1f_bias_row", [1, 4096], BF16)
    tmpT = p.sb("l1f_tmpT", [16, 8 * 128], BF16)
    B_effT = p.sb("l1f_BeffT", [16, 4096], BF16)
    emit_ln_to_fm(p, C, x3, LN_STARTS, ln3_fm, 0, "ln3")
    emit_fence(p)
    emit_l1_ffn_pre(p, C, t_in, ln3_fm, bias_row, tmpT, B_effT)

    def extra(t, pp, ps, og, tt):
        t.matmul(ps[:, :], C["ones_bf"][0:1, :],
                 bias_row[0:1, og * 512:(og + 1) * 512],
                 start=True, stop=False).then_inc(pp.s["pe"].h, 1)
        t.matmul(ps[:, :], tmpT[0:16, tt * 128:(tt + 1) * 128],
                 B_effT[0:16, og * 512:(og + 1) * 512],
                 start=False, stop=False).then_inc(pp.s["pe"].h, 1)

    emit_big_linear(p, C, "l1ffn", "fm", ln3_fm, t_in["L1_ffn_weight"],
                    x3, out, n_extra=2, emit_extra=extra)
    ctx.close()
    return nc


def build_fused():
    nc = bass.Bass("TRN2", target_bir_lowering=False)
    ctx = ExitStack()
    t_in = declare_inputs(nc, FUSED_IN)
    x0s = nc.dram_tensor("x0s", [1026, 4096], F32, kind="ExternalInput")
    out = nc.dram_tensor("out", [1024, 4096], F32, kind="ExternalOutput")

    def dram(nm, shape, dt):
        return nc.dram_tensor(nm, shape, dt, kind="Internal")

    ln0_fm = dram("ln0_fm", [4096, 1040], BF16)
    conv_fm = dram("conv_fm", [3072, 1024], BF16)
    qh_fm = dram("qh_fm", [1024, 1024], BF16)
    kh_fm = dram("kh_fm", [1024, 1024], BF16)
    vh_tm = dram("vh_tm", [1024, 1024], BF16)
    o_fm = dram("o_fm", [1024, 1024], BF16)
    x1 = dram("x1", [1024, 4096], F32)
    ln1_fm = dram("ln1_fm", [4096, 1024], BF16)
    x2 = dram("x2", [1024, 4096], F32)
    cc_in = dram("cc_in", [16, 1], F32)
    cc_out = dram("cc_out", [16, 1], F32)
    ln2_fm = dram("ln2_fm", [4096, 1024], BF16)
    q_dram = dram("q_dram", [4096, 1024], BF16)
    k_c = dram("k_c", [4096, 1024], BF16)
    v_c = dram("v_c", [1024, 4096], BF16)
    kg = dram("kg", [NCORES * 4096, 1024], BF16)
    vg = dram("vg", [NCORES * 1024, 4096], BF16)
    kn_c = dram("kn_c", [32, 1024], F32)
    kng = dram("kng", [NCORES * 32, 1024], F32)
    o2_tm = dram("o2_tm", [1024, 4096], BF16)
    x3 = dram("x3", [1024, 4096], F32)
    ln3_fm = dram("ln3_fm", [4096, 1024], BF16)

    p = Prog(nc, ctx)
    C = {}
    emit_setup_core(p, C)
    emit_setup_l0a(p, t_in, C)
    qnm_row = p.sb("l1_qnm_row", [1, 32], F32)
    negc_bc = p.sb("l1_negc_bc", [128, 32], F32)
    bias_row = p.sb("l1f_bias_row", [1, 4096], BF16)
    tmpT = p.sb("l1f_tmpT", [16, 8 * 128], BF16)
    B_effT = p.sb("l1f_BeffT", [16, 4096], BF16)
    A_effT = p.sb("l1f_AeffT", [128, 32 * 16], BF16)
    wpref = p.sb("wpref", [128, 16384], BF16)
    big_a = p.sb("big_a", [128, 32768], BF16)

    emit_ln_to_fm(p, C, x0s, LN0_STARTS, ln0_fm, 0, "ln0")
    emit_zero_cols(p, ln0_fm, 1026, "ln0pad")
    emit_fence(p)
    emit_conv(p, C, ln0_fm, t_in["L0_attn_conv_w"], conv_fm)
    emit_l0_qkv(p, C, conv_fm, t_in["L0_attn_wq"], t_in["L0_attn_wk"],
                t_in["L0_attn_wv"], qh_fm, kh_fm, vh_tm, conv_buf=big_a)
    emit_l0_battn(p, C, qh_fm, kh_fm, vh_tm, o_fm)
    emit_l0_out(p, C, o_fm, t_in["L0_attn_wo"], t_in["L0_attn_proj_w"], x0s, x1,
                pw_buf=big_a)

    emit_ln_to_fm(p, C, x1, LN_STARTS, ln1_fm, 0, "ln1")
    emit_fence(p)
    l0ffn_free = emit_l0_ffn(p, C, t_in, x1, ln1_fm, x2, cc_in, cc_out,
                             wTb=t_in["L0_ffn_weightTb"], wpref_sb=wpref,
                             ln_buf=big_a)

    pf_qkv = emit_prefetch(p, wpref, t_in["L1_attn_w_kTb"], free_pe=l0ffn_free)
    emit_l1_ffn_pre_w(p, C, t_in, bias_row, A_effT, B_effT)
    emit_ln_to_fm(p, C, x2, LN_STARTS, ln2_fm, 0, "ln2")
    emit_fence(p)
    cc_base = p.s["cc"].n
    qkv_free = emit_l1_qkv_t2(p, C, t_in["L1_attn_w_kTb"], t_in["L1_attn_w_vTb"],
                              t_in["L1_attn_w_qTb"], ln2_fm, k_c, v_c, q_dram,
                              kn_c, qnm_row, kg, vg, kng,
                              wpref=wpref, pref_cnt=pf_qkv, ln_buf=big_a)
    pf_oproj = emit_prefetch(p, wpref, t_in["L1_attn_w_oTb"], free_pe=qkv_free)
    negc_dve = emit_l1_negc(p, C, kng, qnm_row, negc_bc, cc_base)
    emit_l1_flash2(p, C, q_dram, kg, vg, negc_bc, o2_tm, NR=NCORES,
                   cc_base=cc_base, negc_dve=negc_dve, a_pref=big_a)
    oproj_free = emit_big_linear_t(p, C, "oproj", "tm_transpose", o2_tm,
                                   t_in["L1_attn_w_oTb"], x2, x3,
                                   wpref=wpref, pref_cnt=pf_oproj,
                                   a_buf=big_a, skip_a_loads=True)
    pf_l1f = emit_prefetch(p, wpref, t_in["L1_ffn_weightTb"], free_pe=oproj_free)

    emit_ln_to_fm(p, C, x3, LN_STARTS, ln3_fm, 0, "ln3")
    emit_fence(p)
    emit_l1_ffn_tmp(p, C, ln3_fm, A_effT, tmpT, ln_buf=big_a)

    def extra(t, pp, ps, og, tt):
        t.matmul(ps[:, :], C["ones_bf"][0:1, :],
                 bias_row[0:1, og * 512:(og + 1) * 512],
                 start=True, stop=False).then_inc(pp.s["pe"].h, 1)
        t.matmul(ps[:, :], tmpT[0:16, tt * 128:(tt + 1) * 128],
                 B_effT[0:16, og * 512:(og + 1) * 512],
                 start=False, stop=False).then_inc(pp.s["pe"].h, 1)

    emit_big_linear_t(p, C, "l1ffn", "fm", ln3_fm, t_in["L1_ffn_weightTb"],
                      x3, out, n_extra=2, emit_extra=extra,
                      wpref=wpref, pref_cnt=pf_l1f,
                      a_buf=big_a, skip_a_loads=True)
    ctx.close()
    return nc


def _blocked_T(w):
    return np.ascontiguousarray(
        w.T.reshape(32, 128, 8, 512).transpose(2, 1, 0, 3).reshape(1024, 16384))


def _prep_inputs(inputs):
    ins = {k: np.ascontiguousarray(v) for k, v in inputs.items()}
    lsi = ins["L1_ffn_lora_start_indices"]
    if lsi.dtype == np.int64:
        ins["L1_ffn_lora_start_indices"] = np.ascontiguousarray(
            lsi.view(np.int32)[0::2])
    return ins


G1_W = ["L0_attn_conv_w", "L0_attn_conv_b", "L0_attn_wq", "L0_attn_bq",
        "L0_attn_wk", "L0_attn_bk", "L0_attn_wv", "L0_attn_bv",
        "L0_attn_wo", "L0_attn_bo", "L0_attn_proj_w", "L0_attn_proj_b"]
G2_W = ["L0_ffn_weight", "L0_ffn_lora_A", "L0_ffn_lora_B", "L0_ffn_gw1",
        "L0_ffn_gb1", "L0_ffn_ln_gamma", "L0_ffn_ln_beta", "L0_ffn_gw2",
        "L0_ffn_gb2"]
G3_W = ["L1_attn_w_q", "L1_attn_w_k", "L1_attn_w_v", "L1_attn_w_o"]
G4_W = ["L1_ffn_weight", "L1_ffn_bias", "L1_ffn_lora_A", "L1_ffn_lora_B",
        "L1_ffn_lora_start_indices"]


TB_SRC = {
    "L0_ffn_weightTb": "L0_ffn_weight",
    "L1_attn_w_qTb": "L1_attn_w_q", "L1_attn_w_kTb": "L1_attn_w_k",
    "L1_attn_w_vTb": "L1_attn_w_v", "L1_attn_w_oTb": "L1_attn_w_o",
    "L1_ffn_weightTb": "L1_ffn_weight",
}


def kernel(**inputs):
    ins = _prep_inputs(inputs)
    for tb, src in TB_SRC.items():
        ins[tb] = _blocked_T(ins[src])
    x0 = ins["L0_attn_inputs"].reshape(8192, 4096)
    x0p = np.pad(x0, ((1, 1), (0, 0)))
    cores = list(range(NCORES))

    nc = build_fused()
    maps = []
    for c in cores:
        m = {nm: ins[nm] for nm in FUSED_IN}
        m["x0s"] = np.ascontiguousarray(x0p[c * 1024:c * 1024 + 1026])
        maps.append(m)
    res = run_bass_kernel_spmd(nc, maps, core_ids=cores)
    outs = [res.results[c]["out"] for c in cores]

    return np.concatenate(outs, axis=0).reshape(1, 8192, 4096)
